# version 04
# baseline (speedup 1.0000x reference)
import math
from contextlib import ExitStack

import numpy as np
import ml_dtypes

import concourse.bass as bass
import concourse.mybir as mybir
from concourse import library_config
from concourse.bass_core_rust import IndirectOffsetOnAxis

F32 = mybir.dt.float32
BF16 = mybir.dt.bfloat16
I16 = mybir.dt.int16
I32 = mybir.dt.int32
AF = mybir.ActivationFunctionType
ALU = mybir.AluOpType
bfdt = ml_dtypes.bfloat16

T = 4096
D = 1024
E = 8
CAP = 1024
H = 4096
NTILE = T // 128

ENGINES = ("sync", "scalar", "vector", "tensor", "gpsimd")


class Sem:
    def __init__(self, handle, name):
        self.h = handle
        self.name = name
        self.n = 0

    def add(self, k):
        self.n += k
        return self.n


class Ctx:

    def __init__(self, nc, stack):
        self.nc = nc
        self.stack = stack
        self.ops = {e: [] for e in ENGINES}
        self.sems = {}
        self.last_op = {}

    def sem(self, name):
        if name not in self.sems:
            h = self.stack.enter_context(self.nc.semaphore(name))
            self.sems[name] = Sem(h, name)
        return self.sems[name]

    def emit(self, engine, fn):
        self.ops[engine].append(fn)

    def wait(self, engine, sem, val):
        self.emit(engine, lambda eng, s=sem.h, v=val: eng.wait_ge(s, v))

    def dma(self, engine, out, in_, sem, **kw):
        v = sem.add(16)
        def f(eng, out=out, in_=in_, s=sem.h, kw=kw):
            eng.dma_start(out=out, in_=in_, **kw).then_inc(s, 16)
        self.emit(engine, f)
        return v

    def dma_tp(self, engine, out, in_, sem):
        v = sem.add(16)
        def f(eng, out=out, in_=in_, s=sem.h):
            eng.dma_start_transpose(out=out, in_=in_).then_inc(s, 16)
        self.emit(engine, f)
        return v

    def op(self, engine, meth, sem=None, ninc=1, **kw):
        chain = engine in ("vector", "scalar", "gpsimd")
        if chain:
            prev = self.last_op.get(engine)
            if prev is not None:
                self.wait(engine, prev[0], prev[1])
        if sem is None and chain:
            sem = self.sem("_self_" + engine)
            ninc = 1
        v = sem.add(ninc) if sem is not None else None
        def f(eng, meth=meth, kw=kw, s=sem, v=v, ninc=ninc):
            ins = getattr(eng, meth)(**kw)
            if s is not None:
                ins.then_inc(s.h, ninc)
        self.emit(engine, f)
        if chain:
            self.last_op[engine] = (sem, v)
        return v

    def finalize(self):
        nc = self.nc
        ops = self.ops
        with nc.Block() as blk:
            @blk.sync
            def _(eng):
                for f in ops["sync"]:
                    f(eng)

            @blk.scalar
            def _(eng):
                for f in ops["scalar"]:
                    f(eng)

            @blk.vector
            def _(eng):
                for f in ops["vector"]:
                    f(eng)

            @blk.tensor
            def _(eng):
                for f in ops["tensor"]:
                    f(eng)

            @blk.gpsimd
            def _(eng):
                for f in ops["gpsimd"]:
                    f(eng)



def host_consts():
    c = {}
    c["eye_f32"] = np.eye(128, dtype=np.float32)
    S = np.zeros((128, 128), np.float32)
    for p in range(128):
        for p2 in range(128):
            if p % 16 == p2 % 16:
                S[p, p2] = 1.0
    c["Snorm"] = S
    G8 = np.zeros((128, 8), np.float32)
    for p in range(128):
        G8[p, p // 16] = 1.0
    c["G8"] = G8
    E16 = np.zeros((8, 128), np.float32)
    for p in range(128):
        E16[p // 16, p] = 1.0
    c["E16"] = E16
    L = np.zeros((128, 128), np.float32)
    for p in range(128):
        for p2 in range(128):
            if p // 16 == p2 // 16 and (p2 % 16) < (p % 16):
                L[p2, p] = 1.0
    c["Ltri"] = L
    c["iota16"] = np.broadcast_to(np.arange(16, dtype=np.float32), (128, 16)).astype(bfdt).copy()
    c["iota8"] = np.broadcast_to(np.arange(8, dtype=np.float32), (128, 8)).astype(bfdt).copy()
    thi = np.zeros((128, 2, 128), np.float32)
    tlo = np.zeros((128, 2, 128), np.float32)
    for p in range(128):
        for h in range(2):
            for col in range(128):
                c256 = col % 16
                t = 256 * c256 + 128 * h + p
                thi[p, h, col] = t >> 5
                tlo[p, h, col] = t & 31
    c["thi"] = thi.astype(bfdt)
    c["tlo"] = tlo.astype(bfdt)
    c["ones1"] = np.ones((128, 1), np.float32).astype(bfdt)
    c["eps5"] = np.full((128, 1), 1e-5, np.float32)
    c["eps6"] = np.full((128, 1), 1e-6, np.float32)
    c["ones1_f32"] = np.ones((128, 1), np.float32)
    return c


CONST_SPECS = {
    "eye_f32": ([128, 128], F32),
    "Snorm": ([128, 128], F32),
    "G8": ([128, 8], F32),
    "E16": ([8, 128], F32),
    "Ltri": ([128, 128], F32),
    "iota16": ([128, 16], BF16),
    "iota8": ([128, 8], BF16),
    "thi": ([128, 2, 128], BF16),
    "tlo": ([128, 2, 128], BF16),
    "ones1": ([128, 1], BF16),
    "eps5": ([128, 1], F32),
    "eps6": ([128, 1], F32),
    "ones1_f32": ([128, 1], F32),
}


def declare_consts(nc):
    aps = {}
    for name, (shape, dt) in CONST_SPECS.items():
        aps[name] = nc.dram_tensor("c_" + name, shape, dt, kind="ExternalInput").ap()
    return aps



def emit_ln(c, bufs, x_in, xln_tm, sem_pre=None, gamma=None, beta=None,
            eps_ap=None, stats_out=None, tag="L"):
    nc = c.nc
    ld = c.sem(tag + "ld")
    ve = c.sem(tag + "ve")
    sc = c.sem(tag + "sc")
    st = c.sem(tag + "st")

    GB = 8
    NG = NTILE // GB
    xt = bufs["ln_x"]
    x2 = bufs["ln_x2"]
    sm = bufs["ln_sm"]
    ot = bufs["ln_out"]
    xv = x_in.rearrange("(g p) d -> p g d", p=128)
    ov = xln_tm.rearrange("(g p) d -> p g d", p=128)

    sq_m, sub_m, ap_m, st_m = {}, {}, {}, {}
    for i in range(NG):
        b = i % 2
        if i >= 2:
            c.wait("sync", *sq_m[i - 2])
            c.wait("sync", *sub_m[i - 2])
        c.dma("sync", xt[:, b], xv[:, GB * i:GB * (i + 1), :], ld)
        c.wait("scalar", ld, ld.n)
        if i >= 1:
            c.wait("scalar", *ap_m[i - 1])
        c.op("scalar", "activation", out=x2.rearrange("p g d -> p (g d)"),
             in_=xt[:, b].rearrange("p g d -> p (g d)"), func=AF.Square)
        sq_m[i] = c.last_op["scalar"]
        c.wait("vector", ld, ld.n)
        c.op("vector", "tensor_reduce", out=sm[:, b, 0, :], in_=xt[:, b],
             op=ALU.add, axis=mybir.AxisListType.X)
        c.wait("vector", *sq_m[i])
        c.op("vector", "tensor_reduce", out=sm[:, b, 1, :], in_=x2[:],
             op=ALU.add, axis=mybir.AxisListType.X)
        c.op("vector", "tensor_scalar", out=sm[:, b, 2, :], in0=sm[:, b, 0, :],
             scalar1=1.0 / D, scalar2=None, op0=ALU.mult)
        c.op("vector", "tensor_scalar", out=sm[:, b, 3, :], in0=sm[:, b, 1, :],
             scalar1=1.0 / D, scalar2=None, op0=ALU.mult)
        c.op("vector", "tensor_tensor", out=sm[:, b, 4, :], in0=sm[:, b, 2, :],
             in1=sm[:, b, 2, :], op=ALU.mult)
        c.op("vector", "tensor_tensor", out=sm[:, b, 4, :], in0=sm[:, b, 3, :],
             in1=sm[:, b, 4, :], op=ALU.subtract)
        var_m = c.last_op["vector"]
        c.wait("scalar", *var_m)
        c.op("scalar", "activation", out=sm[:, b, 5, :], in_=sm[:, b, 4, :],
             func=AF.Sqrt, bias=eps_ap)
        sd_m = c.last_op["scalar"]
        c.wait("vector", *sd_m)
        c.op("vector", "reciprocal", out=sm[:, b, 6, :], in_=sm[:, b, 5, :])
        mb = sm[:, b, 2, :].unsqueeze(2).broadcast_to([128, GB, D])
        rb = sm[:, b, 6, :].unsqueeze(2).broadcast_to([128, GB, D])
        c.op("vector", "tensor_tensor", out=x2[:], in0=xt[:, b], in1=mb,
             op=ALU.subtract)
        sub_m[i] = c.last_op["vector"]
        if i >= 2:
            c.wait("vector", st, 16 * st_m[i - 2])
        if gamma is None:
            c.op("vector", "tensor_tensor", out=ot[:, b], in0=x2[:], in1=rb,
                 op=ALU.mult)
        else:
            c.op("vector", "tensor_tensor", out=x2[:], in0=x2[:], in1=rb,
                 op=ALU.mult)
            c.op("vector", "tensor_tensor", out=x2[:], in0=x2[:],
                 in1=gamma[:].unsqueeze(1).broadcast_to([128, GB, D]),
                 op=ALU.mult)
            c.op("vector", "tensor_tensor", out=ot[:, b], in0=x2[:],
                 in1=beta[:].unsqueeze(1).broadcast_to([128, GB, D]),
                 op=ALU.add)
        ap_m[i] = c.last_op["vector"]
        c.wait("sync", *ap_m[i])
        c.dma("sync", ov[:, GB * i:GB * (i + 1), :], ot[:, b], st)
        st_m[i] = st.n // 16
    return st.n



def emit_ffn(c, x_in, x_out, gate_wT, w1, w2, cst, con, xln_tm, expd, tag="", stag=None, disp=None, xinf=None, dbg=None, routing_only=False):
    if stag is None:
        stag = tag
    """con: dict of const SBUF tiles (loaded by caller)."""
    nc = c.nc
    st_ = c.stack
    t_ = lambda n: tag + n
    s_ = lambda n: stag + n
    sb = lambda name, shape, dt: st_.enter_context(nc.sbuf_tensor(t_(name), shape, dt))

    if xinf is None:
        xinf = nc.dram_tensor(t_("xinf"), [2, 1024, D], BF16, kind="Internal").ap()
    if disp is None:
        disp = {
            "idx32": sb("idx32", [128, E, 8], I32),
            "g128": sb("g128", [128, E, 8], F32),
            "y_sb": sb("ex_y", [128, 2, D], F32),
            "gtm": sb("gtm", [128, 1, D], BF16),
        }
    idx32 = disp["idx32"]
    g128 = disp["g128"]
    gw_sb = sb("gw", [128, 8, E], BF16)
    w1sb = sb("w1sb", [128, 8, H], BF16)

    seedS = c.sem(s_("seed"))
    c.dma("sync", x_out[:], x_in[:], seedS)
    seed_done = seedS.n

    mm = c.sem(s_("mm"))
    ve = c.sem(s_("ve"))
    sc = c.sem(s_("sc"))
    po = c.sem(s_("po"))
    ld = c.sem(s_("ld"))
    gat = c.sem(s_("gat"))
    sca = c.sem(s_("sca"))
    w1c = c.sem(s_("w1c"))
    w2c = c.sem(s_("w2c"))
    mmG = c.sem(s_("mmG"))
    mmK = c.sem(s_("mmK"))
    mmY = c.sem(s_("mmY"))
    geS = c.sem(s_("geS"))
    yv = c.sem(s_("yv"))
    gld = c.sem(s_("gld"))
    xga = c.sem(s_("xga"))

    w1c_done = {}
    w2c_done = {}
    mmG_b = c.sem(s_("mmG")).n
    mmK_b = c.sem(s_("mmK")).n
    mmY_b = c.sem(s_("mmY")).n
    geS_b = c.sem(s_("geS")).n
    yv_b = c.sem(s_("yv")).n

    def emit_w1c(e):
        if e > 0:
            c.wait("gpsimd", mmG, mmG_b + 64 * e)
        vals = []
        for k in range(8):
            v = c.dma("gpsimd", w1sb[:, k, :], w1[e, 128 * k:128 * (k + 1), :], w1c)
            vals.append(v)
        w1c_done[e] = vals

    emit_w1c(0)
    c.dma("gpsimd", gw_sb[:], gate_wT.rearrange("(c p) e -> p c e", p=128), gld)
    gw_ok = gld.n

    with ExitStack() as ph:
        sbp = lambda name, shape, dt: ph.enter_context(nc.sbuf_tensor(t_("ln_" + name), shape, dt))
        bufs = {
            "ln_x": sbp("x", [128, 2, 8, D], F32),
            "ln_x2": sbp("x2", [128, 8, D], F32),
            "ln_sm": sbp("sm", [128, 2, 8, 8], F32),
            "ln_out": sbp("o", [128, 2, 8, D], BF16),
        }
        emit_ln(c, bufs, x_in, xln_tm, eps_ap=con["eps5"][:, 0:1], tag=s_("L"))
    ln_st = c.sems[s_("L") + "st"]
    ln_done = ln_st.n
    for _e in ("gpsimd", "vector", "scalar"):
        c.wait(_e, ln_st, ln_done)

    with ExitStack() as ph:
        sbp = lambda name, shape, dt: ph.enter_context(nc.sbuf_tensor(t_("rt_" + name), shape, dt))
        psp = lambda name, shape: ph.enter_context(nc.psum_tensor(t_("rp_" + name), shape, F32))
        xlnT = sbp("xlnT", [128, 2, T], BF16)
        exp_s = sbp("exps", [8, T], F32)
        expc = sbp("expc", [128, 256], F32)
        rnorm = sbp("rnorm", [128, 256], F32)
        probs = sbp("probs", [128, 256], F32)
        junk = sbp("junk", [128, 256], BF16)
        cntp = sbp("cntp", [128, 1], F32)
        bis = sbp("bis", [8, 8], F32)
        thc = sbp("thc", [128, 1], F32)
        offs = sbp("offs", [128, 1], F32)
        maskc = sbp("maskc", [128, 256], F32)
        cum = sbp("cum", [128, 256], F32)
        pos = sbp("pos", [128, 256], F32)
        sel = sbp("sel", [128, 256], F32)
        pos_m = sbp("pos_m", [128, 256], F32)
        posT = sbp("posT", [128, 2, 128], F32)
        probT = sbp("probT", [128, 2, 128], F32)
        q_i16 = sbp("qi", [128, 2, 128], I16)
        qf = sbp("qf", [128, 2, 128], F32)
        q_bf = sbp("qb", [128, 2, 128], BF16)
        r_bf = sbp("rb", [128, 2, 128], BF16)
        j_i16 = sbp("ji", [128, 2, 128], I16)
        jf = sbp("jf", [128, 2, 128], F32)
        j_bf = sbp("jb", [128, 2, 128], BF16)
        a_bf = sbp("ab", [128, 2, 128], BF16)
        rf_t = sbp("rf", [128, 2, 128], F32)
        jr_t = sbp("jr", [128, 2, 128], F32)
        neg_t = sbp("neg", [128, 2, 128], F32)
        phi = sbp("phi", [128, 2, 128], BF16)
        plo = sbp("plo", [128, 2, 128], BF16)
        GW = 32
        Fl16 = sbp("Fl16", [128, 2, GW, 16], BF16)
        Fm8 = sbp("Fm8", [128, 2, GW, 8], BF16)
        Fh8 = sbp("Fh8", [128, 2, GW, 8], BF16)
        Fh8w = sbp("Fh8w", [128, 2, GW, 32], BF16)
        Fl128 = sbp("Fl128", [128, 2, GW, 8, 16], BF16)
        g2s = sbp("g2s", [128, 32], F32)
        idx128f = sbp("idx128f", [128, 8], F32)
        gtmp = sbp("gtmp", [128, 8], F32)
        ps_sc_stack = ExitStack()
        ps_sc = ps_sc_stack.enter_context(nc.psum_tensor(t_("rp_sc"), [8, T], F32))
        xbar_vals = []
        mm_base = mm.n
        gate_mm_marks = [mm_base + ci + 1 for ci in range(8)]
        for ci in range(8):
            if ci >= 2:
                c.wait("sync", mm, gate_mm_marks[ci - 2])
            if ci == 0:
                c.wait("sync", ln_st, ln_done)
            v = c.dma_tp("sync", xlnT[:, ci % 2, :], xln_tm[:, 128 * ci:128 * (ci + 1)], ld)
            xbar_vals.append(v)
        for ci in range(8):
            c.wait("tensor", ld, xbar_vals[ci])
            if ci == 0:
                c.wait("tensor", gld, gw_ok)
            for n in range(8):
                c.op("tensor", "matmul", sem=(mm if n == 7 else None),
                     out=ps_sc[:, 512 * n:512 * (n + 1)],
                     lhsT=gw_sb[:, ci, :], rhs=xlnT[:, ci % 2, 512 * n:512 * (n + 1)],
                     start=(ci == 0), stop=(ci == 7), skip_group_check=True)
            assert mm.n == gate_mm_marks[ci]
        gate_mm_done = mm.n

        c.wait("scalar", mm, gate_mm_done)
        for n in range(8):
            c.op("scalar", "activation", sem=(sc if n == 7 else None),
                 out=exp_s[:, 512 * n:512 * (n + 1)], in_=ps_sc[:, 512 * n:512 * (n + 1)],
                 func=AF.Exp)
        ps_sc_stack.close()
        ps_nrm = psp("nrm", [128, 256])
        ps_c8 = psp("c8", [8, 1])
        ps_thc = psp("thc", [128, 1])
        ps_off = psp("off", [128, 1])
        ps_tp = psp("tp", [128, 4, 128])
        ps_G2 = psp("G2", [128, 32])
        c.wait("sync", sc, sc.n)
        v1 = c.dma("sync", expd[:].rearrange("(e t) -> e t", e=8), exp_s[:], ld)
        c.wait("sync", ld, v1)
        v2 = c.dma("sync", expc[:], expd[:].rearrange("(p t) -> p t", p=128), ld)
        c.wait("tensor", ld, v2)
        c.op("tensor", "matmul", sem=mm, out=ps_nrm[:], lhsT=con["Snorm"][:],
             rhs=expc[:], start=True, stop=True)
        c.wait("vector", mm, mm.n)
        c.op("vector", "reciprocal", out=rnorm[:], in_=ps_nrm[:])
        c.op("vector", "tensor_tensor", sem=ve, out=probs[:], in0=expc[:],
             in1=rnorm[:], op=ALU.mult)

        c.op("vector", "memset", ap=bis[:], constant=0.0)
        c.op("vector", "memset", ap=bis[:, 1:2], constant=1.0)
        c.op("vector", "memset", ap=bis[:, 2:3], constant=0.5)
        c.op("vector", "memset", sem=ve, ap=thc[:], constant=0.5)
        NIT = 16
        for it in range(NIT):
            c.op("vector", "tensor_scalar", out=junk[:], in0=probs[:],
                 scalar1=thc[:], scalar2=None, op0=ALU.is_ge)
            c.op("vector", "reduce_sum", sem=ve, out=cntp[:], in_=junk[:],
                 axis=mybir.AxisListType.X)
            c.wait("tensor", ve, ve.n)
            c.op("tensor", "matmul", sem=mm, out=ps_c8[:], lhsT=con["G8"][:],
                 rhs=cntp[:], start=True, stop=True)
            c.wait("vector", mm, mm.n)
            c.op("vector", "tensor_copy", out=bis[:, 3:4], in_=ps_c8[:])
            c.op("vector", "tensor_scalar", out=bis[:, 4:5], in0=bis[:, 3:4],
                 scalar1=1023.5, scalar2=None, op0=ALU.is_gt)
            c.op("vector", "tensor_tensor", out=bis[:, 5:6], in0=bis[:, 2:3], in1=bis[:, 0:1], op=ALU.subtract)
            c.op("vector", "tensor_tensor", out=bis[:, 5:6], in0=bis[:, 4:5], in1=bis[:, 5:6], op=ALU.mult)
            c.op("vector", "tensor_tensor", out=bis[:, 0:1], in0=bis[:, 0:1], in1=bis[:, 5:6], op=ALU.add)
            c.op("vector", "tensor_tensor", out=bis[:, 6:7], in0=bis[:, 1:2], in1=bis[:, 2:3], op=ALU.subtract)
            c.op("vector", "tensor_tensor", out=bis[:, 6:7], in0=bis[:, 4:5], in1=bis[:, 6:7], op=ALU.mult)
            c.op("vector", "tensor_tensor", out=bis[:, 1:2], in0=bis[:, 2:3], in1=bis[:, 6:7], op=ALU.add)
            c.op("vector", "tensor_tensor", out=bis[:, 7:8], in0=bis[:, 0:1], in1=bis[:, 1:2], op=ALU.add)
            c.op("vector", "tensor_scalar", sem=ve, out=bis[:, 2:3], in0=bis[:, 7:8],
                 scalar1=0.5, scalar2=None, op0=ALU.mult)
            if it < NIT - 1:
                c.wait("tensor", ve, ve.n)
                c.op("tensor", "matmul", sem=mm, out=ps_thc[:], lhsT=con["E16"][:],
                     rhs=bis[:, 2:3], start=True, stop=True)
                c.wait("vector", mm, mm.n)
                c.op("vector", "tensor_copy", sem=ve, out=thc[:], in_=ps_thc[:])
        c.wait("tensor", ve, ve.n)
        c.op("tensor", "matmul", sem=mm, out=ps_thc[:], lhsT=con["E16"][:],
             rhs=bis[:, 0:1], start=True, stop=True)
        c.wait("vector", mm, mm.n)
        c.op("vector", "tensor_copy", out=thc[:], in_=ps_thc[:])

        c.op("vector", "tensor_scalar", out=maskc[:], in0=probs[:],
             scalar1=thc[:], scalar2=None, op0=ALU.is_ge)
        c.op("vector", "tensor_tensor_scan", out=cum[:],
             data0=con["ones1_f32"][:].broadcast_to([128, 256]), data1=maskc[:],
             initial=0.0, op0=ALU.mult, op1=ALU.add)
        c.op("vector", "tensor_tensor", sem=ve, out=pos[:], in0=cum[:], in1=maskc[:], op=ALU.subtract)
        c.wait("tensor", ve, ve.n)
        c.op("tensor", "matmul", sem=mm, out=ps_off[:], lhsT=con["Ltri"][:],
             rhs=cum[:, 255:256], start=True, stop=True)
        c.wait("vector", mm, mm.n)
        c.op("vector", "tensor_copy", out=offs[:], in_=ps_off[:])
        c.op("vector", "tensor_scalar", out=pos[:], in0=pos[:], scalar1=offs[:],
             scalar2=None, op0=ALU.add)
        c.op("vector", "tensor_scalar", out=sel[:], in0=pos[:], scalar1=1023.5,
             scalar2=None, op0=ALU.is_lt)
        c.op("vector", "tensor_tensor", out=sel[:], in0=maskc[:], in1=sel[:], op=ALU.mult)
        c.op("vector", "tensor_scalar", out=pos_m[:], in0=pos[:], scalar1=4096.0,
             scalar2=None, op0=ALU.subtract)
        c.op("vector", "tensor_tensor", out=pos_m[:], in0=pos_m[:], in1=sel[:], op=ALU.mult)
        c.op("vector", "tensor_scalar", sem=ve, out=pos_m[:], in0=pos_m[:],
             scalar1=4096.0, scalar2=None, op0=ALU.add)

        c.wait("tensor", ve, ve.n)
        for h in range(2):
            c.op("tensor", "transpose", out=ps_tp[:, h, :], in_=pos_m[:, 128 * h:128 * (h + 1)],
                 identity=con["eye_f32"][:])
            c.op("tensor", "transpose", sem=(mm if h == 1 else None),
                 out=ps_tp[:, 2 + h, :], in_=probs[:, 128 * h:128 * (h + 1)],
                 identity=con["eye_f32"][:])
        c.wait("vector", mm, mm.n)
        for h in range(2):
            c.op("vector", "tensor_copy", out=posT[:, h, :], in_=ps_tp[:, h, :])
            c.op("vector", "tensor_copy", sem=(ve if h == 1 else None),
                 out=probT[:, h, :], in_=ps_tp[:, 2 + h, :])

        rf = qf
        c.op("vector", "tensor_scalar", out=q_i16[:], in0=posT[:],
             scalar1=1.0 / 16, scalar2=None, op0=ALU.mult)
        c.op("vector", "tensor_copy", out=qf[:], in_=q_i16[:])
        c.op("vector", "scalar_tensor_tensor", out=rf_t[:], in0=qf[:], scalar=-16.0,
             in1=posT[:], op0=ALU.mult, op1=ALU.add)
        c.op("vector", "tensor_scalar", out=neg_t[:], in0=rf_t[:],
             scalar1=0.0, scalar2=None, op0=ALU.is_lt)
        c.op("vector", "tensor_tensor", out=qf[:], in0=qf[:], in1=neg_t[:], op=ALU.subtract)
        c.op("vector", "scalar_tensor_tensor", out=rf_t[:], in0=neg_t[:], scalar=16.0,
             in1=rf_t[:], op0=ALU.mult, op1=ALU.add)
        c.op("vector", "tensor_copy", out=q_bf[:], in_=qf[:])
        c.op("vector", "tensor_copy", out=r_bf[:], in_=rf_t[:])
        c.op("vector", "tensor_scalar", out=j_i16[:], in0=posT[:],
             scalar1=1.0 / 128, scalar2=None, op0=ALU.mult)
        c.op("vector", "tensor_copy", out=jf[:], in_=j_i16[:])
        c.op("vector", "scalar_tensor_tensor", out=jr_t[:], in0=jf[:], scalar=-128.0,
             in1=posT[:], op0=ALU.mult, op1=ALU.add)
        c.op("vector", "tensor_scalar", out=neg_t[:], in0=jr_t[:],
             scalar1=0.0, scalar2=None, op0=ALU.is_lt)
        c.op("vector", "tensor_tensor", out=jf[:], in0=jf[:], in1=neg_t[:], op=ALU.subtract)
        c.op("vector", "tensor_copy", out=j_bf[:], in_=jf[:])
        c.op("vector", "scalar_tensor_tensor", out=a_bf[:], in0=jf[:], scalar=-8.0,
             in1=qf[:], op0=ALU.mult, op1=ALU.add)
        c.op("vector", "tensor_copy", out=phi[:], in_=probT[:])
        c.op("vector", "tensor_tensor", sem=ve, out=plo[:], in0=probT[:], in1=phi[:],
             op=ALU.subtract)
        cols_ve = ve.n

        def iotab(nm, w):
            return con[nm][:].unsqueeze(1).unsqueeze(2).broadcast_to([128, 2, GW, w])

        def colb(ap, g, w):
            return ap[:, :, GW * g:GW * (g + 1)].unsqueeze(3).broadcast_to([128, 2, GW, w])

        for g in range(4):
            if g > 0:
                c.wait("vector", mm, mm.n)
            c.op("vector", "tensor_tensor", out=Fl16[:],
                 in0=iotab("iota16", 16), in1=colb(r_bf[:], g, 16), op=ALU.is_equal)
            c.op("vector", "tensor_tensor", out=Fm8[:],
                 in0=iotab("iota8", 8), in1=colb(a_bf[:], g, 8), op=ALU.is_equal)
            c.op("vector", "tensor_tensor", out=Fh8[:],
                 in0=iotab("iota8", 8), in1=colb(j_bf[:], g, 8), op=ALU.is_equal)
            c.op("vector", "tensor_tensor", out=Fh8w[:, :, :, 0:8], in0=Fh8[:],
                 in1=colb(phi[:], g, 8), op=ALU.mult)
            c.op("vector", "tensor_tensor", out=Fh8w[:, :, :, 8:16], in0=Fh8[:],
                 in1=colb(plo[:], g, 8), op=ALU.mult)
            c.op("vector", "tensor_tensor", out=Fh8w[:, :, :, 16:24], in0=Fh8[:],
                 in1=con["thi"][:, :, GW * g:GW * (g + 1)].unsqueeze(3).broadcast_to([128, 2, GW, 8]),
                 op=ALU.mult)
            c.op("vector", "tensor_tensor", sem=ve, out=Fh8w[:, :, :, 24:32], in0=Fh8[:],
                 in1=con["tlo"][:, :, GW * g:GW * (g + 1)].unsqueeze(3).broadcast_to([128, 2, GW, 8]),
                 op=ALU.mult)
            c.op("vector", "tensor_tensor", sem=ve, out=Fl128[:],
                 in0=Fl16[:].unsqueeze(3).broadcast_to([128, 2, GW, 8, 16]),
                 in1=Fm8[:].unsqueeze(4).broadcast_to([128, 2, GW, 8, 16]), op=ALU.mult)
            c.wait("tensor", ve, ve.n)
            for el in range(2):
                e = 2 * g + el
                for c128 in range(32):
                    h = c128 & 1
                    col = 16 * el + (c128 >> 1)
                    c.op("tensor", "matmul", sem=(mm if c128 == 31 else None),
                         out=ps_G2[:], lhsT=Fl128[:, h, col, :, :].rearrange("p a l -> p (a l)"),
                         rhs=Fh8w[:, h, col, :],
                         start=(c128 == 0), stop=(c128 == 31), skip_group_check=True)
                c.wait("vector", mm, mm.n)
                c.op("vector", "tensor_copy", out=g2s[:], in_=ps_G2[:])
                c.op("vector", "tensor_tensor", out=gtmp[:], in0=g2s[:, 0:8], in1=g2s[:, 8:16], op=ALU.add)
                c.op("vector", "tensor_scalar", out=g128[:, e, :], in0=gtmp[:],
                     scalar1=0.5, scalar2=None, op0=ALU.mult)
                c.op("vector", "scalar_tensor_tensor", out=idx128f[:], in0=g2s[:, 16:24],
                     scalar=32.0, in1=g2s[:, 24:32], op0=ALU.mult, op1=ALU.add)
                c.op("vector", "tensor_copy", sem=ve, out=idx32[:, e, :], in_=idx128f[:])
                c.wait("tensor", ve, ve.n)
        routing_done = ve.n
        if routing_only:
            dsem = c.sem(t_("dsem"))
            c.wait("sync", ve, routing_done)
            for nm, tile in dbg.items():
                if nm.startswith("rt_"):
                    c.dma("sync", tile, {"rt_jbf": j_bf, "rt_abf": a_bf, "rt_phi": phi,
                                         "rt_plo": plo, "rt_Fh8": Fh8, "rt_Fm8": Fm8,
                                         "rt_Fh8w": Fh8w, "rt_Fl16": Fl16, "rt_g2s": g2s,
                                         "rt_posT": posT, "rt_probT": probT, "rt_qbf": q_bf,
                                         "rt_probs": probs, "rt_expc": expc,
                                         "rt_rnorm": rnorm, "rt_exps": exp_s,
                                         "rt_maskc": maskc, "rt_pos": pos, "rt_thc": thc,
                                         "rt_rbf": r_bf}[nm][:], dsem)
            c.dma("sync", dbg["idx32"][:], idx32[:], dsem)
            c.dma("sync", dbg["g128"][:], g128[:], dsem)
            if "xln" in dbg:
                c.dma("sync", dbg["xln"][:], xln_tm[:], dsem)
            c.wait("sync", dsem, dsem.n)
            return None

    xin = sb("ex_xin", [128, 2, 8, 1024], BF16)
    y_sb = disp["y_sb"]
    gtm = disp["gtm"]
    with ExitStack() as ph:
        sbp = lambda name, shape, dt: ph.enter_context(nc.sbuf_tensor(t_("ex_" + name), shape, dt))
        w2sb = sbp("w2sb", [128, 32, D], BF16)
        h_sb = sbp("h", [128, 32, 512], BF16)
        ps8 = ph.enter_context(nc.psum_tensor(t_("ps8"), [128, 8, 512], F32))

        gather_done = {}
        scatter_done = {}
        gtm_free = [None]
        xinf_free = {}

        def emit_gathers(e):
            c.wait("gpsimd", ve, routing_done)
            if e >= 2:
                c.wait("gpsimd", xga, xinf_free[e % 2])
            for j in range(8):
                if gtm_free[0] is not None:
                    c.wait("gpsimd", gat, gtm_free[0])
                v = gat.add(16)
                def f(eng, e=e, j=j):
                    eng.indirect_dma_start(
                        out=gtm[:, 0, :], out_offset=None, in_=xln_tm[:],
                        in_offset=IndirectOffsetOnAxis(idx32[:, e, j:j + 1], 0)
                    ).then_inc(gat.h, 16)
                c.emit("gpsimd", f)
                c.wait("gpsimd", gat, gat.n)
                vout = c.dma("gpsimd", xinf[e % 2][128 * j:128 * (j + 1), :],
                             gtm[:, 0, :], gat)
                gtm_free[0] = vout
            c.wait("sync", gat, gtm_free[0])
            if e >= 2:
                c.wait("sync", mmG, mmG_b + 64 * (e - 1))
            for k in range(8):
                c.dma_tp("sync", xin[:, e % 2, k, :],
                         xinf[e % 2][:, 128 * k:128 * (k + 1)], xga)
            xinf_free[e % 2] = xga.n
            gather_done[e] = xga.n

        def emit_w2c(e):
            if e == 0:
                c.wait("gpsimd", ve, routing_done)
            for k in range(32):
                if e > 0:
                    if k <= 30:
                        c.wait("gpsimd", mmK, mmK_b + 62 * (e - 1) + 31 + k + 1)
                    else:
                        c.wait("gpsimd", mmY, mmY_b + 8 * e)
                w2c_done[(e, k)] = c.dma("gpsimd", w2sb[:, k, :],
                                         w2[e, 128 * k:128 * (k + 1), :], w2c)

        def emit_scatters(e, hcall):
            if True:
                for s in range(4):
                    c.wait("gpsimd", yv, yv_b + 8 * e + 4 * hcall + s + 1)
                    c.wait("gpsimd", seedS, seed_done)
                    if sca.n > 0:
                        c.wait("gpsimd", sca, sca.n)
                    v = sca.add(16)
                    def f(eng, e=e, hcall=hcall, s=s):
                        eng.indirect_dma_start(
                            out=x_out[:],
                            out_offset=IndirectOffsetOnAxis(idx32[:, e, 4 * hcall + s:4 * hcall + s + 1], 0),
                            in_=y_sb[:, s % 2, :], in_offset=None,
                            compute_op=ALU.add).then_inc(sca.h, 16)
                    c.emit("gpsimd", f)
                    scatter_done[(e, hcall, s)] = v

        emit_gathers(0)
        emit_gathers(1)
        for e in range(E):
            emit_w2c(e)
            emit_scatters(e, 0)
            if e + 1 < E:
                emit_w1c(e + 1)
            emit_scatters(e, 1)
            if e + 2 < E:
                emit_gathers(e + 2)
        if dbg is not None:
            dbgs = c.sem(t_("dbgs"))
            c.wait("sync", ve, routing_done)
            c.dma("sync", dbg["idx32"][:], idx32[:], dbgs)
            c.dma("sync", dbg["g128"][:], g128[:], dbgs)
            c.wait("sync", dbgs, dbgs.n)
        final_sca = sca.n
        c.wait("gpsimd", sca, final_sca)

        grp = 0
        for e in range(E):
            c.wait("tensor", xga, gather_done[e])
            for hl in range(2):
                if 2 * e + hl > 0:
                    c.wait("tensor", yv, yv_b + 8 * e + 4 * hl)
                for mg in range(4):
                    if grp > 0:
                        c.wait("tensor", geS, geS_b + 8 * grp)
                    for k in range(8):
                        if hl == 0 and mg == 0:
                            c.wait("tensor", w1c, w1c_done[e][k])
                        for ml in range(8):
                            m = mg * 8 + ml
                            c.op("tensor", "matmul", sem=(mmG if k == 7 else None),
                                 out=ps8[:, ml, :],
                                 lhsT=w1sb[:, k, 128 * m:128 * (m + 1)],
                                 rhs=xin[:, e % 2, k, 512 * hl:512 * (hl + 1)],
                                 start=(k == 0), stop=(k == 7), skip_group_check=True)
                    for ml in range(8):
                        m = mg * 8 + ml
                        c.wait("scalar", mmG, mmG_b + 8 * grp + ml + 1)
                        c.op("scalar", "activation", sem=geS,
                             out=h_sb[:, m, :], in_=ps8[:, ml, :], func=AF.Gelu)
                    grp += 1
                c.wait("tensor", geS, geS_b + 8 * grp)
                for k in range(32):
                    c.wait("tensor", w2c, w2c_done[(e, k)])
                    for s in range(4):
                        c.op("tensor", "matmul",
                             out=ps8[:, 2 * s, :],
                             lhsT=h_sb[:, k, 128 * s:128 * (s + 1)],
                             rhs=w2sb[:, k, 0:512],
                             start=(k == 0), stop=(k == 31), skip_group_check=True)
                        c.op("tensor", "matmul",
                             sem=(mmY if k == 31 else (mmK if s == 3 else None)),
                             out=ps8[:, 2 * s + 1, :],
                             lhsT=h_sb[:, k, 128 * s:128 * (s + 1)],
                             rhs=w2sb[:, k, 512:1024],
                             start=(k == 0), stop=(k == 31), skip_group_check=True)
                for s in range(4):
                    c.wait("vector", mmY, mmY_b + 4 * (2 * e + hl) + s + 1)
                    if s >= 2:
                        c.wait("vector", sca, scatter_done[(e, hl, s - 2)])
                    elif hl == 1:
                        c.wait("vector", sca, scatter_done[(e, 0, 2 + s)])
                    elif e >= 1:
                        c.wait("vector", sca, scatter_done[(e - 1, 1, 2 + s)])
                    c.op("vector", "tensor_scalar", sem=yv,
                         out=y_sb[:, s % 2, :].rearrange("p (a b) -> p a b", a=2),
                         in0=ps8[:, 2 * s:2 * s + 2, :],
                         scalar1=g128[:, e, 4 * hl + s:4 * hl + s + 1], scalar2=None,
                         op0=ALU.mult)
    return final_sca


def load_consts(c, cst):
    nc = c.nc
    con = {}
    for nm, (shape, dt) in CONST_SPECS.items():
        con[nm] = c.stack.enter_context(nc.sbuf_tensor("cs_" + nm, shape, dt))
    cl = c.sem("cl")
    for nm in CONST_SPECS:
        c.dma("sync", con[nm][:], cst[nm][:], cl)
    for engx in ("tensor", "vector", "scalar", "gpsimd"):
        c.wait(engx, cl, cl.n)
    return con


def build_ffn_program():
    nc = bass.Bass("TRN2", target_bir_lowering=False, detect_race_conditions=False)
    x_in = nc.dram_tensor("x_in", [T, D], F32, kind="ExternalInput").ap()
    gate_wT = nc.dram_tensor("gate_wT", [D, E], F32, kind="ExternalInput").ap()
    w1 = nc.dram_tensor("w1", [E, D, H], F32, kind="ExternalInput").ap()
    w2 = nc.dram_tensor("w2", [E, H, D], F32, kind="ExternalInput").ap()
    x_out = nc.dram_tensor("x_out", [T, D], F32, kind="ExternalOutput").ap()
    cst = declare_consts(nc)
    xln_tm = nc.dram_tensor("xln_tm", [T, D], BF16, kind="Internal").ap()
    expd = nc.dram_tensor("expd", [E * T], F32, kind="Internal").ap()

    dbg = {}
    dbg["idx"] = nc.dram_tensor("dbg_idx", [128, E, 64], I16, kind="ExternalOutput").ap()
    dbg["idx32"] = nc.dram_tensor("dbg_idx32", [128, E, 8], I32, kind="ExternalOutput").ap()
    dbg["g128"] = nc.dram_tensor("dbg_g128", [128, E, 8], F32, kind="ExternalOutput").ap()
    with ExitStack() as stack:
        c = Ctx(nc, stack)
        con = load_consts(c, cst)
        emit_ffn(c, x_in, x_out, gate_wT, w1, w2, cst, con, xln_tm, expd, dbg=dbg)
        c.finalize()
    mybir.codegen_inst_isa_subclasses(nc)
    return nc


def ffn_inputs(x, gate_w, w1, w2, l0_style=True):
    cc = host_consts()
    gw = gate_w if l0_style else np.ascontiguousarray(gate_w.T)
    m = {"x_in": np.ascontiguousarray(x), "gate_wT": np.ascontiguousarray(gw),
         "w1": np.ascontiguousarray(w1), "w2": np.ascontiguousarray(w2)}
    for k, v in cc.items():
        m["c_" + k] = v
    return m


def build_routing_probe():
    nc = bass.Bass("TRN2", target_bir_lowering=False, detect_race_conditions=False)
    x_in = nc.dram_tensor("x_in", [T, D], F32, kind="ExternalInput").ap()
    gate_wT = nc.dram_tensor("gate_wT", [D, E], F32, kind="ExternalInput").ap()
    w1 = nc.dram_tensor("w1", [E, D, H], F32, kind="ExternalInput").ap()
    w2 = nc.dram_tensor("w2", [E, H, D], F32, kind="ExternalInput").ap()
    x_out = nc.dram_tensor("x_out", [T, D], F32, kind="ExternalOutput").ap()
    cst = declare_consts(nc)
    xln_tm = nc.dram_tensor("xln_tm", [T, D], BF16, kind="Internal").ap()
    expd = nc.dram_tensor("expd", [E * T], F32, kind="Internal").ap()
    dbg = {}
    dbg["idx"] = nc.dram_tensor("dbg_idx", [128, E, 64], I16, kind="ExternalOutput").ap()
    dbg["idx32"] = nc.dram_tensor("dbg_idx32", [128, E, 8], I32, kind="ExternalOutput").ap()
    dbg["g128"] = nc.dram_tensor("dbg_g128", [128, E, 8], F32, kind="ExternalOutput").ap()
    dbg["xln"] = nc.dram_tensor("dbg_xln", [T, D], BF16, kind="ExternalOutput").ap()
    for nm, shape, dt in (("rt_jbf", [128,2,128], BF16), ("rt_abf", [128,2,128], BF16),
                          ("rt_phi", [128,2,128], BF16), ("rt_plo", [128,2,128], BF16),
                          ("rt_Fh8", [128,2,32,8], BF16), ("rt_Fm8", [128,2,32,8], BF16),
                          ("rt_Fh8w", [128,2,32,32], BF16), ("rt_Fl16", [128,2,32,16], BF16),
                          ("rt_g2s", [128,32], F32), ("rt_posT", [128,2,128], F32),
                          ("rt_probT", [128,2,128], F32), ("rt_qbf", [128,2,128], BF16),
                          ("rt_rbf", [128,2,128], BF16),
                          ("rt_probs", [128,256], F32), ("rt_expc", [128,256], F32),
                          ("rt_rnorm", [128,256], F32), ("rt_exps", [8, T], F32),
                          ("rt_maskc", [128,256], F32), ("rt_pos", [128,256], F32),
                          ("rt_thc", [128,1], F32)):
        dbg[nm] = nc.dram_tensor("o" + nm, shape, dt, kind="ExternalOutput").ap()
    with ExitStack() as stack:
        c = Ctx(nc, stack)
        con = load_consts(c, cst)
        emit_ffn(c, x_in, x_out, gate_wT, w1, w2, cst, con, xln_tm, expd, dbg=dbg, routing_only=True)
        c.finalize()
    mybir.codegen_inst_isa_subclasses(nc)
    return nc



ATTN0_CONSTS = {
    "D64": ([128, 64], BF16),
    "DUP": ([64, 128], F32),
}


def attn0_host_consts():
    c = {}
    D64 = np.zeros((128, 64), np.float32)
    for p in range(128):
        D64[p, p & 63] = 1.0
    c["D64"] = D64.astype(bfdt)
    DUP = np.zeros((64, 128), np.float32)
    for p in range(128):
        DUP[p & 63, p] = 1.0
    c["DUP"] = DUP
    return c


def emit_attn0(c, x_in, x_out, Wq, Wk, Wv, Wo, bo, con, xln_tm, files, tag="a0", stag=None):
    if stag is None:
        stag = tag
    nc = c.nc
    st_ = c.stack
    t_ = lambda n: tag + n
    s_ = lambda n: stag + n
    sb = lambda name, shape, dt: st_.enter_context(nc.sbuf_tensor(t_(name), shape, dt))

    mm = c.sem(s_("mm"))
    ld = c.sem(s_("ld"))
    gld = c.sem(s_("gld"))
    stf = c.sem(s_("stf"))
    SCALE = 64.0 ** -0.5
    sc_self = lambda: c.last_op.get("scalar")
    ve_self = lambda: c.last_op.get("vector")

    with ExitStack() as ph:
        sbp = lambda name, shape, dt: ph.enter_context(nc.sbuf_tensor(t_("ln_" + name), shape, dt))
        bufs = {
            "ln_x": sbp("x", [128, 2, 8, D], F32),
            "ln_x2": sbp("x2", [128, 8, D], F32),
            "ln_sm": sbp("sm", [128, 2, 8, 8], F32),
            "ln_out": sbp("o", [128, 2, 8, D], BF16),
        }
        emit_ln(c, bufs, x_in, xln_tm, eps_ap=con["eps5"][:, 0:1], tag=s_("L"))
    ln_st = c.sems[s_("L") + "st"]
    ln_done = ln_st.n
    for _e in ("gpsimd", "vector", "scalar"):
        c.wait(_e, ln_st, ln_done)

    with ExitStack() as ph:
        sbp = lambda name, shape, dt: ph.enter_context(nc.sbuf_tensor(t_("qk_" + name), shape, dt))
        xlnT = sbp("xlnT", [128, 8, T], BF16)
        wt = sbp("wt", [128, 2, 8, 128], BF16)
        ev = sbp("ev", [128, 2, 8, 512], BF16)
        qtmp = sbp("qtmp", [128, 2, 512], F32)
        ks_t = sbp("ks", [64, 2, 512], F32)
        ksr = sbp("ksr", [128, 8, 512], BF16)
        krb = sbp("krb", [128, 2, 8, 512], BF16)
        ps = ph.enter_context(nc.psum_tensor(t_("ps_qk"), [128, 8, 512], F32))

        c.wait("sync", ln_st, ln_done)
        xb_vals = []
        for ci in range(8):
            v = c.dma_tp("sync", xlnT[:, ci, :], xln_tm[:, 128 * ci:128 * (ci + 1)], ld)
            xb_vals.append(v)

        pass_tail = [None, None]

        def gemm_pass(W, outfile, kind):
            base_mm = mm.n
            evdone = []
            for mt in range(32):
                b = mt % 2
                if mt >= 2:
                    c.wait("gpsimd", mm, base_mm + 8 * (mt - 1))
                else:
                    c.wait("gpsimd", mm, base_mm)
                c.dma("gpsimd", wt[:, b],
                      W[:, 128 * mt:128 * (mt + 1)].rearrange("(c p) m -> p c m", p=128), gld)
                wv_ok = gld.n
                c.wait("tensor", gld, wv_ok)
                if mt == 0:
                    for v in xb_vals:
                        c.wait("tensor", ld, v)
                if mt >= 1:
                    prev = c.last_op.get("scalar")
                    if prev is not None:
                        c.wait("tensor", prev[0], prev[1])
                    if kind == "q":
                        pv = c.last_op.get("vector")
                        if pv is not None:
                            c.wait("tensor", pv[0], pv[1])
                for ci in range(8):
                    for n in range(8):
                        c.op("tensor", "matmul", sem=(mm if n == 7 else None),
                             out=ps[:, n, :], lhsT=wt[:, b, ci, :],
                             rhs=xlnT[:, ci, 512 * n:512 * (n + 1)],
                             start=(ci == 0), stop=(ci == 7), skip_group_check=True)
                c.wait("scalar", mm, base_mm + 8 * (mt + 1))
                if mt >= 2:
                    c.wait("scalar", stf, evdone[mt - 2])
                elif pass_tail[mt] is not None:
                    c.wait("scalar", stf, pass_tail[mt])
                vec_marks = {}
                for n in range(8):
                    if kind == "q":
                        if n >= 2:
                            m2 = vec_marks[n - 2]
                            c.wait("scalar", m2[0], m2[1])
                        c.op("scalar", "activation", out=qtmp[:, n % 2, :], in_=ps[:, n, :],
                             func=AF.Relu, scale=SCALE)
                        pv = c.last_op.get("scalar")
                        c.wait("vector", pv[0], pv[1])
                        if mt >= 2 and n == 0:
                            c.wait("vector", stf, evdone[mt - 2])
                        elif mt < 2 and n == 0 and pass_tail[mt] is not None:
                            c.wait("vector", stf, pass_tail[mt])
                        c.op("vector", "tensor_tensor", out=ev[:, b, n, :],
                             in0=qtmp[:, n % 2, :], in1=ksr[:, n, :], op=ALU.mult)
                        vec_marks[n] = c.last_op["vector"]
                    elif kind == "k":
                        c.op("scalar", "activation", out=ev[:, b, n, :], in_=ps[:, n, :],
                             func=AF.Relu)
                    else:
                        c.op("scalar", "activation", out=ev[:, b, n, :], in_=ps[:, n, :],
                             func=AF.Copy)
                prev = c.last_op.get("vector" if kind == "q" else "scalar")
                c.wait("sync", prev[0], prev[1])
                v = c.dma("sync", outfile[128 * mt:128 * (mt + 1), :].rearrange("p (n t) -> p n t", n=8),
                          ev[:, b], stf)
                evdone.append(v)
            pass_tail[0] = evdone[-2]
            pass_tail[1] = evdone[-1]
            return base_mm

        gemm_pass(Wk, files["kf"], "k")
        k_done = stf.n

        pv = c.last_op["scalar"]
        c.wait("tensor", pv[0], pv[1])
        for nt in range(8):
            for grp in range(4):
                b = grp % 2
                if grp >= 2 or nt > 0:
                    c.wait("sync", mm, mm.n)
                if nt == 0 and grp == 0:
                    c.wait("sync", stf, k_done)
                c.dma("sync", krb[:, b],
                      files["kf"][1024 * grp:1024 * (grp + 1), 512 * nt:512 * (nt + 1)]
                      .rearrange("(ct p) t -> p ct t", p=128), ld)
                c.wait("tensor", ld, ld.n)
                for ct in range(8):
                    c.op("tensor", "matmul",
                         sem=(mm if (ct == 7) else None),
                         out=ps[0:64, 0, :], lhsT=con["D64"][:], rhs=krb[:, b, ct, :],
                         start=(grp == 0 and ct == 0), stop=(grp == 3 and ct == 7),
                         skip_group_check=True)
            c.wait("vector", mm, mm.n)
            c.op("vector", "tensor_copy", out=ks_t[:, nt % 2, :], in_=ps[0:64, 0, :])
            pv = c.last_op.get("vector")
            c.wait("tensor", pv[0], pv[1])
            c.op("tensor", "matmul", sem=mm, out=ps[:, 1, :], lhsT=con["DUP"][:],
                 rhs=ks_t[:, nt % 2, :], start=True, stop=True, skip_group_check=True)
            c.wait("vector", mm, mm.n)
            c.op("vector", "tensor_copy", out=ksr[:, nt, :], in_=ps[:, 1, :])
        pv = c.last_op.get("vector")
        c.wait("tensor", pv[0], pv[1])

        gemm_pass(Wq, files["qf"], "q")
        gemm_pass(Wv, files["vf"], "v")
        qv_done = stf.n

    G = 256
    NG = T // G
    with ExitStack() as ph:
        sbp = lambda name, shape, dt: ph.enter_context(nc.sbuf_tensor(t_("sm_" + name), shape, dt))
        qrb = sbp("qrb", [64, 2, 64, G], BF16)
        vrb = sbp("vrb", [64, 2, 65, G], BF16)
        og = sbp("og", [128, 64, 128], BF16)
        rc = sbp("rc", [128, 2, 16], F32)
        ps_a = ph.enter_context(nc.psum_tensor(t_("ps_a"), [128, 2, 16, 128], F32))

        c.op("vector", "memset", ap=vrb[:, 0, 64, :], constant=1.0)
        c.op("vector", "memset", ap=vrb[:, 1, 64, :], constant=1.0)
        ones_set = c.last_op["vector"]

        qview = files["qf"].rearrange("(i d) t -> d i t", d=64)
        vview = files["vf"].rearrange("(j d) t -> d j t", d=64)
        c.wait("sync", stf, qv_done)
        c.wait("sync", ones_set[0], ones_set[1])
        rb_vals = {}

        def load_group(g, wait_mm):
            b = g % 2
            if wait_mm is not None:
                c.wait("sync", mm, wait_mm)
            c.dma("sync", qrb[:, b], qview[:, :, G * g:G * (g + 1)], ld)
            rb_vals[g] = c.dma("sync", vrb[:, b, 0:64, :], vview[:, :, G * g:G * (g + 1)], ld)

        load_group(0, None)
        load_group(1, None)
        ev_done_flight = {}
        NF = 8
        for g in range(NG):
            b = g % 2
            c.wait("tensor", ld, rb_vals[g])
            for f in range(NF):
                pb = f % 2
                key = (g, f - 2) if f >= 2 else (g - 1, f + NF - 2)
                if key in ev_done_flight:
                    c.wait("tensor", ev_done_flight[key][0], ev_done_flight[key][1])
                for u in range(32):
                    tloc = 32 * f + u
                    h = u & 1
                    s = u >> 1
                    tp = (0, 64) if h else None
                    c.op("tensor", "matmul", sem=(mm if u == 31 else None),
                         out=ps_a[64 * h:64 * (h + 1), pb, s, 0:65],
                         lhsT=qrb[:, b, :, tloc], rhs=vrb[:, b, :, tloc],
                         start=True, stop=True, skip_group_check=True,
                         tile_position=tp)
                mmv = mm.n
                c.wait("vector", mm, mmv)
                c.op("vector", "tensor_scalar", out=rc[:, pb, :],
                     in0=ps_a[:, pb, :, 64:65], scalar1=1e-6, scalar2=None, op0=ALU.add)
                c.op("vector", "reciprocal", out=rc[:, pb, :], in_=rc[:, pb, :])
                c.op("vector", "tensor_tensor",
                     out=og[:, :, 16 * f:16 * (f + 1)].transpose([0, 2, 1]),
                     in0=ps_a[:, pb, :, 0:64],
                     in1=rc[:, pb, :].unsqueeze(2).broadcast_to([128, 16, 64]),
                     op=ALU.mult)
                ev_done_flight[(g, f)] = c.last_op["vector"]
            if g + 2 < NG:
                load_group(g + 2, mm.n)
            pv = c.last_op["vector"]
            c.wait("sync", pv[0], pv[1])
            if g >= 1:
                pass
            v_e = c.dma("sync", files["oe"].rearrange("(i j) u -> i j u", j=64)[:, :, 128 * g:128 * (g + 1)],
                        og[0:64], stf)
            v_o = c.dma("sync", files["oo"].rearrange("(i j) u -> i j u", j=64)[:, :, 128 * g:128 * (g + 1)],
                        og[64:128], stf)
            c.wait("vector", stf, v_o)
        o_done = stf.n

    with ExitStack() as ph:
        sbp = lambda name, shape, dt: ph.enter_context(nc.sbuf_tensor(t_("wo_" + name), shape, dt))
        wo_sb = sbp("wo", [128, 32, D], BF16)
        orb = sbp("orb", [128, 2, 512], BF16)
        rev = sbp("rev", [128, 8, 512], BF16)
        bo_sb = sbp("bo", [128, 8], F32)
        ps_r = ph.enter_context(nc.psum_tensor(t_("ps_r"), [128, 8, 512], F32))

        c.wait("sync", stf, o_done)
        c.wait("gpsimd", stf, o_done)
        c.dma("sync", bo_sb[:], bo[:], ld)
        for ct in range(32):
            c.dma("gpsimd", wo_sb[:, ct, :], Wo[128 * ct:128 * (ct + 1), :], gld)
        wo_ok = gld.n
        c.wait("tensor", gld, wo_ok)
        c.wait("tensor", ld, ld.n)

        rfiles = {0: files["re"], 1: files["ro"]}
        ofiles = {0: files["oe"], 1: files["oo"]}
        first = True
        for par in range(2):
            for ut in range(4):
                base_mm = mm.n
                for ct in range(32):
                    bb = ct % 2
                    if ct >= 2:
                        c.wait("sync", mm, base_mm + ct - 1)
                    if first and ct < 2:
                        c.wait("sync", stf, o_done)
                    c.dma("sync", orb[:, bb, :],
                          ofiles[par][128 * ct:128 * (ct + 1), 512 * ut:512 * (ut + 1)], ld)
                first = False
                lv = ld.n
                for ct in range(32):
                    bb = ct % 2
                    c.wait("tensor", ld, lv - 16 * (31 - ct))
                    if ct == 0 and (par + ut > 0):
                        prev = c.last_op.get("vector")
                        c.wait("tensor", prev[0], prev[1])
                    for cc in range(8):
                        c.op("tensor", "matmul", sem=(mm if cc == 7 else None),
                             out=ps_r[:, cc, :], lhsT=wo_sb[:, ct, 128 * cc:128 * (cc + 1)],
                             rhs=orb[:, bb, :],
                             start=(ct == 0), stop=(ct == 31), skip_group_check=True)
                c.wait("vector", mm, mm.n)
                if par + ut > 0:
                    c.wait("vector", stf, stf.n)
                for cc in range(8):
                    c.op("vector", "tensor_scalar", out=rev[:, cc, :], in0=ps_r[:, cc, :],
                         scalar1=bo_sb[:, cc:cc + 1], scalar2=None, op0=ALU.add)
                prev = c.last_op["vector"]
                c.wait("sync", prev[0], prev[1])
                c.dma("sync", rfiles[par][:, 512 * ut:512 * (ut + 1)]
                      .rearrange("(cc p) t -> p cc t", p=128), rev[:], stf)
        r_done = stf.n

    with ExitStack() as ph:
        sbp = lambda name, shape, dt: ph.enter_context(nc.sbuf_tensor(t_("ra_" + name), shape, dt))
        xt = sbp("x", [128, 2, D], F32)
        rt = sbp("r", [128, 2, D], BF16)
        ot = sbp("o", [128, 2, D], F32)
        rao = c.sem(s_("rao"))
        rao_b = rao.n

        xv = x_in.rearrange("(u two) d -> two u d", two=2)
        ov = x_out.rearrange("(u two) d -> two u d", two=2)
        rfiles = {0: files["re"], 1: files["ro"]}
        c.wait("sync", stf, r_done)
        for par in range(2):
            for pt in range(16):
                b = pt % 2
                if par * 16 + pt >= 2:
                    pv = c.last_op.get("vector")
                    c.wait("sync", pv[0], pv[1])
                c.dma("sync", xt[:, b, :], xv[par, 128 * pt:128 * (pt + 1), :], ld)
                c.dma_tp("sync", rt[:, b, :], rfiles[par][:, 128 * pt:128 * (pt + 1)], ld)
                c.wait("vector", ld, ld.n)
                if par * 16 + pt >= 2:
                    c.wait("vector", rao, rao.n - 16)
                c.op("vector", "tensor_tensor", out=ot[:, b, :], in0=xt[:, b, :],
                     in1=rt[:, b, :], op=ALU.add)
                pv = c.last_op["vector"]
                c.wait("sync", pv[0], pv[1])
                c.dma("sync", ov[par, 128 * pt:128 * (pt + 1), :], ot[:, b, :], rao)
        c.wait("sync", rao, rao.n)
    return rao.n



def attn1_host_consts(extra=True):
    return {"onesrow": np.ones((1, 128), np.float32)}

ATTN1_CONSTS = {"onesrow": ([1, 128], F32)}


def emit_attn1(c, x_in, x_out, Wt, Wo, Wg, par, con, xln_tm, files, tag="a1", stag=None):
    if stag is None:
        stag = tag
    nc = c.nc
    st_ = c.stack
    t_ = lambda n: tag + n
    s_ = lambda n: stag + n
    sb = lambda name, shape, dt: st_.enter_context(nc.sbuf_tensor(t_(name), shape, dt))

    mm = c.sem(s_("mm"))
    ld = c.sem(s_("ld"))
    gld = c.sem(s_("gld"))
    stf = c.sem(s_("stf"))

    pt = {}
    pl = c.sem(s_("pl"))
    for nm in ("dec", "beta", "lncw", "lncb", "lngw", "lngb", "bg"):
        pt[nm] = sb(nm, [128, 8], F32)
        c.dma("sync", pt[nm][:], par[nm][:], pl)
    for nm in ("g1w", "g1b"):
        pt[nm] = sb(nm, [128, D], F32)
        c.dma("sync", pt[nm][:], par[nm][:], pl)
    pl_done = pl.n
    for e in ("vector", "scalar", "gpsimd", "tensor"):
        c.wait(e, pl, pl_done)

    with ExitStack() as ph:
        sbp = lambda name, shape, dt: ph.enter_context(nc.sbuf_tensor(t_("ln_" + name), shape, dt))
        bufs = {
            "ln_x": sbp("x", [128, 2, 8, D], F32),
            "ln_x2": sbp("x2", [128, 8, D], F32),
            "ln_sm": sbp("sm", [128, 2, 8, 8], F32),
            "ln_out": sbp("o", [128, 2, 8, D], BF16),
        }
        emit_ln(c, bufs, x_in, xln_tm, eps_ap=con["eps6"][:, 0:1],
                gamma=pt["g1w"], beta=pt["g1b"], tag=s_("L"))
    ln_st = c.sems[s_("L") + "st"]
    ln_done = ln_st.n
    for _e in ("gpsimd", "vector", "scalar"):
        c.wait(_e, ln_st, ln_done)

    xlnT1 = sb("xlnT1", [128, 8, T], BF16)
    c.wait("sync", ln_st, ln_done)
    xb_vals = []
    for ci in range(8):
        v = c.dma_tp("sync", xlnT1[:, ci, :], xln_tm[:, 128 * ci:128 * (ci + 1)], ld)
        xb_vals.append(v)

    with ExitStack() as ph:
        sbp = lambda name, shape, dt: ph.enter_context(nc.sbuf_tensor(t_("u_" + name), shape, dt))
        wt_sb = sbp("wt", [128, 2, 8, 128], BF16)
        ud = sbp("ud", [128, T], BF16)
        cxd = sbp("cxd", [128, 2, T], BF16)
        ps = ph.enter_context(nc.psum_tensor(t_("ps_u"), [128, 8, 512], F32))

        base_mm = mm.n
        cx_store = {}
        for mt in range(8):
            b = mt % 2
            if mt >= 2:
                c.wait("gpsimd", mm, base_mm + 8 * (mt - 1))
            c.dma("gpsimd", wt_sb[:, b],
                  Wt[:, 128 * mt:128 * (mt + 1)].rearrange("(c p) m -> p c m", p=128), gld)
            c.wait("tensor", gld, gld.n)
            if mt == 0:
                for v in xb_vals:
                    c.wait("tensor", ld, v)
            if mt >= 1:
                pv = c.last_op["vector"]
                c.wait("tensor", pv[0], pv[1])
            for ci in range(8):
                for n in range(8):
                    c.op("tensor", "matmul", sem=(mm if n == 7 else None),
                         out=ps[:, n, :], lhsT=wt_sb[:, b, ci, :],
                         rhs=xlnT1[:, ci, 512 * n:512 * (n + 1)],
                         start=(ci == 0), stop=(ci == 7), skip_group_check=True)
            c.wait("vector", mm, base_mm + 8 * (mt + 1))
            for n in range(8):
                c.op("vector", "tensor_scalar", out=ud[:, 512 * n:512 * (n + 1)],
                     in0=ps[:, n, :], scalar1=pt["beta"][:, mt:mt + 1], scalar2=None,
                     op0=ALU.mult)
            if mt >= 2:
                c.wait("vector", stf, cx_store[mt - 2])
            c.op("vector", "tensor_tensor_scan", out=cxd[:, b, :],
                 data0=pt["dec"][:, mt:mt + 1].broadcast_to([128, T]), data1=ud[:],
                 initial=0.0, op0=ALU.mult, op1=ALU.add)
            pv = c.last_op["vector"]
            c.wait("sync", pv[0], pv[1])
            cx_store[mt] = c.dma("sync", files["cxf"][128 * mt:128 * (mt + 1), :], cxd[:, b, :], stf)
        cx_done = stf.n

    with ExitStack() as ph:
        sbp = lambda name, shape, dt: ph.enter_context(nc.sbuf_tensor(t_("cs_" + name), shape, dt))
        wo_sb = sbp("wo", [128, 8, D], BF16)
        cxg = sbp("cxg", [128, 8, 512], BF16)
        cx2 = sbp("cx2", [128, 8, 512], BF16)
        rrows = sbp("rr", [1, 4, 512], F32)
        mcb = sbp("mcb", [128, 512], F32)
        rcb = sbp("rcb", [128, 512], F32)
        lncg = sbp("lncg", [128, 8, 512], BF16)
        oev = sbp("oev", [128, 8, 512], BF16)
        ps = ph.enter_context(nc.psum_tensor(t_("ps_c"), [128, 8, 512], F32))

        c.wait("gpsimd", stf, cx_done)
        for ci in range(8):
            c.dma("gpsimd", wo_sb[:, ci, :], Wo[128 * ci:128 * (ci + 1), :], gld)
        wo_ok = gld.n
        cview = files["cxf"].rearrange("(dt p) t -> p dt t", p=128)
        out_store = {}
        for nt in range(8):
            if nt == 0:
                c.wait("sync", stf, cx_done)
            else:
                c.wait("sync", mm, mm.n)
            c.dma("sync", cxg[:], cview[:, :, 512 * nt:512 * (nt + 1)], ld)
            lv = ld.n
            c.wait("scalar", ld, lv)
            c.op("scalar", "activation", out=cx2[:].rearrange("p a t -> p (a t)"),
                 in_=cxg[:].rearrange("p a t -> p (a t)"), func=AF.Square)
            pv = c.last_op["scalar"]
            c.wait("tensor", ld, lv)
            c.wait("tensor", pv[0], pv[1])
            for dt_ in range(8):
                c.op("tensor", "matmul", out=ps[0:1, 0, :], lhsT=con["ones1"][:],
                     rhs=cxg[:, dt_, :], start=(dt_ == 0), stop=(dt_ == 7),
                     skip_group_check=True)
                c.op("tensor", "matmul", sem=(mm if dt_ == 7 else None),
                     out=ps[0:1, 1, :], lhsT=con["ones1"][:],
                     rhs=cx2[:, dt_, :], start=(dt_ == 0), stop=(dt_ == 7),
                     skip_group_check=True)
            c.wait("vector", mm, mm.n)
            c.op("vector", "tensor_scalar", out=rrows[:, 0, :], in0=ps[0:1, 0, :],
                 scalar1=1.0 / D, scalar2=None, op0=ALU.mult)
            c.op("vector", "tensor_scalar", out=rrows[:, 1, :], in0=ps[0:1, 1, :],
                 scalar1=1.0 / D, scalar2=None, op0=ALU.mult)
            c.op("vector", "tensor_tensor", out=rrows[:, 2, :], in0=rrows[:, 0, :],
                 in1=rrows[:, 0, :], op=ALU.mult)
            c.op("vector", "tensor_tensor", out=rrows[:, 2, :], in0=rrows[:, 1, :],
                 in1=rrows[:, 2, :], op=ALU.subtract)
            pv = c.last_op["vector"]
            c.wait("scalar", pv[0], pv[1])
            c.op("scalar", "activation", out=rrows[:, 3, :], in_=rrows[:, 2, :],
                 func=AF.Sqrt, bias=con["eps6"][0:1, 0:1])
            pv = c.last_op["scalar"]
            c.wait("vector", pv[0], pv[1])
            c.op("vector", "reciprocal", out=rrows[:, 3, :], in_=rrows[:, 3, :])
            pv = c.last_op["vector"]
            c.wait("tensor", pv[0], pv[1])
            c.op("tensor", "matmul", out=ps[:, 2, :], lhsT=con["onesrow"][:],
                 rhs=rrows[:, 0, :], start=True, stop=True, skip_group_check=True)
            c.op("tensor", "matmul", sem=mm, out=ps[:, 3, :], lhsT=con["onesrow"][:],
                 rhs=rrows[:, 3, :], start=True, stop=True, skip_group_check=True)
            c.wait("vector", mm, mm.n)
            c.op("vector", "tensor_copy", out=mcb[:], in_=ps[:, 2, :])
            c.op("vector", "tensor_copy", out=rcb[:], in_=ps[:, 3, :])
            c.op("vector", "tensor_tensor", out=lncg[:], in0=cxg[:],
                 in1=mcb[:].unsqueeze(1).broadcast_to([128, 8, 512]), op=ALU.subtract)
            c.op("vector", "tensor_tensor", out=lncg[:], in0=lncg[:],
                 in1=rcb[:].unsqueeze(1).broadcast_to([128, 8, 512]), op=ALU.mult)
            for dt_ in range(8):
                c.op("vector", "tensor_scalar", out=lncg[:, dt_, :],
                     in0=lncg[:, dt_, :], scalar1=pt["lncw"][:, dt_:dt_ + 1],
                     scalar2=pt["lncb"][:, dt_:dt_ + 1], op0=ALU.mult, op1=ALU.add)
            pv = c.last_op["vector"]
            c.wait("tensor", pv[0], pv[1])
            if nt == 0:
                c.wait("tensor", gld, wo_ok)
            if nt >= 1:
                c.wait("tensor", stf, out_store[nt - 1])
            for cc in range(8):
                for ci in range(8):
                    c.op("tensor", "matmul",
                         sem=(mm if ci == 7 else None),
                         out=ps[:, cc, :], lhsT=wo_sb[:, ci, 128 * cc:128 * (cc + 1)],
                         rhs=lncg[:, ci, :],
                         start=(ci == 0), stop=(ci == 7), skip_group_check=True)
            c.wait("vector", mm, mm.n)
            for cc in range(8):
                c.op("vector", "tensor_copy", out=oev[:, cc, :], in_=ps[:, cc, :])
            pv = c.last_op["vector"]
            c.wait("sync", pv[0], pv[1])
            out_store[nt] = c.dma("sync",
                                  files["outf"][:, 512 * nt:512 * (nt + 1)]
                                  .rearrange("(cc p) t -> p cc t", p=128), oev[:], stf)
        out_done = stf.n

    with ExitStack() as ph:
        sbp = lambda name, shape, dt: ph.enter_context(nc.sbuf_tensor(t_("g_" + name), shape, dt))
        wg_sb = sbp("wg", [128, 16, D], BF16)
        gpt = sbp("gpt", [128, 8, 512], BF16)
        gp2 = sbp("gp2", [128, 8, 512], BF16)
        grow = sbp("grow", [1, 4, 512], F32)
        mgb = sbp("mgb", [128, 512], F32)
        rgb = sbp("rgb", [128, 512], F32)
        gt = sbp("gt", [128, 8, 512], BF16)
        ofr = sbp("ofr", [128, 8, 512], BF16)
        mt_ = sbp("mt", [128, 8, 512], BF16)
        ps = ph.enter_context(nc.psum_tensor(t_("ps_g"), [128, 8, 512], F32))

        c.wait("gpsimd", stf, out_done)
        for ci in range(16):
            c.dma("gpsimd", wg_sb[:, ci, :], Wg[128 * ci:128 * (ci + 1), :], gld)
        wg_ok = gld.n
        oview = files["outf"].rearrange("(cc p) t -> p cc t", p=128)
        mt_store = {}
        for nt in range(8):
            if nt == 0:
                c.wait("sync", stf, out_done)
            else:
                c.wait("sync", mm, mm.n)
            c.dma("sync", ofr[:], oview[:, :, 512 * nt:512 * (nt + 1)], ld)
            ol = ld.n
            if nt == 0:
                c.wait("tensor", gld, wg_ok)
            if nt >= 1:
                pv = c.last_op["vector"]
                c.wait("tensor", pv[0], pv[1])
            c.wait("tensor", ld, ol)
            for cc in range(8):
                for ci in range(16):
                    rhs = xlnT1[:, ci, 512 * nt:512 * (nt + 1)] if ci < 8 else \
                        ofr[:, ci - 8, :]
                    c.op("tensor", "matmul", sem=(mm if ci == 15 else None),
                         out=ps[:, cc, :], lhsT=wg_sb[:, ci, 128 * cc:128 * (cc + 1)],
                         rhs=rhs, start=(ci == 0), stop=(ci == 15), skip_group_check=True)
            c.wait("vector", mm, mm.n)
            for cc in range(8):
                c.op("vector", "tensor_scalar", out=gpt[:, cc, :], in0=ps[:, cc, :],
                     scalar1=pt["bg"][:, cc:cc + 1], scalar2=None, op0=ALU.add)
            pv = c.last_op["vector"]
            c.wait("scalar", pv[0], pv[1])
            c.op("scalar", "activation", out=gp2[:].rearrange("p a t -> p (a t)"),
                 in_=gpt[:].rearrange("p a t -> p (a t)"), func=AF.Square)
            pv2 = c.last_op["scalar"]
            c.wait("tensor", pv[0], pv[1])
            c.wait("tensor", pv2[0], pv2[1])
            for cc in range(8):
                c.op("tensor", "matmul", out=ps[0:1, 0, :], lhsT=con["ones1"][:],
                     rhs=gpt[:, cc, :], start=(cc == 0), stop=(cc == 7),
                     skip_group_check=True)
                c.op("tensor", "matmul", sem=(mm if cc == 7 else None),
                     out=ps[0:1, 1, :], lhsT=con["ones1"][:], rhs=gp2[:, cc, :],
                     start=(cc == 0), stop=(cc == 7), skip_group_check=True)
            c.wait("vector", mm, mm.n)
            c.op("vector", "tensor_scalar", out=grow[:, 0, :], in0=ps[0:1, 0, :],
                 scalar1=1.0 / D, scalar2=None, op0=ALU.mult)
            c.op("vector", "tensor_scalar", out=grow[:, 1, :], in0=ps[0:1, 1, :],
                 scalar1=1.0 / D, scalar2=None, op0=ALU.mult)
            c.op("vector", "tensor_tensor", out=grow[:, 2, :], in0=grow[:, 0, :],
                 in1=grow[:, 0, :], op=ALU.mult)
            c.op("vector", "tensor_tensor", out=grow[:, 2, :], in0=grow[:, 1, :],
                 in1=grow[:, 2, :], op=ALU.subtract)
            pv = c.last_op["vector"]
            c.wait("scalar", pv[0], pv[1])
            c.op("scalar", "activation", out=grow[:, 3, :], in_=grow[:, 2, :],
                 func=AF.Sqrt, bias=con["eps6"][0:1, 0:1])
            pv = c.last_op["scalar"]
            c.wait("vector", pv[0], pv[1])
            c.op("vector", "reciprocal", out=grow[:, 3, :], in_=grow[:, 3, :])
            pv = c.last_op["vector"]
            c.wait("tensor", pv[0], pv[1])
            c.op("tensor", "matmul", out=ps[:, 2, :], lhsT=con["onesrow"][:],
                 rhs=grow[:, 0, :], start=True, stop=True, skip_group_check=True)
            c.op("tensor", "matmul", sem=mm, out=ps[:, 3, :], lhsT=con["onesrow"][:],
                 rhs=grow[:, 3, :], start=True, stop=True, skip_group_check=True)
            c.wait("vector", mm, mm.n)
            c.op("vector", "tensor_copy", out=mgb[:], in_=ps[:, 2, :])
            c.op("vector", "tensor_copy", out=rgb[:], in_=ps[:, 3, :])
            for cc in range(8):
                c.op("vector", "tensor_tensor", out=gt[:, cc, :], in0=gpt[:, cc, :],
                     in1=mgb[:], op=ALU.subtract)
                c.op("vector", "tensor_tensor", out=gt[:, cc, :], in0=gt[:, cc, :],
                     in1=rgb[:], op=ALU.mult)
                c.op("vector", "tensor_scalar", out=gt[:, cc, :], in0=gt[:, cc, :],
                     scalar1=pt["lngw"][:, cc:cc + 1], scalar2=pt["lngb"][:, cc:cc + 1],
                     op0=ALU.mult, op1=ALU.add)
            pv = c.last_op["vector"]
            c.wait("scalar", pv[0], pv[1])
            c.op("scalar", "activation", out=gt[:].rearrange("p a t -> p (a t)"),
                 in_=gt[:].rearrange("p a t -> p (a t)"), func=AF.Sigmoid)
            pv = c.last_op["scalar"]
            c.wait("vector", pv[0], pv[1])
            if nt >= 2:
                c.wait("vector", stf, mt_store[nt - 2])
            for cc in range(8):
                c.op("vector", "tensor_tensor", out=mt_[:, cc, :], in0=ofr[:, cc, :],
                     in1=xlnT1[:, cc, 512 * nt:512 * (nt + 1)], op=ALU.subtract)
                c.op("vector", "tensor_tensor", out=mt_[:, cc, :], in0=gt[:, cc, :],
                     in1=mt_[:, cc, :], op=ALU.mult)
                c.op("vector", "tensor_tensor", out=mt_[:, cc, :],
                     in0=xlnT1[:, cc, 512 * nt:512 * (nt + 1)], in1=mt_[:, cc, :],
                     op=ALU.add)
            pv = c.last_op["vector"]
            c.wait("sync", pv[0], pv[1])
            mt_store[nt] = c.dma("sync", files["mtf"][:, 512 * nt:512 * (nt + 1)]
                                 .rearrange("(cc p) t -> p cc t", p=128), mt_[:], stf)
        m_done = stf.n

    with ExitStack() as ph:
        sbp = lambda name, shape, dt: ph.enter_context(nc.sbuf_tensor(t_("ra_" + name), shape, dt))
        xt = sbp("x", [128, 2, D], F32)
        rt = sbp("r", [128, 2, D], BF16)
        ot = sbp("o", [128, 2, D], F32)
        rao = c.sem(s_("rao"))
        rao_b = rao.n
        c.wait("sync", stf, m_done)
        for tt in range(32):
            b = tt % 2
            if tt >= 2:
                pv = c.last_op["vector"]
                c.wait("sync", pv[0], pv[1])
            c.dma("sync", xt[:, b, :], x_in[128 * tt:128 * (tt + 1), :], ld)
            c.dma_tp("sync", rt[:, b, :], files["mtf"][:, 128 * tt:128 * (tt + 1)], ld)
            c.wait("vector", ld, ld.n)
            if tt >= 2:
                c.wait("vector", rao, rao_b + 16 * (tt - 1))
            c.op("vector", "tensor_tensor", out=ot[:, b, :], in0=xt[:, b, :],
                 in1=rt[:, b, :], op=ALU.add)
            pv = c.last_op["vector"]
            c.wait("sync", pv[0], pv[1])
            c.dma("sync", x_out[128 * tt:128 * (tt + 1), :], ot[:, b, :], rao)
        c.wait("sync", rao, rao.n)
    return rao.n



def build_fused_prog():
    nc = bass.Bass("TRN2", target_bir_lowering=False, detect_race_conditions=False)
    xi = nc.dram_tensor("x_in", [T, D], F32, kind="ExternalInput").ap()
    xo = nc.dram_tensor("x_out", [T, D], F32, kind="ExternalOutput").ap()
    xA = nc.dram_tensor("xA", [T, D], F32, kind="Internal").ap()
    xB = nc.dram_tensor("xB", [T, D], F32, kind="Internal").ap()

    gwT0 = nc.dram_tensor("gate_wT0", [D, E], F32, kind="ExternalInput").ap()
    w10 = nc.dram_tensor("w1_0", [E, D, H], F32, kind="ExternalInput").ap()
    w20 = nc.dram_tensor("w2_0", [E, H, D], F32, kind="ExternalInput").ap()
    gwT1 = nc.dram_tensor("gate_wT1", [D, E], F32, kind="ExternalInput").ap()
    w11 = nc.dram_tensor("w1_1", [E, D, H], F32, kind="ExternalInput").ap()
    w21 = nc.dram_tensor("w2_1", [E, H, D], F32, kind="ExternalInput").ap()
    Wq = nc.dram_tensor("Wq", [D, 4096], F32, kind="ExternalInput").ap()
    Wk = nc.dram_tensor("Wk", [D, 4096], F32, kind="ExternalInput").ap()
    Wv = nc.dram_tensor("Wv", [D, 4096], F32, kind="ExternalInput").ap()
    Wo0 = nc.dram_tensor("Wo0", [4096, D], F32, kind="ExternalInput").ap()
    bo = nc.dram_tensor("bo", [128, 8], F32, kind="ExternalInput").ap()
    Wt = nc.dram_tensor("Wt", [D, D], F32, kind="ExternalInput").ap()
    Wo1 = nc.dram_tensor("Wo1", [D, D], F32, kind="ExternalInput").ap()
    Wg = nc.dram_tensor("Wg", [2 * D, D], F32, kind="ExternalInput").ap()
    par = {}
    for nm in ("dec", "beta", "lncw", "lncb", "lngw", "lngb", "bg"):
        par[nm] = nc.dram_tensor("p_" + nm, [128, 8], F32, kind="ExternalInput").ap()
    for nm in ("g1w", "g1b"):
        par[nm] = nc.dram_tensor("p_" + nm, [128, D], F32, kind="ExternalInput").ap()

    cst = declare_consts(nc)
    for nm, (shape, dt) in {**ATTN0_CONSTS, **ATTN1_CONSTS}.items():
        cst[nm] = nc.dram_tensor("c_" + nm, shape, dt, kind="ExternalInput").ap()

    xln_tm = nc.dram_tensor("xln_tm", [T, D], BF16, kind="Internal").ap()
    expd = nc.dram_tensor("expd", [E * T], F32, kind="Internal").ap()
    xinf = nc.dram_tensor("xinf", [2, 1024, D], BF16, kind="Internal").ap()
    f0 = {}
    for nm, shape in (("qf", [4096, 4096]), ("kf", [4096, 4096]), ("vf", [4096, 4096]),
                      ("oe", [4096, 2048]), ("oo", [4096, 2048]),
                      ("re", [1024, 2048]), ("ro", [1024, 2048])):
        f0[nm] = nc.dram_tensor(nm, shape, BF16, kind="Internal").ap()
    f1 = {"cxf": nc.dram_tensor("cxf", [D, T], BF16, kind="Internal").ap(),
          "outf": nc.dram_tensor("outf", [D, T], BF16, kind="Internal").ap(),
          "mtf": nc.dram_tensor("mtf", [D, T], BF16, kind="Internal").ap()}

    with ExitStack() as gstack:
        c = Ctx(nc, gstack)
        con = load_consts(c, cst)
        cl = c.sem("cl")
        for nm in {**ATTN0_CONSTS, **ATTN1_CONSTS}:
            con[nm] = gstack.enter_context(nc.sbuf_tensor("cs_" + nm,
                                                          *({**ATTN0_CONSTS, **ATTN1_CONSTS}[nm])))
            c.dma("sync", con[nm][:], cst[nm][:], cl)
        for e in ("tensor", "vector", "scalar", "gpsimd"):
            c.wait(e, cl, cl.n)
        disp = {
            "idx32": gstack.enter_context(nc.sbuf_tensor("g_idx32", [128, E, 8], I32)),
            "g128": gstack.enter_context(nc.sbuf_tensor("g_g128", [128, E, 8], F32)),
            "y_sb": gstack.enter_context(nc.sbuf_tensor("g_y", [128, 2, D], F32)),
            "gtm": gstack.enter_context(nc.sbuf_tensor("g_gtm", [128, 1, D], BF16)),
        }

        def barrier(sem_name, val):
            s = c.sem(sem_name)
            for e in ENGINES:
                c.wait(e, s, val)

        def run_sub(fn):
            ss = ExitStack()
            old = c.stack
            c.stack = ss
            try:
                with ss:
                    fn()
            finally:
                c.stack = old

        run_sub(lambda: emit_ffn(c, xi, xA, gwT0, w10, w20, cst, con, xln_tm, expd,
                                 tag="f1", stag="G", disp=disp, xinf=xinf))
        barrier("Gsca", c.sem("Gsca").n)
        run_sub(lambda: emit_attn0(c, xA, xB, Wq, Wk, Wv, Wo0, bo, con, xln_tm, f0,
                                   tag="a0", stag="G"))
        barrier("Grao", c.sem("Grao").n)
        run_sub(lambda: emit_ffn(c, xB, xA, gwT0, w10, w20, cst, con, xln_tm, expd,
                                 tag="f2", stag="G", disp=disp, xinf=xinf))
        barrier("Gsca", c.sem("Gsca").n)
        run_sub(lambda: emit_ffn(c, xA, xB, gwT1, w11, w21, cst, con, xln_tm, expd,
                                 tag="f3", stag="G", disp=disp, xinf=xinf))
        barrier("Gsca", c.sem("Gsca").n)
        run_sub(lambda: emit_attn1(c, xB, xA, Wt, Wo1, Wg, par, con, xln_tm, f1,
                                   tag="a1", stag="G"))
        barrier("Grao", c.sem("Grao").n)
        run_sub(lambda: emit_ffn(c, xA, xo, gwT1, w11, w21, cst, con, xln_tm, expd,
                                 tag="f4", stag="G", disp=disp, xinf=xinf))
        barrier("Gsca", c.sem("Gsca").n)
        c.finalize()
    mybir.codegen_inst_isa_subclasses(nc)
    return nc


_PROGS = {}


def _fused():
    if "fused" not in _PROGS:
        _PROGS["fused"] = build_fused_prog()
    return _PROGS["fused"]


def kernel(L0_attn_x, L0_attn_Wq, L0_attn_Wk, L0_attn_Wv, L0_attn_Wo, L0_attn_bo,
           L0_ffn_gate_w, L0_ffn_w1, L0_ffn_w2,
           L1_attn_Wt, L1_attn_Wo, L1_attn_Wg, L1_attn_bg,
           L1_attn_ln1_w, L1_attn_ln1_b, L1_attn_lnc_w, L1_attn_lnc_b,
           L1_attn_lng_w, L1_attn_lng_b,
           L1_ffn_gate_w, L1_ffn_w1, L1_ffn_w2):
    from concourse.bass_utils import run_bass_kernel_spmd
    nc = _fused()
    NC = 8
    col = lambda a: np.ascontiguousarray(np.asarray(a, np.float32).reshape(8, 128).T)
    rep = lambda a: np.ascontiguousarray(np.broadcast_to(np.asarray(a, np.float32), (128, D)))
    f32c = lambda a: np.ascontiguousarray(np.asarray(a, np.float32))
    decay = (0.5 ** (1.0 / np.linspace(2.0, 8.0, D))).astype(np.float32)

    shared = {
        "gate_wT0": f32c(L0_ffn_gate_w), "w1_0": f32c(L0_ffn_w1), "w2_0": f32c(L0_ffn_w2),
        "gate_wT1": np.ascontiguousarray(np.asarray(L1_ffn_gate_w, np.float32).T),
        "w1_1": f32c(L1_ffn_w1), "w2_1": f32c(L1_ffn_w2),
        "Wq": f32c(L0_attn_Wq), "Wk": f32c(L0_attn_Wk), "Wv": f32c(L0_attn_Wv),
        "Wo0": f32c(L0_attn_Wo), "bo": col(L0_attn_bo),
        "Wt": f32c(L1_attn_Wt), "Wo1": f32c(L1_attn_Wo), "Wg": f32c(L1_attn_Wg),
        "p_dec": col(decay), "p_beta": col(1.0 - decay),
        "p_lncw": col(L1_attn_lnc_w), "p_lncb": col(L1_attn_lnc_b),
        "p_lngw": col(L1_attn_lng_w), "p_lngb": col(L1_attn_lng_b),
        "p_bg": col(L1_attn_bg),
        "p_g1w": rep(L1_attn_ln1_w), "p_g1b": rep(L1_attn_ln1_b),
    }
    for k, v in host_consts().items():
        shared["c_" + k] = v
    for k, v in attn0_host_consts().items():
        shared["c_" + k] = v
    for k, v in attn1_host_consts().items():
        shared["c_" + k] = v

    x = np.ascontiguousarray(np.asarray(L0_attn_x, np.float32))
    in_maps = []
    for i in range(NC):
        m = dict(shared)
        m["x_in"] = np.ascontiguousarray(x[i])
        in_maps.append(m)
    res = run_bass_kernel_spmd(nc, in_maps, core_ids=list(range(NC)))
    return np.stack([np.asarray(r["x_out"]) for r in res.results], 0)


# version 05 (final)
# speedup vs baseline: 1.0182x; 1.0182x over previous
import math
from contextlib import ExitStack

import numpy as np
import ml_dtypes

import concourse.bass as bass
import concourse.mybir as mybir
from concourse import library_config
from concourse.bass_core_rust import IndirectOffsetOnAxis

F32 = mybir.dt.float32
BF16 = mybir.dt.bfloat16
I16 = mybir.dt.int16
I32 = mybir.dt.int32
AF = mybir.ActivationFunctionType
ALU = mybir.AluOpType
bfdt = ml_dtypes.bfloat16

T = 4096
D = 1024
E = 8
CAP = 1024
H = 4096
NTILE = T // 128

ENGINES = ("sync", "scalar", "vector", "tensor", "gpsimd")


class Sem:
    def __init__(self, handle, name):
        self.h = handle
        self.name = name
        self.n = 0

    def add(self, k):
        self.n += k
        return self.n


class Ctx:

    def __init__(self, nc, stack):
        self.nc = nc
        self.stack = stack
        self.ops = {e: [] for e in ENGINES}
        self.sems = {}
        self.last_op = {}

    def sem(self, name):
        if name not in self.sems:
            h = self.stack.enter_context(self.nc.semaphore(name))
            self.sems[name] = Sem(h, name)
        return self.sems[name]

    def emit(self, engine, fn):
        self.ops[engine].append(fn)

    def wait(self, engine, sem, val):
        self.emit(engine, lambda eng, s=sem.h, v=val: eng.wait_ge(s, v))

    def dma(self, engine, out, in_, sem, **kw):
        v = sem.add(16)
        def f(eng, out=out, in_=in_, s=sem.h, kw=kw):
            eng.dma_start(out=out, in_=in_, **kw).then_inc(s, 16)
        self.emit(engine, f)
        return v

    def dma_tp(self, engine, out, in_, sem):
        v = sem.add(16)
        def f(eng, out=out, in_=in_, s=sem.h):
            eng.dma_start_transpose(out=out, in_=in_).then_inc(s, 16)
        self.emit(engine, f)
        return v

    def op(self, engine, meth, sem=None, ninc=1, **kw):
        chain = engine in ("vector", "scalar", "gpsimd")
        if chain:
            prev = self.last_op.get(engine)
            if prev is not None:
                self.wait(engine, prev[0], prev[1])
        if sem is None and chain:
            sem = self.sem("_self_" + engine)
            ninc = 1
        v = sem.add(ninc) if sem is not None else None
        def f(eng, meth=meth, kw=kw, s=sem, v=v, ninc=ninc):
            ins = getattr(eng, meth)(**kw)
            if s is not None:
                ins.then_inc(s.h, ninc)
        self.emit(engine, f)
        if chain:
            self.last_op[engine] = (sem, v)
        return v

    def finalize(self):
        nc = self.nc
        ops = self.ops
        with nc.Block() as blk:
            @blk.sync
            def _(eng):
                for f in ops["sync"]:
                    f(eng)

            @blk.scalar
            def _(eng):
                for f in ops["scalar"]:
                    f(eng)

            @blk.vector
            def _(eng):
                for f in ops["vector"]:
                    f(eng)

            @blk.tensor
            def _(eng):
                for f in ops["tensor"]:
                    f(eng)

            @blk.gpsimd
            def _(eng):
                for f in ops["gpsimd"]:
                    f(eng)



def host_consts():
    c = {}
    c["eye_f32"] = np.eye(128, dtype=np.float32)
    S = np.zeros((128, 128), np.float32)
    for p in range(128):
        for p2 in range(128):
            if p % 16 == p2 % 16:
                S[p, p2] = 1.0
    c["Snorm"] = S
    G8 = np.zeros((128, 8), np.float32)
    for p in range(128):
        G8[p, p // 16] = 1.0
    c["G8"] = G8
    E16 = np.zeros((8, 128), np.float32)
    for p in range(128):
        E16[p // 16, p] = 1.0
    c["E16"] = E16
    L = np.zeros((128, 128), np.float32)
    for p in range(128):
        for p2 in range(128):
            if p // 16 == p2 // 16 and (p2 % 16) < (p % 16):
                L[p2, p] = 1.0
    c["Ltri"] = L
    c["iota16"] = np.broadcast_to(np.arange(16, dtype=np.float32), (128, 16)).astype(bfdt).copy()
    c["iota8"] = np.broadcast_to(np.arange(8, dtype=np.float32), (128, 8)).astype(bfdt).copy()
    thi = np.zeros((128, 2, 128), np.float32)
    tlo = np.zeros((128, 2, 128), np.float32)
    for p in range(128):
        for h in range(2):
            for col in range(128):
                c256 = col % 16
                t = 256 * c256 + 128 * h + p
                thi[p, h, col] = t >> 5
                tlo[p, h, col] = t & 31
    c["thi"] = thi.astype(bfdt)
    c["tlo"] = tlo.astype(bfdt)
    c["ones1"] = np.ones((128, 1), np.float32).astype(bfdt)
    c["eps5"] = np.full((128, 1), 1e-5, np.float32)
    c["eps6"] = np.full((128, 1), 1e-6, np.float32)
    c["ones1_f32"] = np.ones((128, 1), np.float32)
    return c


CONST_SPECS = {
    "eye_f32": ([128, 128], F32),
    "Snorm": ([128, 128], F32),
    "G8": ([128, 8], F32),
    "E16": ([8, 128], F32),
    "Ltri": ([128, 128], F32),
    "iota16": ([128, 16], BF16),
    "iota8": ([128, 8], BF16),
    "thi": ([128, 2, 128], BF16),
    "tlo": ([128, 2, 128], BF16),
    "ones1": ([128, 1], BF16),
    "eps5": ([128, 1], F32),
    "eps6": ([128, 1], F32),
    "ones1_f32": ([128, 1], F32),
}


def declare_consts(nc):
    aps = {}
    for name, (shape, dt) in CONST_SPECS.items():
        aps[name] = nc.dram_tensor("c_" + name, shape, dt, kind="ExternalInput").ap()
    return aps



def emit_ln(c, bufs, x_in, xln_tm, sem_pre=None, gamma=None, beta=None,
            eps_ap=None, stats_out=None, tag="L"):
    nc = c.nc
    ld = c.sem(tag + "ld")
    ve = c.sem(tag + "ve")
    sc = c.sem(tag + "sc")
    st = c.sem(tag + "st")

    GB = 8
    NG = NTILE // GB
    xt = bufs["ln_x"]
    x2 = bufs["ln_x2"]
    sm = bufs["ln_sm"]
    ot = bufs["ln_out"]
    xv = x_in.rearrange("(g p) d -> p g d", p=128)
    ov = xln_tm.rearrange("(g p) d -> p g d", p=128)

    sq_m, sub_m, ap_m, st_m = {}, {}, {}, {}
    for i in range(NG):
        b = i % 2
        if i >= 2:
            c.wait("sync", *sq_m[i - 2])
            c.wait("sync", *sub_m[i - 2])
        c.dma("sync", xt[:, b], xv[:, GB * i:GB * (i + 1), :], ld)
        c.wait("scalar", ld, ld.n)
        if i >= 1:
            c.wait("scalar", *ap_m[i - 1])
        c.op("scalar", "activation", out=x2.rearrange("p g d -> p (g d)"),
             in_=xt[:, b].rearrange("p g d -> p (g d)"), func=AF.Square)
        sq_m[i] = c.last_op["scalar"]
        c.wait("vector", ld, ld.n)
        c.op("vector", "tensor_reduce", out=sm[:, b, 0, :], in_=xt[:, b],
             op=ALU.add, axis=mybir.AxisListType.X)
        c.wait("vector", *sq_m[i])
        c.op("vector", "tensor_reduce", out=sm[:, b, 1, :], in_=x2[:],
             op=ALU.add, axis=mybir.AxisListType.X)
        c.op("vector", "tensor_scalar", out=sm[:, b, 2, :], in0=sm[:, b, 0, :],
             scalar1=1.0 / D, scalar2=None, op0=ALU.mult)
        c.op("vector", "tensor_scalar", out=sm[:, b, 3, :], in0=sm[:, b, 1, :],
             scalar1=1.0 / D, scalar2=None, op0=ALU.mult)
        c.op("vector", "tensor_tensor", out=sm[:, b, 4, :], in0=sm[:, b, 2, :],
             in1=sm[:, b, 2, :], op=ALU.mult)
        c.op("vector", "tensor_tensor", out=sm[:, b, 4, :], in0=sm[:, b, 3, :],
             in1=sm[:, b, 4, :], op=ALU.subtract)
        var_m = c.last_op["vector"]
        c.wait("scalar", *var_m)
        c.op("scalar", "activation", out=sm[:, b, 5, :], in_=sm[:, b, 4, :],
             func=AF.Sqrt, bias=eps_ap)
        sd_m = c.last_op["scalar"]
        c.wait("vector", *sd_m)
        c.op("vector", "reciprocal", out=sm[:, b, 6, :], in_=sm[:, b, 5, :])
        mb = sm[:, b, 2, :].unsqueeze(2).broadcast_to([128, GB, D])
        rb = sm[:, b, 6, :].unsqueeze(2).broadcast_to([128, GB, D])
        c.op("vector", "tensor_tensor", out=x2[:], in0=xt[:, b], in1=mb,
             op=ALU.subtract)
        sub_m[i] = c.last_op["vector"]
        if i >= 2:
            c.wait("vector", st, 16 * st_m[i - 2])
        if gamma is None:
            c.op("vector", "tensor_tensor", out=ot[:, b], in0=x2[:], in1=rb,
                 op=ALU.mult)
        else:
            c.op("vector", "tensor_tensor", out=x2[:], in0=x2[:], in1=rb,
                 op=ALU.mult)
            c.op("vector", "tensor_tensor", out=x2[:], in0=x2[:],
                 in1=gamma[:].unsqueeze(1).broadcast_to([128, GB, D]),
                 op=ALU.mult)
            c.op("vector", "tensor_tensor", out=ot[:, b], in0=x2[:],
                 in1=beta[:].unsqueeze(1).broadcast_to([128, GB, D]),
                 op=ALU.add)
        ap_m[i] = c.last_op["vector"]
        c.wait("sync", *ap_m[i])
        c.dma("sync", ov[:, GB * i:GB * (i + 1), :], ot[:, b], st)
        st_m[i] = st.n // 16
    return st.n



def emit_ffn(c, x_in, x_out, gate_wT, w1, w2, cst, con, xln_tm, expd, tag="", stag=None, disp=None, xinf=None, dbg=None, routing_only=False):
    if stag is None:
        stag = tag
    """con: dict of const SBUF tiles (loaded by caller)."""
    nc = c.nc
    st_ = c.stack
    t_ = lambda n: tag + n
    s_ = lambda n: stag + n
    sb = lambda name, shape, dt: st_.enter_context(nc.sbuf_tensor(t_(name), shape, dt))

    if xinf is None:
        xinf = nc.dram_tensor(t_("xinf"), [2, 1024, D], BF16, kind="Internal").ap()
    if disp is None:
        disp = {
            "idx32": sb("idx32", [128, E, 8], I32),
            "g128": sb("g128", [128, E, 8], F32),
            "y_sb": sb("ex_y", [128, 2, D], F32),
            "gtm": sb("gtm", [128, 1, D], BF16),
        }
    idx32 = disp["idx32"]
    g128 = disp["g128"]
    gw_sb = sb("gw", [128, 8, E], BF16)
    w1sb = sb("w1sb", [128, 8, H], BF16)

    seedS = c.sem(s_("seed"))
    c.dma("sync", x_out[:], x_in[:], seedS)
    seed_done = seedS.n

    mm = c.sem(s_("mm"))
    ve = c.sem(s_("ve"))
    sc = c.sem(s_("sc"))
    po = c.sem(s_("po"))
    ld = c.sem(s_("ld"))
    gat = c.sem(s_("gat"))
    sca = c.sem(s_("sca"))
    w1c = c.sem(s_("w1c"))
    w2c = c.sem(s_("w2c"))
    mmG = c.sem(s_("mmG"))
    mmK = c.sem(s_("mmK"))
    mmY = c.sem(s_("mmY"))
    geS = c.sem(s_("geS"))
    yv = c.sem(s_("yv"))
    gld = c.sem(s_("gld"))
    xga = c.sem(s_("xga"))

    w1c_done = {}
    w2c_done = {}
    mmG_b = c.sem(s_("mmG")).n
    mmK_b = c.sem(s_("mmK")).n
    mmY_b = c.sem(s_("mmY")).n
    geS_b = c.sem(s_("geS")).n
    yv_b = c.sem(s_("yv")).n

    def emit_w1c(e):
        if e > 0:
            c.wait("gpsimd", mmG, mmG_b + 64 * e)
        vals = []
        for k in range(8):
            v = c.dma("gpsimd", w1sb[:, k, :], w1[e, 128 * k:128 * (k + 1), :], w1c)
            vals.append(v)
        w1c_done[e] = vals

    emit_w1c(0)
    c.dma("gpsimd", gw_sb[:], gate_wT.rearrange("(c p) e -> p c e", p=128), gld)
    gw_ok = gld.n

    with ExitStack() as ph:
        sbp = lambda name, shape, dt: ph.enter_context(nc.sbuf_tensor(t_("ln_" + name), shape, dt))
        bufs = {
            "ln_x": sbp("x", [128, 2, 8, D], F32),
            "ln_x2": sbp("x2", [128, 8, D], F32),
            "ln_sm": sbp("sm", [128, 2, 8, 8], F32),
            "ln_out": sbp("o", [128, 2, 8, D], BF16),
        }
        emit_ln(c, bufs, x_in, xln_tm, eps_ap=con["eps5"][:, 0:1], tag=s_("L"))
    ln_st = c.sems[s_("L") + "st"]
    ln_done = ln_st.n
    for _e in ("gpsimd", "vector", "scalar"):
        c.wait(_e, ln_st, ln_done)

    with ExitStack() as ph:
        sbp = lambda name, shape, dt: ph.enter_context(nc.sbuf_tensor(t_("rt_" + name), shape, dt))
        psp = lambda name, shape: ph.enter_context(nc.psum_tensor(t_("rp_" + name), shape, F32))
        xlnT = sbp("xlnT", [128, 2, T], BF16)
        exp_s = sbp("exps", [8, T], F32)
        expc = sbp("expc", [128, 256], F32)
        rnorm = sbp("rnorm", [128, 256], F32)
        probs = sbp("probs", [128, 256], F32)
        junk = sbp("junk", [128, 256], BF16)
        cntp = sbp("cntp", [128, 1], F32)
        bis = sbp("bis", [8, 8], F32)
        thc = sbp("thc", [128, 1], F32)
        offs = sbp("offs", [128, 1], F32)
        maskc = sbp("maskc", [128, 256], F32)
        cum = sbp("cum", [128, 256], F32)
        pos = sbp("pos", [128, 256], F32)
        sel = sbp("sel", [128, 256], F32)
        pos_m = sbp("pos_m", [128, 256], F32)
        posT = sbp("posT", [128, 2, 128], F32)
        probT = sbp("probT", [128, 2, 128], F32)
        q_i16 = sbp("qi", [128, 2, 128], I16)
        qf = sbp("qf", [128, 2, 128], F32)
        q_bf = sbp("qb", [128, 2, 128], BF16)
        r_bf = sbp("rb", [128, 2, 128], BF16)
        j_i16 = sbp("ji", [128, 2, 128], I16)
        jf = sbp("jf", [128, 2, 128], F32)
        j_bf = sbp("jb", [128, 2, 128], BF16)
        a_bf = sbp("ab", [128, 2, 128], BF16)
        rf_t = sbp("rf", [128, 2, 128], F32)
        jr_t = sbp("jr", [128, 2, 128], F32)
        neg_t = sbp("neg", [128, 2, 128], F32)
        phi = sbp("phi", [128, 2, 128], BF16)
        plo = sbp("plo", [128, 2, 128], BF16)
        GW = 32
        Fl16 = sbp("Fl16", [128, 2, GW, 16], BF16)
        Fm8 = sbp("Fm8", [128, 2, GW, 8], BF16)
        Fh8 = sbp("Fh8", [128, 2, GW, 8], BF16)
        Fh8w = sbp("Fh8w", [128, 2, GW, 32], BF16)
        Fl128 = sbp("Fl128", [128, 2, GW, 8, 16], BF16)
        g2s = sbp("g2s", [128, 32], F32)
        idx128f = sbp("idx128f", [128, 8], F32)
        gtmp = sbp("gtmp", [128, 8], F32)
        ps_sc_stack = ExitStack()
        ps_sc = ps_sc_stack.enter_context(nc.psum_tensor(t_("rp_sc"), [8, T], F32))
        xbar_vals = []
        mm_base = mm.n
        gate_mm_marks = [mm_base + ci + 1 for ci in range(8)]
        for ci in range(8):
            if ci >= 2:
                c.wait("sync", mm, gate_mm_marks[ci - 2])
            if ci == 0:
                c.wait("sync", ln_st, ln_done)
            v = c.dma_tp("sync", xlnT[:, ci % 2, :], xln_tm[:, 128 * ci:128 * (ci + 1)], ld)
            xbar_vals.append(v)
        for ci in range(8):
            c.wait("tensor", ld, xbar_vals[ci])
            if ci == 0:
                c.wait("tensor", gld, gw_ok)
            for n in range(8):
                c.op("tensor", "matmul", sem=(mm if n == 7 else None),
                     out=ps_sc[:, 512 * n:512 * (n + 1)],
                     lhsT=gw_sb[:, ci, :], rhs=xlnT[:, ci % 2, 512 * n:512 * (n + 1)],
                     start=(ci == 0), stop=(ci == 7), skip_group_check=True)
            assert mm.n == gate_mm_marks[ci]
        gate_mm_done = mm.n

        c.wait("scalar", mm, gate_mm_done)
        for n in range(8):
            c.op("scalar", "activation", sem=(sc if n == 7 else None),
                 out=exp_s[:, 512 * n:512 * (n + 1)], in_=ps_sc[:, 512 * n:512 * (n + 1)],
                 func=AF.Exp)
        ps_sc_stack.close()
        ps_nrm = psp("nrm", [128, 256])
        ps_c8 = psp("c8", [8, 1])
        ps_thc = psp("thc", [128, 1])
        ps_off = psp("off", [128, 1])
        ps_tp = psp("tp", [128, 4, 128])
        ps_G2 = psp("G2", [128, 32])
        c.wait("sync", sc, sc.n)
        v1 = c.dma("sync", expd[:].rearrange("(e t) -> e t", e=8), exp_s[:], ld)
        c.wait("sync", ld, v1)
        v2 = c.dma("sync", expc[:], expd[:].rearrange("(p t) -> p t", p=128), ld)
        c.wait("tensor", ld, v2)
        c.op("tensor", "matmul", sem=mm, out=ps_nrm[:], lhsT=con["Snorm"][:],
             rhs=expc[:], start=True, stop=True)
        c.wait("vector", mm, mm.n)
        c.op("vector", "reciprocal", out=rnorm[:], in_=ps_nrm[:])
        c.op("vector", "tensor_tensor", sem=ve, out=probs[:], in0=expc[:],
             in1=rnorm[:], op=ALU.mult)

        c.op("vector", "memset", ap=bis[:], constant=0.0)
        c.op("vector", "memset", ap=bis[:, 1:2], constant=1.0)
        c.op("vector", "memset", ap=bis[:, 2:3], constant=0.5)
        c.op("vector", "memset", sem=ve, ap=thc[:], constant=0.5)
        NIT = 16
        for it in range(NIT):
            c.op("vector", "tensor_scalar", out=junk[:], in0=probs[:],
                 scalar1=thc[:], scalar2=None, op0=ALU.is_ge)
            c.op("vector", "reduce_sum", sem=ve, out=cntp[:], in_=junk[:],
                 axis=mybir.AxisListType.X)
            c.wait("tensor", ve, ve.n)
            c.op("tensor", "matmul", sem=mm, out=ps_c8[:], lhsT=con["G8"][:],
                 rhs=cntp[:], start=True, stop=True)
            c.wait("vector", mm, mm.n)
            c.op("vector", "tensor_copy", out=bis[:, 3:4], in_=ps_c8[:])
            c.op("vector", "tensor_scalar", out=bis[:, 4:5], in0=bis[:, 3:4],
                 scalar1=1023.5, scalar2=None, op0=ALU.is_gt)
            c.op("vector", "tensor_tensor", out=bis[:, 5:6], in0=bis[:, 2:3], in1=bis[:, 0:1], op=ALU.subtract)
            c.op("vector", "tensor_tensor", out=bis[:, 5:6], in0=bis[:, 4:5], in1=bis[:, 5:6], op=ALU.mult)
            c.op("vector", "tensor_tensor", out=bis[:, 0:1], in0=bis[:, 0:1], in1=bis[:, 5:6], op=ALU.add)
            c.op("vector", "tensor_tensor", out=bis[:, 6:7], in0=bis[:, 1:2], in1=bis[:, 2:3], op=ALU.subtract)
            c.op("vector", "tensor_tensor", out=bis[:, 6:7], in0=bis[:, 4:5], in1=bis[:, 6:7], op=ALU.mult)
            c.op("vector", "tensor_tensor", out=bis[:, 1:2], in0=bis[:, 2:3], in1=bis[:, 6:7], op=ALU.add)
            c.op("vector", "tensor_tensor", out=bis[:, 7:8], in0=bis[:, 0:1], in1=bis[:, 1:2], op=ALU.add)
            c.op("vector", "tensor_scalar", sem=ve, out=bis[:, 2:3], in0=bis[:, 7:8],
                 scalar1=0.5, scalar2=None, op0=ALU.mult)
            if it < NIT - 1:
                c.wait("tensor", ve, ve.n)
                c.op("tensor", "matmul", sem=mm, out=ps_thc[:], lhsT=con["E16"][:],
                     rhs=bis[:, 2:3], start=True, stop=True)
                c.wait("vector", mm, mm.n)
                c.op("vector", "tensor_copy", sem=ve, out=thc[:], in_=ps_thc[:])
        c.wait("tensor", ve, ve.n)
        c.op("tensor", "matmul", sem=mm, out=ps_thc[:], lhsT=con["E16"][:],
             rhs=bis[:, 0:1], start=True, stop=True)
        c.wait("vector", mm, mm.n)
        c.op("vector", "tensor_copy", out=thc[:], in_=ps_thc[:])

        c.op("vector", "tensor_scalar", out=maskc[:], in0=probs[:],
             scalar1=thc[:], scalar2=None, op0=ALU.is_ge)
        c.op("vector", "tensor_tensor_scan", out=cum[:],
             data0=con["ones1_f32"][:].broadcast_to([128, 256]), data1=maskc[:],
             initial=0.0, op0=ALU.mult, op1=ALU.add)
        c.op("vector", "tensor_tensor", sem=ve, out=pos[:], in0=cum[:], in1=maskc[:], op=ALU.subtract)
        c.wait("tensor", ve, ve.n)
        c.op("tensor", "matmul", sem=mm, out=ps_off[:], lhsT=con["Ltri"][:],
             rhs=cum[:, 255:256], start=True, stop=True)
        c.wait("vector", mm, mm.n)
        c.op("vector", "tensor_copy", out=offs[:], in_=ps_off[:])
        c.op("vector", "tensor_scalar", out=pos[:], in0=pos[:], scalar1=offs[:],
             scalar2=None, op0=ALU.add)
        c.op("vector", "tensor_scalar", out=sel[:], in0=pos[:], scalar1=1023.5,
             scalar2=None, op0=ALU.is_lt)
        c.op("vector", "tensor_tensor", out=sel[:], in0=maskc[:], in1=sel[:], op=ALU.mult)
        c.op("vector", "tensor_scalar", out=pos_m[:], in0=pos[:], scalar1=4096.0,
             scalar2=None, op0=ALU.subtract)
        c.op("vector", "tensor_tensor", out=pos_m[:], in0=pos_m[:], in1=sel[:], op=ALU.mult)
        c.op("vector", "tensor_scalar", sem=ve, out=pos_m[:], in0=pos_m[:],
             scalar1=4096.0, scalar2=None, op0=ALU.add)

        c.wait("tensor", ve, ve.n)
        for h in range(2):
            c.op("tensor", "transpose", out=ps_tp[:, h, :], in_=pos_m[:, 128 * h:128 * (h + 1)],
                 identity=con["eye_f32"][:])
            c.op("tensor", "transpose", sem=(mm if h == 1 else None),
                 out=ps_tp[:, 2 + h, :], in_=probs[:, 128 * h:128 * (h + 1)],
                 identity=con["eye_f32"][:])
        c.wait("vector", mm, mm.n)
        for h in range(2):
            c.op("vector", "tensor_copy", out=posT[:, h, :], in_=ps_tp[:, h, :])
            c.op("vector", "tensor_copy", sem=(ve if h == 1 else None),
                 out=probT[:, h, :], in_=ps_tp[:, 2 + h, :])

        rf = qf
        c.op("vector", "tensor_scalar", out=q_i16[:], in0=posT[:],
             scalar1=1.0 / 16, scalar2=None, op0=ALU.mult)
        c.op("vector", "tensor_copy", out=qf[:], in_=q_i16[:])
        c.op("vector", "scalar_tensor_tensor", out=rf_t[:], in0=qf[:], scalar=-16.0,
             in1=posT[:], op0=ALU.mult, op1=ALU.add)
        c.op("vector", "tensor_scalar", out=neg_t[:], in0=rf_t[:],
             scalar1=0.0, scalar2=None, op0=ALU.is_lt)
        c.op("vector", "tensor_tensor", out=qf[:], in0=qf[:], in1=neg_t[:], op=ALU.subtract)
        c.op("vector", "scalar_tensor_tensor", out=rf_t[:], in0=neg_t[:], scalar=16.0,
             in1=rf_t[:], op0=ALU.mult, op1=ALU.add)
        c.op("vector", "tensor_copy", out=q_bf[:], in_=qf[:])
        c.op("vector", "tensor_copy", out=r_bf[:], in_=rf_t[:])
        c.op("vector", "tensor_scalar", out=j_i16[:], in0=posT[:],
             scalar1=1.0 / 128, scalar2=None, op0=ALU.mult)
        c.op("vector", "tensor_copy", out=jf[:], in_=j_i16[:])
        c.op("vector", "scalar_tensor_tensor", out=jr_t[:], in0=jf[:], scalar=-128.0,
             in1=posT[:], op0=ALU.mult, op1=ALU.add)
        c.op("vector", "tensor_scalar", out=neg_t[:], in0=jr_t[:],
             scalar1=0.0, scalar2=None, op0=ALU.is_lt)
        c.op("vector", "tensor_tensor", out=jf[:], in0=jf[:], in1=neg_t[:], op=ALU.subtract)
        c.op("vector", "tensor_copy", out=j_bf[:], in_=jf[:])
        c.op("vector", "scalar_tensor_tensor", out=a_bf[:], in0=jf[:], scalar=-8.0,
             in1=qf[:], op0=ALU.mult, op1=ALU.add)
        c.op("vector", "tensor_copy", out=phi[:], in_=probT[:])
        c.op("vector", "tensor_tensor", sem=ve, out=plo[:], in0=probT[:], in1=phi[:],
             op=ALU.subtract)
        cols_ve = ve.n

        def iotab(nm, w):
            return con[nm][:].unsqueeze(1).unsqueeze(2).broadcast_to([128, 2, GW, w])

        def colb(ap, g, w):
            return ap[:, :, GW * g:GW * (g + 1)].unsqueeze(3).broadcast_to([128, 2, GW, w])

        for g in range(4):
            if g > 0:
                c.wait("vector", mm, mm.n)
            c.op("vector", "tensor_tensor", out=Fl16[:],
                 in0=iotab("iota16", 16), in1=colb(r_bf[:], g, 16), op=ALU.is_equal)
            c.op("vector", "tensor_tensor", out=Fm8[:],
                 in0=iotab("iota8", 8), in1=colb(a_bf[:], g, 8), op=ALU.is_equal)
            c.op("vector", "tensor_tensor", out=Fh8[:],
                 in0=iotab("iota8", 8), in1=colb(j_bf[:], g, 8), op=ALU.is_equal)
            c.op("vector", "tensor_tensor", out=Fh8w[:, :, :, 0:8], in0=Fh8[:],
                 in1=colb(phi[:], g, 8), op=ALU.mult)
            c.op("vector", "tensor_tensor", out=Fh8w[:, :, :, 8:16], in0=Fh8[:],
                 in1=colb(plo[:], g, 8), op=ALU.mult)
            c.op("vector", "tensor_tensor", out=Fh8w[:, :, :, 16:24], in0=Fh8[:],
                 in1=con["thi"][:, :, GW * g:GW * (g + 1)].unsqueeze(3).broadcast_to([128, 2, GW, 8]),
                 op=ALU.mult)
            c.op("vector", "tensor_tensor", sem=ve, out=Fh8w[:, :, :, 24:32], in0=Fh8[:],
                 in1=con["tlo"][:, :, GW * g:GW * (g + 1)].unsqueeze(3).broadcast_to([128, 2, GW, 8]),
                 op=ALU.mult)
            c.op("vector", "tensor_tensor", sem=ve, out=Fl128[:],
                 in0=Fl16[:].unsqueeze(3).broadcast_to([128, 2, GW, 8, 16]),
                 in1=Fm8[:].unsqueeze(4).broadcast_to([128, 2, GW, 8, 16]), op=ALU.mult)
            c.wait("tensor", ve, ve.n)
            for el in range(2):
                e = 2 * g + el
                for c128 in range(32):
                    h = c128 & 1
                    col = 16 * el + (c128 >> 1)
                    c.op("tensor", "matmul", sem=(mm if c128 == 31 else None),
                         out=ps_G2[:], lhsT=Fl128[:, h, col, :, :].rearrange("p a l -> p (a l)"),
                         rhs=Fh8w[:, h, col, :],
                         start=(c128 == 0), stop=(c128 == 31), skip_group_check=True)
                c.wait("vector", mm, mm.n)
                c.op("vector", "tensor_copy", out=g2s[:], in_=ps_G2[:])
                c.op("vector", "tensor_tensor", out=gtmp[:], in0=g2s[:, 0:8], in1=g2s[:, 8:16], op=ALU.add)
                c.op("vector", "tensor_scalar", out=g128[:, e, :], in0=gtmp[:],
                     scalar1=0.5, scalar2=None, op0=ALU.mult)
                c.op("vector", "scalar_tensor_tensor", out=idx128f[:], in0=g2s[:, 16:24],
                     scalar=32.0, in1=g2s[:, 24:32], op0=ALU.mult, op1=ALU.add)
                c.op("vector", "tensor_copy", sem=ve, out=idx32[:, e, :], in_=idx128f[:])
                c.wait("tensor", ve, ve.n)
        routing_done = ve.n
        if routing_only:
            dsem = c.sem(t_("dsem"))
            c.wait("sync", ve, routing_done)
            for nm, tile in dbg.items():
                if nm.startswith("rt_"):
                    c.dma("sync", tile, {"rt_jbf": j_bf, "rt_abf": a_bf, "rt_phi": phi,
                                         "rt_plo": plo, "rt_Fh8": Fh8, "rt_Fm8": Fm8,
                                         "rt_Fh8w": Fh8w, "rt_Fl16": Fl16, "rt_g2s": g2s,
                                         "rt_posT": posT, "rt_probT": probT, "rt_qbf": q_bf,
                                         "rt_probs": probs, "rt_expc": expc,
                                         "rt_rnorm": rnorm, "rt_exps": exp_s,
                                         "rt_maskc": maskc, "rt_pos": pos, "rt_thc": thc,
                                         "rt_rbf": r_bf}[nm][:], dsem)
            c.dma("sync", dbg["idx32"][:], idx32[:], dsem)
            c.dma("sync", dbg["g128"][:], g128[:], dsem)
            if "xln" in dbg:
                c.dma("sync", dbg["xln"][:], xln_tm[:], dsem)
            c.wait("sync", dsem, dsem.n)
            return None

    xin = sb("ex_xin", [128, 2, 8, 1024], BF16)
    y_sb = disp["y_sb"]
    gtm = disp["gtm"]
    with ExitStack() as ph:
        sbp = lambda name, shape, dt: ph.enter_context(nc.sbuf_tensor(t_("ex_" + name), shape, dt))
        w2sb = sbp("w2sb", [128, 32, D], BF16)
        h_sb = sbp("h", [128, 32, 512], BF16)
        ps8 = ph.enter_context(nc.psum_tensor(t_("ps8"), [128, 8, 512], F32))

        gather_done = {}
        scatter_done = {}
        gtm_free = [None]
        xinf_free = {}

        def emit_gathers(e):
            c.wait("gpsimd", ve, routing_done)
            if e >= 2:
                c.wait("gpsimd", xga, xinf_free[e % 2])
            for j in range(8):
                if gtm_free[0] is not None:
                    c.wait("gpsimd", gat, gtm_free[0])
                v = gat.add(16)
                def f(eng, e=e, j=j):
                    eng.indirect_dma_start(
                        out=gtm[:, 0, :], out_offset=None, in_=xln_tm[:],
                        in_offset=IndirectOffsetOnAxis(idx32[:, e, j:j + 1], 0)
                    ).then_inc(gat.h, 16)
                c.emit("gpsimd", f)
                c.wait("gpsimd", gat, gat.n)
                vout = c.dma("gpsimd", xinf[e % 2][128 * j:128 * (j + 1), :],
                             gtm[:, 0, :], gat)
                gtm_free[0] = vout
            c.wait("sync", gat, gtm_free[0])
            if e >= 2:
                c.wait("sync", mmG, mmG_b + 64 * (e - 1))
            for k in range(8):
                c.dma_tp("sync", xin[:, e % 2, k, :],
                         xinf[e % 2][:, 128 * k:128 * (k + 1)], xga)
            xinf_free[e % 2] = xga.n
            gather_done[e] = xga.n

        def emit_w2c(e):
            if e == 0:
                c.wait("gpsimd", ve, routing_done)
            for k in range(32):
                if e > 0:
                    if k <= 30:
                        c.wait("gpsimd", mmK, mmK_b + 62 * (e - 1) + 31 + k + 1)
                    else:
                        c.wait("gpsimd", mmY, mmY_b + 8 * e)
                w2c_done[(e, k)] = c.dma("gpsimd", w2sb[:, k, :],
                                         w2[e, 128 * k:128 * (k + 1), :], w2c)

        def emit_scatters(e, hcall):
            if True:
                for s in range(4):
                    c.wait("gpsimd", yv, yv_b + 8 * e + 4 * hcall + s + 1)
                    c.wait("gpsimd", seedS, seed_done)
                    if sca.n > 0:
                        c.wait("gpsimd", sca, sca.n)
                    v = sca.add(16)
                    def f(eng, e=e, hcall=hcall, s=s):
                        eng.indirect_dma_start(
                            out=x_out[:],
                            out_offset=IndirectOffsetOnAxis(idx32[:, e, 4 * hcall + s:4 * hcall + s + 1], 0),
                            in_=y_sb[:, s % 2, :], in_offset=None,
                            compute_op=ALU.add).then_inc(sca.h, 16)
                    c.emit("gpsimd", f)
                    scatter_done[(e, hcall, s)] = v

        emit_gathers(0)
        emit_gathers(1)
        for e in range(E):
            emit_w2c(e)
            emit_scatters(e, 0)
            if e + 1 < E:
                emit_w1c(e + 1)
            emit_scatters(e, 1)
            if e + 2 < E:
                emit_gathers(e + 2)
        if dbg is not None:
            dbgs = c.sem(t_("dbgs"))
            c.wait("sync", ve, routing_done)
            c.dma("sync", dbg["idx32"][:], idx32[:], dbgs)
            c.dma("sync", dbg["g128"][:], g128[:], dbgs)
            c.wait("sync", dbgs, dbgs.n)
        final_sca = sca.n
        c.wait("gpsimd", sca, final_sca)

        grp = 0
        for e in range(E):
            c.wait("tensor", xga, gather_done[e])
            for hl in range(2):
                if 2 * e + hl > 0:
                    c.wait("tensor", yv, yv_b + 8 * e + 4 * hl)
                for mg in range(4):
                    if grp > 0:
                        c.wait("tensor", geS, geS_b + 8 * grp)
                    for k in range(8):
                        if hl == 0 and mg == 0:
                            c.wait("tensor", w1c, w1c_done[e][k])
                        for ml in range(8):
                            m = mg * 8 + ml
                            c.op("tensor", "matmul", sem=(mmG if k == 7 else None),
                                 out=ps8[:, ml, :],
                                 lhsT=w1sb[:, k, 128 * m:128 * (m + 1)],
                                 rhs=xin[:, e % 2, k, 512 * hl:512 * (hl + 1)],
                                 start=(k == 0), stop=(k == 7), skip_group_check=True)
                    for ml in range(8):
                        m = mg * 8 + ml
                        c.wait("scalar", mmG, mmG_b + 8 * grp + ml + 1)
                        c.op("scalar", "activation", sem=geS,
                             out=h_sb[:, m, :], in_=ps8[:, ml, :], func=AF.Gelu)
                    grp += 1
                c.wait("tensor", geS, geS_b + 8 * grp)
                for k in range(32):
                    c.wait("tensor", w2c, w2c_done[(e, k)])
                    for s in range(4):
                        c.op("tensor", "matmul",
                             out=ps8[:, 2 * s, :],
                             lhsT=h_sb[:, k, 128 * s:128 * (s + 1)],
                             rhs=w2sb[:, k, 0:512],
                             start=(k == 0), stop=(k == 31), skip_group_check=True)
                        c.op("tensor", "matmul",
                             sem=(mmY if k == 31 else (mmK if s == 3 else None)),
                             out=ps8[:, 2 * s + 1, :],
                             lhsT=h_sb[:, k, 128 * s:128 * (s + 1)],
                             rhs=w2sb[:, k, 512:1024],
                             start=(k == 0), stop=(k == 31), skip_group_check=True)
                for s in range(4):
                    c.wait("vector", mmY, mmY_b + 4 * (2 * e + hl) + s + 1)
                    if s >= 2:
                        c.wait("vector", sca, scatter_done[(e, hl, s - 2)])
                    elif hl == 1:
                        c.wait("vector", sca, scatter_done[(e, 0, 2 + s)])
                    elif e >= 1:
                        c.wait("vector", sca, scatter_done[(e - 1, 1, 2 + s)])
                    c.op("vector", "tensor_scalar", sem=yv,
                         out=y_sb[:, s % 2, :].rearrange("p (a b) -> p a b", a=2),
                         in0=ps8[:, 2 * s:2 * s + 2, :],
                         scalar1=g128[:, e, 4 * hl + s:4 * hl + s + 1], scalar2=None,
                         op0=ALU.mult)
    return final_sca


def load_consts(c, cst):
    nc = c.nc
    con = {}
    for nm, (shape, dt) in CONST_SPECS.items():
        con[nm] = c.stack.enter_context(nc.sbuf_tensor("cs_" + nm, shape, dt))
    cl = c.sem("cl")
    for nm in CONST_SPECS:
        c.dma("sync", con[nm][:], cst[nm][:], cl)
    for engx in ("tensor", "vector", "scalar", "gpsimd"):
        c.wait(engx, cl, cl.n)
    return con


def build_ffn_program():
    nc = bass.Bass("TRN2", target_bir_lowering=False, detect_race_conditions=False)
    x_in = nc.dram_tensor("x_in", [T, D], F32, kind="ExternalInput").ap()
    gate_wT = nc.dram_tensor("gate_wT", [D, E], F32, kind="ExternalInput").ap()
    w1 = nc.dram_tensor("w1", [E, D, H], F32, kind="ExternalInput").ap()
    w2 = nc.dram_tensor("w2", [E, H, D], F32, kind="ExternalInput").ap()
    x_out = nc.dram_tensor("x_out", [T, D], F32, kind="ExternalOutput").ap()
    cst = declare_consts(nc)
    xln_tm = nc.dram_tensor("xln_tm", [T, D], BF16, kind="Internal").ap()
    expd = nc.dram_tensor("expd", [E * T], F32, kind="Internal").ap()

    dbg = {}
    dbg["idx"] = nc.dram_tensor("dbg_idx", [128, E, 64], I16, kind="ExternalOutput").ap()
    dbg["idx32"] = nc.dram_tensor("dbg_idx32", [128, E, 8], I32, kind="ExternalOutput").ap()
    dbg["g128"] = nc.dram_tensor("dbg_g128", [128, E, 8], F32, kind="ExternalOutput").ap()
    with ExitStack() as stack:
        c = Ctx(nc, stack)
        con = load_consts(c, cst)
        emit_ffn(c, x_in, x_out, gate_wT, w1, w2, cst, con, xln_tm, expd, dbg=dbg)
        c.finalize()
    mybir.codegen_inst_isa_subclasses(nc)
    return nc


def ffn_inputs(x, gate_w, w1, w2, l0_style=True):
    cc = host_consts()
    gw = gate_w if l0_style else np.ascontiguousarray(gate_w.T)
    m = {"x_in": np.ascontiguousarray(x), "gate_wT": np.ascontiguousarray(gw),
         "w1": np.ascontiguousarray(w1), "w2": np.ascontiguousarray(w2)}
    for k, v in cc.items():
        m["c_" + k] = v
    return m


def build_routing_probe():
    nc = bass.Bass("TRN2", target_bir_lowering=False, detect_race_conditions=False)
    x_in = nc.dram_tensor("x_in", [T, D], F32, kind="ExternalInput").ap()
    gate_wT = nc.dram_tensor("gate_wT", [D, E], F32, kind="ExternalInput").ap()
    w1 = nc.dram_tensor("w1", [E, D, H], F32, kind="ExternalInput").ap()
    w2 = nc.dram_tensor("w2", [E, H, D], F32, kind="ExternalInput").ap()
    x_out = nc.dram_tensor("x_out", [T, D], F32, kind="ExternalOutput").ap()
    cst = declare_consts(nc)
    xln_tm = nc.dram_tensor("xln_tm", [T, D], BF16, kind="Internal").ap()
    expd = nc.dram_tensor("expd", [E * T], F32, kind="Internal").ap()
    dbg = {}
    dbg["idx"] = nc.dram_tensor("dbg_idx", [128, E, 64], I16, kind="ExternalOutput").ap()
    dbg["idx32"] = nc.dram_tensor("dbg_idx32", [128, E, 8], I32, kind="ExternalOutput").ap()
    dbg["g128"] = nc.dram_tensor("dbg_g128", [128, E, 8], F32, kind="ExternalOutput").ap()
    dbg["xln"] = nc.dram_tensor("dbg_xln", [T, D], BF16, kind="ExternalOutput").ap()
    for nm, shape, dt in (("rt_jbf", [128,2,128], BF16), ("rt_abf", [128,2,128], BF16),
                          ("rt_phi", [128,2,128], BF16), ("rt_plo", [128,2,128], BF16),
                          ("rt_Fh8", [128,2,32,8], BF16), ("rt_Fm8", [128,2,32,8], BF16),
                          ("rt_Fh8w", [128,2,32,32], BF16), ("rt_Fl16", [128,2,32,16], BF16),
                          ("rt_g2s", [128,32], F32), ("rt_posT", [128,2,128], F32),
                          ("rt_probT", [128,2,128], F32), ("rt_qbf", [128,2,128], BF16),
                          ("rt_rbf", [128,2,128], BF16),
                          ("rt_probs", [128,256], F32), ("rt_expc", [128,256], F32),
                          ("rt_rnorm", [128,256], F32), ("rt_exps", [8, T], F32),
                          ("rt_maskc", [128,256], F32), ("rt_pos", [128,256], F32),
                          ("rt_thc", [128,1], F32)):
        dbg[nm] = nc.dram_tensor("o" + nm, shape, dt, kind="ExternalOutput").ap()
    with ExitStack() as stack:
        c = Ctx(nc, stack)
        con = load_consts(c, cst)
        emit_ffn(c, x_in, x_out, gate_wT, w1, w2, cst, con, xln_tm, expd, dbg=dbg, routing_only=True)
        c.finalize()
    mybir.codegen_inst_isa_subclasses(nc)
    return nc



ATTN0_CONSTS = {
    "D64": ([128, 64], BF16),
    "DUP": ([64, 128], F32),
}


def attn0_host_consts():
    c = {}
    D64 = np.zeros((128, 64), np.float32)
    for p in range(128):
        D64[p, p & 63] = 1.0
    c["D64"] = D64.astype(bfdt)
    DUP = np.zeros((64, 128), np.float32)
    for p in range(128):
        DUP[p & 63, p] = 1.0
    c["DUP"] = DUP
    return c


def emit_attn0(c, x_in, x_out, Wq, Wk, Wv, Wo, bo, con, xln_tm, files, tag="a0", stag=None):
    if stag is None:
        stag = tag
    nc = c.nc
    st_ = c.stack
    t_ = lambda n: tag + n
    s_ = lambda n: stag + n
    sb = lambda name, shape, dt: st_.enter_context(nc.sbuf_tensor(t_(name), shape, dt))

    mm = c.sem(s_("mm"))
    ld = c.sem(s_("ld"))
    gld = c.sem(s_("gld"))
    stf = c.sem(s_("stf"))
    SCALE = 64.0 ** -0.5
    sc_self = lambda: c.last_op.get("scalar")
    ve_self = lambda: c.last_op.get("vector")

    with ExitStack() as ph:
        sbp = lambda name, shape, dt: ph.enter_context(nc.sbuf_tensor(t_("ln_" + name), shape, dt))
        bufs = {
            "ln_x": sbp("x", [128, 2, 8, D], F32),
            "ln_x2": sbp("x2", [128, 8, D], F32),
            "ln_sm": sbp("sm", [128, 2, 8, 8], F32),
            "ln_out": sbp("o", [128, 2, 8, D], BF16),
        }
        emit_ln(c, bufs, x_in, xln_tm, eps_ap=con["eps5"][:, 0:1], tag=s_("L"))
    ln_st = c.sems[s_("L") + "st"]
    ln_done = ln_st.n
    for _e in ("gpsimd", "vector", "scalar"):
        c.wait(_e, ln_st, ln_done)

    with ExitStack() as ph:
        sbp = lambda name, shape, dt: ph.enter_context(nc.sbuf_tensor(t_("qk_" + name), shape, dt))
        xlnT = sbp("xlnT", [128, 8, T], BF16)
        wt = sbp("wt", [128, 2, 8, 128], BF16)
        ev = sbp("ev", [128, 2, 8, 512], BF16)
        qtmp = sbp("qtmp", [128, 2, 512], F32)
        ks_t = sbp("ks", [64, 2, 512], F32)
        ksr = sbp("ksr", [128, 8, 512], BF16)
        krb = sbp("krb", [128, 2, 8, 512], BF16)
        ps = ph.enter_context(nc.psum_tensor(t_("ps_qk"), [128, 8, 512], F32))

        c.wait("sync", ln_st, ln_done)
        xb_vals = []
        for ci in range(8):
            v = c.dma_tp("sync", xlnT[:, ci, :], xln_tm[:, 128 * ci:128 * (ci + 1)], ld)
            xb_vals.append(v)

        pass_tail = [None, None]

        def gemm_pass(W, outfile, kind):
            base_mm = mm.n
            evdone = []
            for mt in range(32):
                b = mt % 2
                if mt >= 2:
                    c.wait("gpsimd", mm, base_mm + 8 * (mt - 1))
                else:
                    c.wait("gpsimd", mm, base_mm)
                c.dma("gpsimd", wt[:, b],
                      W[:, 128 * mt:128 * (mt + 1)].rearrange("(c p) m -> p c m", p=128), gld)
                wv_ok = gld.n
                c.wait("tensor", gld, wv_ok)
                if mt == 0:
                    for v in xb_vals:
                        c.wait("tensor", ld, v)
                if mt >= 1:
                    prev = c.last_op.get("scalar")
                    if prev is not None:
                        c.wait("tensor", prev[0], prev[1])
                    if kind == "q":
                        pv = c.last_op.get("vector")
                        if pv is not None:
                            c.wait("tensor", pv[0], pv[1])
                for ci in range(8):
                    for n in range(8):
                        c.op("tensor", "matmul", sem=(mm if n == 7 else None),
                             out=ps[:, n, :], lhsT=wt[:, b, ci, :],
                             rhs=xlnT[:, ci, 512 * n:512 * (n + 1)],
                             start=(ci == 0), stop=(ci == 7), skip_group_check=True)
                c.wait("scalar", mm, base_mm + 8 * (mt + 1))
                if mt >= 2:
                    c.wait("scalar", stf, evdone[mt - 2])
                elif pass_tail[mt] is not None:
                    c.wait("scalar", stf, pass_tail[mt])
                vec_marks = {}
                for n in range(8):
                    if kind == "q":
                        if n >= 2:
                            m2 = vec_marks[n - 2]
                            c.wait("scalar", m2[0], m2[1])
                        c.op("scalar", "activation", out=qtmp[:, n % 2, :], in_=ps[:, n, :],
                             func=AF.Relu, scale=SCALE)
                        pv = c.last_op.get("scalar")
                        c.wait("vector", pv[0], pv[1])
                        if mt >= 2 and n == 0:
                            c.wait("vector", stf, evdone[mt - 2])
                        elif mt < 2 and n == 0 and pass_tail[mt] is not None:
                            c.wait("vector", stf, pass_tail[mt])
                        c.op("vector", "tensor_tensor", out=ev[:, b, n, :],
                             in0=qtmp[:, n % 2, :], in1=ksr[:, n, :], op=ALU.mult)
                        vec_marks[n] = c.last_op["vector"]
                    elif kind == "k":
                        c.op("scalar", "activation", out=ev[:, b, n, :], in_=ps[:, n, :],
                             func=AF.Relu)
                    else:
                        c.op("scalar", "activation", out=ev[:, b, n, :], in_=ps[:, n, :],
                             func=AF.Copy)
                prev = c.last_op.get("vector" if kind == "q" else "scalar")
                c.wait("sync", prev[0], prev[1])
                v = c.dma("sync", outfile[128 * mt:128 * (mt + 1), :].rearrange("p (n t) -> p n t", n=8),
                          ev[:, b], stf)
                evdone.append(v)
            pass_tail[0] = evdone[-2]
            pass_tail[1] = evdone[-1]
            return base_mm

        gemm_pass(Wk, files["kf"], "k")
        k_done = stf.n

        pv = c.last_op["scalar"]
        c.wait("tensor", pv[0], pv[1])
        for nt in range(8):
            for grp in range(4):
                b = grp % 2
                if grp >= 2 or nt > 0:
                    c.wait("sync", mm, mm.n)
                if nt == 0 and grp == 0:
                    c.wait("sync", stf, k_done)
                c.dma("sync", krb[:, b],
                      files["kf"][1024 * grp:1024 * (grp + 1), 512 * nt:512 * (nt + 1)]
                      .rearrange("(ct p) t -> p ct t", p=128), ld)
                c.wait("tensor", ld, ld.n)
                for ct in range(8):
                    c.op("tensor", "matmul",
                         sem=(mm if (ct == 7) else None),
                         out=ps[0:64, 0, :], lhsT=con["D64"][:], rhs=krb[:, b, ct, :],
                         start=(grp == 0 and ct == 0), stop=(grp == 3 and ct == 7),
                         skip_group_check=True)
            c.wait("vector", mm, mm.n)
            c.op("vector", "tensor_copy", out=ks_t[:, nt % 2, :], in_=ps[0:64, 0, :])
            pv = c.last_op.get("vector")
            c.wait("tensor", pv[0], pv[1])
            c.op("tensor", "matmul", sem=mm, out=ps[:, 1, :], lhsT=con["DUP"][:],
                 rhs=ks_t[:, nt % 2, :], start=True, stop=True, skip_group_check=True)
            c.wait("vector", mm, mm.n)
            c.op("vector", "tensor_copy", out=ksr[:, nt, :], in_=ps[:, 1, :])
        pv = c.last_op.get("vector")
        c.wait("tensor", pv[0], pv[1])

        gemm_pass(Wq, files["qf"], "q")
        gemm_pass(Wv, files["vf"], "v")
        qv_done = stf.n

    G = 256
    NG = T // G
    with ExitStack() as ph:
        sbp = lambda name, shape, dt: ph.enter_context(nc.sbuf_tensor(t_("sm_" + name), shape, dt))
        qrb = sbp("qrb", [64, 2, 64, G], BF16)
        vrb = sbp("vrb", [64, 2, 65, G], BF16)
        og = sbp("og", [128, 2, 64, 128], BF16)
        rc = sbp("rc", [128, 2, 16], F32)
        ps_a = ph.enter_context(nc.psum_tensor(t_("ps_a"), [128, 2, 16, 128], F32))
        sld = c.sem(s_("sld"))
        st2 = c.sem(s_("st2"))

        c.op("vector", "memset", ap=vrb[:, 0, 64, :], constant=1.0)
        c.op("vector", "memset", ap=vrb[:, 1, 64, :], constant=1.0)
        ones_set = c.last_op["vector"]

        qview = files["qf"].rearrange("(i d) t -> d i t", d=64)
        vview = files["vf"].rearrange("(j d) t -> d j t", d=64)
        c.wait("scalar", stf, qv_done)
        rb_vals = {}

        def load_group(g, wait_mm):
            b = g % 2
            if wait_mm is not None:
                c.wait("scalar", mm, wait_mm)
            if wait_mm is not None:
                c.wait("sync", mm, wait_mm)
            c.dma("scalar", qrb[:, b], qview[:, :, G * g:G * (g + 1)], sld)
            rb_vals[g] = (sld.n, c.dma("sync", vrb[:, b, 0:64, :],
                                       vview[:, :, G * g:G * (g + 1)], ld))

        load_group(0, None)
        load_group(1, None)
        ev_done_flight = {}
        og_store = {}
        NF = 8
        for g in range(NG):
            b = g % 2
            c.wait("tensor", sld, rb_vals[g][0])
            c.wait("tensor", ld, rb_vals[g][1])
            for f in range(NF):
                pb = f % 2
                key = (g, f - 2) if f >= 2 else (g - 1, f + NF - 2)
                if key in ev_done_flight:
                    c.wait("tensor", ev_done_flight[key][0], ev_done_flight[key][1])
                for u in range(32):
                    tloc = 32 * f + u
                    h = u & 1
                    s = u >> 1
                    tp = (0, 64) if h else None
                    c.op("tensor", "matmul", sem=(mm if u == 31 else None),
                         out=ps_a[64 * h:64 * (h + 1), pb, s, 0:65],
                         lhsT=qrb[:, b, :, tloc], rhs=vrb[:, b, :, tloc],
                         start=True, stop=True, skip_group_check=True,
                         tile_position=tp)
                mmv = mm.n
                c.wait("vector", mm, mmv)
                c.op("vector", "tensor_scalar", out=rc[:, pb, :],
                     in0=ps_a[:, pb, :, 64:65], scalar1=1e-6, scalar2=None, op0=ALU.add)
                c.op("vector", "reciprocal", out=rc[:, pb, :], in_=rc[:, pb, :])
                if f == 0 and g >= 2:
                    c.wait("vector", stf, og_store[g - 2][0])
                    c.wait("vector", st2, og_store[g - 2][1])
                c.op("vector", "tensor_tensor",
                     out=og[:, b, :, 16 * f:16 * (f + 1)].transpose([0, 2, 1]),
                     in0=ps_a[:, pb, :, 0:64],
                     in1=rc[:, pb, :].unsqueeze(2).broadcast_to([128, 16, 64]),
                     op=ALU.mult)
                ev_done_flight[(g, f)] = c.last_op["vector"]
            if g + 2 < NG:
                load_group(g + 2, mm.n)
            pv = c.last_op["vector"]
            c.wait("sync", pv[0], pv[1])
            c.wait("scalar", pv[0], pv[1])
            v_e = c.dma("sync", files["oe"].rearrange("(i j) u -> i j u", j=64)[:, :, 128 * g:128 * (g + 1)],
                        og[0:64, b], stf)
            v_o = c.dma("scalar", files["oo"].rearrange("(i j) u -> i j u", j=64)[:, :, 128 * g:128 * (g + 1)],
                        og[64:128, b], st2)
            og_store[g] = (v_e, v_o)
        c.wait("sync", st2, st2.n)
        o_done = stf.n

    with ExitStack() as ph:
        sbp = lambda name, shape, dt: ph.enter_context(nc.sbuf_tensor(t_("wo_" + name), shape, dt))
        wo_sb = sbp("wo", [128, 32, D], BF16)
        orb = sbp("orb", [128, 2, 512], BF16)
        rev = sbp("rev", [128, 8, 512], BF16)
        bo_sb = sbp("bo", [128, 8], F32)
        ps_r = ph.enter_context(nc.psum_tensor(t_("ps_r"), [128, 8, 512], F32))

        st2 = c.sem(s_("st2"))
        c.wait("sync", stf, o_done)
        c.wait("gpsimd", stf, o_done)
        c.wait("gpsimd", st2, st2.n)
        c.dma("sync", bo_sb[:], bo[:], ld)
        for ct in range(32):
            c.dma("gpsimd", wo_sb[:, ct, :], Wo[128 * ct:128 * (ct + 1), :], gld)
        wo_ok = gld.n
        c.wait("tensor", gld, wo_ok)
        c.wait("tensor", ld, ld.n)

        xt = sbp("ra_x", [128, 2, D], F32)
        rt = sbp("ra_r", [128, 2, D], BF16)
        ot = sbp("ra_o", [128, 2, D], F32)
        rao = c.sem(s_("rao"))
        rlx = c.sem(s_("rlx"))
        xv = x_in.rearrange("(u two) d -> two u d", two=2)
        ov = x_out.rearrange("(u two) d -> two u d", two=2)

        rfiles = {0: files["re"], 1: files["ro"]}
        ofiles = {0: files["oe"], 1: files["oo"]}
        first = True
        add_m = {}
        rao_m = {}
        for par in range(2):
            for ut in range(4):
                base_mm = mm.n
                for ct in range(32):
                    bb = ct % 2
                    if ct >= 2:
                        c.wait("sync", mm, base_mm + ct - 1)
                    if first and ct < 2:
                        c.wait("sync", stf, o_done)
                        c.wait("sync", st2, st2.n)
                    c.dma("sync", orb[:, bb, :],
                          ofiles[par][128 * ct:128 * (ct + 1), 512 * ut:512 * (ut + 1)], ld)
                first = False
                lv = ld.n
                for ct in range(32):
                    bb = ct % 2
                    c.wait("tensor", ld, lv - 16 * (31 - ct))
                    if ct == 0 and (par + ut > 0):
                        prev = ev_m
                        c.wait("tensor", prev[0], prev[1])
                    for cc in range(8):
                        c.op("tensor", "matmul", sem=(mm if cc == 7 else None),
                             out=ps_r[:, cc, :], lhsT=wo_sb[:, ct, 128 * cc:128 * (cc + 1)],
                             rhs=orb[:, bb, :],
                             start=(ct == 0), stop=(ct == 31), skip_group_check=True)
                c.wait("vector", mm, mm.n)
                if par + ut > 0:
                    c.wait("vector", stf, stf.n)
                for cc in range(8):
                    c.op("vector", "tensor_scalar", out=rev[:, cc, :], in0=ps_r[:, cc, :],
                         scalar1=bo_sb[:, cc:cc + 1], scalar2=None, op0=ALU.add)
                ev_m = c.last_op["vector"]
                c.wait("sync", ev_m[0], ev_m[1])
                st_r = c.dma("sync", rfiles[par][:, 512 * ut:512 * (ut + 1)]
                             .rearrange("(cc p) t -> p cc t", p=128), rev[:], stf)
                for pt in range(4 * ut, 4 * ut + 4):
                    idx = par * 16 + pt
                    b = idx % 2
                    if idx >= 2:
                        c.wait("scalar", *add_m[idx - 2])
                    c.dma("scalar", xt[:, b, :], xv[par, 128 * pt:128 * (pt + 1), :], rlx)
                    c.wait("sync", stf, st_r)
                    c.dma_tp("sync", rt[:, b, :], rfiles[par][:, 128 * pt:128 * (pt + 1)], ld)
                    c.wait("vector", ld, ld.n)
                    c.wait("vector", rlx, rlx.n)
                    if idx >= 2:
                        c.wait("vector", rao, rao_m[idx - 2])
                    c.op("vector", "tensor_tensor", out=ot[:, b, :], in0=xt[:, b, :],
                         in1=rt[:, b, :], op=ALU.add)
                    add_m[idx] = c.last_op["vector"]
                    c.wait("sync", *add_m[idx])
                    rao_m[idx] = c.dma("sync", ov[par, 128 * pt:128 * (pt + 1), :],
                                       ot[:, b, :], rao)
        c.wait("sync", rao, rao.n)
    return rao.n



def attn1_host_consts(extra=True):
    return {"onesrow": np.ones((1, 128), np.float32)}

ATTN1_CONSTS = {"onesrow": ([1, 128], F32)}


def emit_attn1(c, x_in, x_out, Wt, Wo, Wg, par, con, xln_tm, files, tag="a1", stag=None):
    if stag is None:
        stag = tag
    nc = c.nc
    st_ = c.stack
    t_ = lambda n: tag + n
    s_ = lambda n: stag + n
    sb = lambda name, shape, dt: st_.enter_context(nc.sbuf_tensor(t_(name), shape, dt))

    mm = c.sem(s_("mm"))
    ld = c.sem(s_("ld"))
    gld = c.sem(s_("gld"))
    stf = c.sem(s_("stf"))

    pt = {}
    pl = c.sem(s_("pl"))
    for nm in ("dec", "beta", "lncw", "lncb", "lngw", "lngb", "bg"):
        pt[nm] = sb(nm, [128, 8], F32)
        c.dma("sync", pt[nm][:], par[nm][:], pl)
    for nm in ("g1w", "g1b"):
        pt[nm] = sb(nm, [128, D], F32)
        c.dma("sync", pt[nm][:], par[nm][:], pl)
    pl_done = pl.n
    for e in ("vector", "scalar", "gpsimd", "tensor"):
        c.wait(e, pl, pl_done)

    with ExitStack() as ph:
        sbp = lambda name, shape, dt: ph.enter_context(nc.sbuf_tensor(t_("ln_" + name), shape, dt))
        bufs = {
            "ln_x": sbp("x", [128, 2, 8, D], F32),
            "ln_x2": sbp("x2", [128, 8, D], F32),
            "ln_sm": sbp("sm", [128, 2, 8, 8], F32),
            "ln_out": sbp("o", [128, 2, 8, D], BF16),
        }
        emit_ln(c, bufs, x_in, xln_tm, eps_ap=con["eps6"][:, 0:1],
                gamma=pt["g1w"], beta=pt["g1b"], tag=s_("L"))
    ln_st = c.sems[s_("L") + "st"]
    ln_done = ln_st.n
    for _e in ("gpsimd", "vector", "scalar"):
        c.wait(_e, ln_st, ln_done)

    xlnT1 = sb("xlnT1", [128, 8, T], BF16)
    c.wait("sync", ln_st, ln_done)
    xb_vals = []
    for ci in range(8):
        v = c.dma_tp("sync", xlnT1[:, ci, :], xln_tm[:, 128 * ci:128 * (ci + 1)], ld)
        xb_vals.append(v)

    with ExitStack() as ph:
        sbp = lambda name, shape, dt: ph.enter_context(nc.sbuf_tensor(t_("u_" + name), shape, dt))
        wt_sb = sbp("wt", [128, 2, 8, 128], BF16)
        ud = sbp("ud", [128, T], BF16)
        cxd = sbp("cxd", [128, 2, T], BF16)
        ps = ph.enter_context(nc.psum_tensor(t_("ps_u"), [128, 8, 512], F32))

        base_mm = mm.n
        cx_store = {}
        for mt in range(8):
            b = mt % 2
            if mt >= 2:
                c.wait("gpsimd", mm, base_mm + 8 * (mt - 1))
            c.dma("gpsimd", wt_sb[:, b],
                  Wt[:, 128 * mt:128 * (mt + 1)].rearrange("(c p) m -> p c m", p=128), gld)
            c.wait("tensor", gld, gld.n)
            if mt == 0:
                for v in xb_vals:
                    c.wait("tensor", ld, v)
            if mt >= 1:
                pv = c.last_op["vector"]
                c.wait("tensor", pv[0], pv[1])
            for ci in range(8):
                for n in range(8):
                    c.op("tensor", "matmul", sem=(mm if n == 7 else None),
                         out=ps[:, n, :], lhsT=wt_sb[:, b, ci, :],
                         rhs=xlnT1[:, ci, 512 * n:512 * (n + 1)],
                         start=(ci == 0), stop=(ci == 7), skip_group_check=True)
            c.wait("vector", mm, base_mm + 8 * (mt + 1))
            for n in range(8):
                c.op("vector", "tensor_scalar", out=ud[:, 512 * n:512 * (n + 1)],
                     in0=ps[:, n, :], scalar1=pt["beta"][:, mt:mt + 1], scalar2=None,
                     op0=ALU.mult)
            if mt >= 2:
                c.wait("vector", stf, cx_store[mt - 2])
            c.op("vector", "tensor_tensor_scan", out=cxd[:, b, :],
                 data0=pt["dec"][:, mt:mt + 1].broadcast_to([128, T]), data1=ud[:],
                 initial=0.0, op0=ALU.mult, op1=ALU.add)
            pv = c.last_op["vector"]
            c.wait("sync", pv[0], pv[1])
            cx_store[mt] = c.dma("sync", files["cxf"][128 * mt:128 * (mt + 1), :], cxd[:, b, :], stf)
        cx_done = stf.n

    with ExitStack() as ph:
        sbp = lambda name, shape, dt: ph.enter_context(nc.sbuf_tensor(t_("cs_" + name), shape, dt))
        wo_sb = sbp("wo", [128, 8, D], BF16)
        cxg = sbp("cxg", [128, 8, 512], BF16)
        cx2 = sbp("cx2", [128, 8, 512], BF16)
        rrows = sbp("rr", [1, 4, 512], F32)
        mcb = sbp("mcb", [128, 512], F32)
        rcb = sbp("rcb", [128, 512], F32)
        lncg = sbp("lncg", [128, 8, 512], BF16)
        oev = sbp("oev", [128, 8, 512], BF16)
        ps = ph.enter_context(nc.psum_tensor(t_("ps_c"), [128, 8, 512], F32))

        c.wait("gpsimd", stf, cx_done)
        for ci in range(8):
            c.dma("gpsimd", wo_sb[:, ci, :], Wo[128 * ci:128 * (ci + 1), :], gld)
        wo_ok = gld.n
        cview = files["cxf"].rearrange("(dt p) t -> p dt t", p=128)
        out_store = {}
        for nt in range(8):
            if nt == 0:
                c.wait("sync", stf, cx_done)
            else:
                c.wait("sync", mm, mm.n)
            c.dma("sync", cxg[:], cview[:, :, 512 * nt:512 * (nt + 1)], ld)
            lv = ld.n
            c.wait("scalar", ld, lv)
            c.op("scalar", "activation", out=cx2[:].rearrange("p a t -> p (a t)"),
                 in_=cxg[:].rearrange("p a t -> p (a t)"), func=AF.Square)
            pv = c.last_op["scalar"]
            c.wait("tensor", ld, lv)
            c.wait("tensor", pv[0], pv[1])
            for dt_ in range(8):
                c.op("tensor", "matmul", out=ps[0:1, 0, :], lhsT=con["ones1"][:],
                     rhs=cxg[:, dt_, :], start=(dt_ == 0), stop=(dt_ == 7),
                     skip_group_check=True)
                c.op("tensor", "matmul", sem=(mm if dt_ == 7 else None),
                     out=ps[0:1, 1, :], lhsT=con["ones1"][:],
                     rhs=cx2[:, dt_, :], start=(dt_ == 0), stop=(dt_ == 7),
                     skip_group_check=True)
            c.wait("vector", mm, mm.n)
            c.op("vector", "tensor_scalar", out=rrows[:, 0, :], in0=ps[0:1, 0, :],
                 scalar1=1.0 / D, scalar2=None, op0=ALU.mult)
            c.op("vector", "tensor_scalar", out=rrows[:, 1, :], in0=ps[0:1, 1, :],
                 scalar1=1.0 / D, scalar2=None, op0=ALU.mult)
            c.op("vector", "tensor_tensor", out=rrows[:, 2, :], in0=rrows[:, 0, :],
                 in1=rrows[:, 0, :], op=ALU.mult)
            c.op("vector", "tensor_tensor", out=rrows[:, 2, :], in0=rrows[:, 1, :],
                 in1=rrows[:, 2, :], op=ALU.subtract)
            pv = c.last_op["vector"]
            c.wait("scalar", pv[0], pv[1])
            c.op("scalar", "activation", out=rrows[:, 3, :], in_=rrows[:, 2, :],
                 func=AF.Sqrt, bias=con["eps6"][0:1, 0:1])
            pv = c.last_op["scalar"]
            c.wait("vector", pv[0], pv[1])
            c.op("vector", "reciprocal", out=rrows[:, 3, :], in_=rrows[:, 3, :])
            pv = c.last_op["vector"]
            c.wait("tensor", pv[0], pv[1])
            c.op("tensor", "matmul", out=ps[:, 2, :], lhsT=con["onesrow"][:],
                 rhs=rrows[:, 0, :], start=True, stop=True, skip_group_check=True)
            c.op("tensor", "matmul", sem=mm, out=ps[:, 3, :], lhsT=con["onesrow"][:],
                 rhs=rrows[:, 3, :], start=True, stop=True, skip_group_check=True)
            c.wait("vector", mm, mm.n)
            c.op("vector", "tensor_copy", out=mcb[:], in_=ps[:, 2, :])
            c.op("vector", "tensor_copy", out=rcb[:], in_=ps[:, 3, :])
            c.op("vector", "tensor_tensor", out=lncg[:], in0=cxg[:],
                 in1=mcb[:].unsqueeze(1).broadcast_to([128, 8, 512]), op=ALU.subtract)
            c.op("vector", "tensor_tensor", out=lncg[:], in0=lncg[:],
                 in1=rcb[:].unsqueeze(1).broadcast_to([128, 8, 512]), op=ALU.mult)
            for dt_ in range(8):
                c.op("vector", "tensor_scalar", out=lncg[:, dt_, :],
                     in0=lncg[:, dt_, :], scalar1=pt["lncw"][:, dt_:dt_ + 1],
                     scalar2=pt["lncb"][:, dt_:dt_ + 1], op0=ALU.mult, op1=ALU.add)
            pv = c.last_op["vector"]
            c.wait("tensor", pv[0], pv[1])
            if nt == 0:
                c.wait("tensor", gld, wo_ok)
            if nt >= 1:
                c.wait("tensor", stf, out_store[nt - 1])
            for cc in range(8):
                for ci in range(8):
                    c.op("tensor", "matmul",
                         sem=(mm if ci == 7 else None),
                         out=ps[:, cc, :], lhsT=wo_sb[:, ci, 128 * cc:128 * (cc + 1)],
                         rhs=lncg[:, ci, :],
                         start=(ci == 0), stop=(ci == 7), skip_group_check=True)
            c.wait("vector", mm, mm.n)
            for cc in range(8):
                c.op("vector", "tensor_copy", out=oev[:, cc, :], in_=ps[:, cc, :])
            pv = c.last_op["vector"]
            c.wait("sync", pv[0], pv[1])
            out_store[nt] = c.dma("sync",
                                  files["outf"][:, 512 * nt:512 * (nt + 1)]
                                  .rearrange("(cc p) t -> p cc t", p=128), oev[:], stf)
        out_done = stf.n

    with ExitStack() as ph:
        sbp = lambda name, shape, dt: ph.enter_context(nc.sbuf_tensor(t_("g_" + name), shape, dt))
        wg_sb = sbp("wg", [128, 16, D], BF16)
        gpt = sbp("gpt", [128, 8, 512], BF16)
        gp2 = sbp("gp2", [128, 8, 512], BF16)
        grow = sbp("grow", [1, 4, 512], F32)
        mgb = sbp("mgb", [128, 512], F32)
        rgb = sbp("rgb", [128, 512], F32)
        gt = sbp("gt", [128, 8, 512], BF16)
        ofr = sbp("ofr", [128, 8, 512], BF16)
        mt_ = sbp("mt", [128, 8, 512], BF16)
        ra_x = sbp("ra_x", [128, 2, D], F32)
        ra_r = sbp("ra_r", [128, 2, D], BF16)
        ra_o = sbp("ra_o", [128, 2, D], F32)
        rao = c.sem(s_("rao"))
        rlx = c.sem(s_("rlx"))
        add_m = {}
        rao_m = {}
        ps = ph.enter_context(nc.psum_tensor(t_("ps_g"), [128, 8, 512], F32))

        c.wait("gpsimd", stf, out_done)
        for ci in range(16):
            c.dma("gpsimd", wg_sb[:, ci, :], Wg[128 * ci:128 * (ci + 1), :], gld)
        wg_ok = gld.n
        oview = files["outf"].rearrange("(cc p) t -> p cc t", p=128)
        mt_store = {}
        for nt in range(8):
            if nt == 0:
                c.wait("sync", stf, out_done)
            else:
                c.wait("sync", mm, mm.n)
            c.dma("sync", ofr[:], oview[:, :, 512 * nt:512 * (nt + 1)], ld)
            ol = ld.n
            if nt == 0:
                c.wait("tensor", gld, wg_ok)
            if nt >= 1:
                pv = c.last_op["vector"]
                c.wait("tensor", pv[0], pv[1])
            c.wait("tensor", ld, ol)
            for cc in range(8):
                for ci in range(16):
                    rhs = xlnT1[:, ci, 512 * nt:512 * (nt + 1)] if ci < 8 else \
                        ofr[:, ci - 8, :]
                    c.op("tensor", "matmul", sem=(mm if ci == 15 else None),
                         out=ps[:, cc, :], lhsT=wg_sb[:, ci, 128 * cc:128 * (cc + 1)],
                         rhs=rhs, start=(ci == 0), stop=(ci == 15), skip_group_check=True)
            c.wait("vector", mm, mm.n)
            for cc in range(8):
                c.op("vector", "tensor_scalar", out=gpt[:, cc, :], in0=ps[:, cc, :],
                     scalar1=pt["bg"][:, cc:cc + 1], scalar2=None, op0=ALU.add)
            pv = c.last_op["vector"]
            c.wait("scalar", pv[0], pv[1])
            c.op("scalar", "activation", out=gp2[:].rearrange("p a t -> p (a t)"),
                 in_=gpt[:].rearrange("p a t -> p (a t)"), func=AF.Square)
            pv2 = c.last_op["scalar"]
            c.wait("tensor", pv[0], pv[1])
            c.wait("tensor", pv2[0], pv2[1])
            for cc in range(8):
                c.op("tensor", "matmul", out=ps[0:1, 0, :], lhsT=con["ones1"][:],
                     rhs=gpt[:, cc, :], start=(cc == 0), stop=(cc == 7),
                     skip_group_check=True)
                c.op("tensor", "matmul", sem=(mm if cc == 7 else None),
                     out=ps[0:1, 1, :], lhsT=con["ones1"][:], rhs=gp2[:, cc, :],
                     start=(cc == 0), stop=(cc == 7), skip_group_check=True)
            c.wait("vector", mm, mm.n)
            c.op("vector", "tensor_scalar", out=grow[:, 0, :], in0=ps[0:1, 0, :],
                 scalar1=1.0 / D, scalar2=None, op0=ALU.mult)
            c.op("vector", "tensor_scalar", out=grow[:, 1, :], in0=ps[0:1, 1, :],
                 scalar1=1.0 / D, scalar2=None, op0=ALU.mult)
            c.op("vector", "tensor_tensor", out=grow[:, 2, :], in0=grow[:, 0, :],
                 in1=grow[:, 0, :], op=ALU.mult)
            c.op("vector", "tensor_tensor", out=grow[:, 2, :], in0=grow[:, 1, :],
                 in1=grow[:, 2, :], op=ALU.subtract)
            pv = c.last_op["vector"]
            c.wait("scalar", pv[0], pv[1])
            c.op("scalar", "activation", out=grow[:, 3, :], in_=grow[:, 2, :],
                 func=AF.Sqrt, bias=con["eps6"][0:1, 0:1])
            pv = c.last_op["scalar"]
            c.wait("vector", pv[0], pv[1])
            c.op("vector", "reciprocal", out=grow[:, 3, :], in_=grow[:, 3, :])
            pv = c.last_op["vector"]
            c.wait("tensor", pv[0], pv[1])
            c.op("tensor", "matmul", out=ps[:, 2, :], lhsT=con["onesrow"][:],
                 rhs=grow[:, 0, :], start=True, stop=True, skip_group_check=True)
            c.op("tensor", "matmul", sem=mm, out=ps[:, 3, :], lhsT=con["onesrow"][:],
                 rhs=grow[:, 3, :], start=True, stop=True, skip_group_check=True)
            c.wait("vector", mm, mm.n)
            c.op("vector", "tensor_copy", out=mgb[:], in_=ps[:, 2, :])
            c.op("vector", "tensor_copy", out=rgb[:], in_=ps[:, 3, :])
            for cc in range(8):
                c.op("vector", "tensor_tensor", out=gt[:, cc, :], in0=gpt[:, cc, :],
                     in1=mgb[:], op=ALU.subtract)
                c.op("vector", "tensor_tensor", out=gt[:, cc, :], in0=gt[:, cc, :],
                     in1=rgb[:], op=ALU.mult)
                c.op("vector", "tensor_scalar", out=gt[:, cc, :], in0=gt[:, cc, :],
                     scalar1=pt["lngw"][:, cc:cc + 1], scalar2=pt["lngb"][:, cc:cc + 1],
                     op0=ALU.mult, op1=ALU.add)
            pv = c.last_op["vector"]
            c.wait("scalar", pv[0], pv[1])
            c.op("scalar", "activation", out=gt[:].rearrange("p a t -> p (a t)"),
                 in_=gt[:].rearrange("p a t -> p (a t)"), func=AF.Sigmoid)
            pv = c.last_op["scalar"]
            c.wait("vector", pv[0], pv[1])
            if nt >= 2:
                c.wait("vector", stf, mt_store[nt - 2])
            for cc in range(8):
                c.op("vector", "tensor_tensor", out=mt_[:, cc, :], in0=ofr[:, cc, :],
                     in1=xlnT1[:, cc, 512 * nt:512 * (nt + 1)], op=ALU.subtract)
                c.op("vector", "tensor_tensor", out=mt_[:, cc, :], in0=gt[:, cc, :],
                     in1=mt_[:, cc, :], op=ALU.mult)
                c.op("vector", "tensor_tensor", out=mt_[:, cc, :],
                     in0=xlnT1[:, cc, 512 * nt:512 * (nt + 1)], in1=mt_[:, cc, :],
                     op=ALU.add)
            pv = c.last_op["vector"]
            c.wait("sync", pv[0], pv[1])
            mt_store[nt] = c.dma("sync", files["mtf"][:, 512 * nt:512 * (nt + 1)]
                                 .rearrange("(cc p) t -> p cc t", p=128), mt_[:], stf)
            for tt in range(4 * nt, 4 * nt + 4):
                b = tt % 2
                if tt >= 2:
                    c.wait("scalar", *add_m[tt - 2])
                c.dma("scalar", ra_x[:, b, :], x_in[128 * tt:128 * (tt + 1), :], rlx)
                c.wait("sync", stf, mt_store[nt])
                c.dma_tp("sync", ra_r[:, b, :], files["mtf"][:, 128 * tt:128 * (tt + 1)], ld)
                c.wait("vector", ld, ld.n)
                c.wait("vector", rlx, rlx.n)
                if tt >= 2:
                    c.wait("vector", rao, rao_m[tt - 2])
                c.op("vector", "tensor_tensor", out=ra_o[:, b, :], in0=ra_x[:, b, :],
                     in1=ra_r[:, b, :], op=ALU.add)
                add_m[tt] = c.last_op["vector"]
                c.wait("sync", *add_m[tt])
                rao_m[tt] = c.dma("sync", x_out[128 * tt:128 * (tt + 1), :],
                                  ra_o[:, b, :], rao)
        c.wait("sync", rao, rao.n)
    return rao.n



def build_fused_prog():
    nc = bass.Bass("TRN2", target_bir_lowering=False, detect_race_conditions=False)
    xi = nc.dram_tensor("x_in", [T, D], F32, kind="ExternalInput").ap()
    xo = nc.dram_tensor("x_out", [T, D], F32, kind="ExternalOutput").ap()
    xA = nc.dram_tensor("xA", [T, D], F32, kind="Internal").ap()
    xB = nc.dram_tensor("xB", [T, D], F32, kind="Internal").ap()

    gwT0 = nc.dram_tensor("gate_wT0", [D, E], F32, kind="ExternalInput").ap()
    w10 = nc.dram_tensor("w1_0", [E, D, H], F32, kind="ExternalInput").ap()
    w20 = nc.dram_tensor("w2_0", [E, H, D], F32, kind="ExternalInput").ap()
    gwT1 = nc.dram_tensor("gate_wT1", [D, E], F32, kind="ExternalInput").ap()
    w11 = nc.dram_tensor("w1_1", [E, D, H], F32, kind="ExternalInput").ap()
    w21 = nc.dram_tensor("w2_1", [E, H, D], F32, kind="ExternalInput").ap()
    Wq = nc.dram_tensor("Wq", [D, 4096], F32, kind="ExternalInput").ap()
    Wk = nc.dram_tensor("Wk", [D, 4096], F32, kind="ExternalInput").ap()
    Wv = nc.dram_tensor("Wv", [D, 4096], F32, kind="ExternalInput").ap()
    Wo0 = nc.dram_tensor("Wo0", [4096, D], F32, kind="ExternalInput").ap()
    bo = nc.dram_tensor("bo", [128, 8], F32, kind="ExternalInput").ap()
    Wt = nc.dram_tensor("Wt", [D, D], F32, kind="ExternalInput").ap()
    Wo1 = nc.dram_tensor("Wo1", [D, D], F32, kind="ExternalInput").ap()
    Wg = nc.dram_tensor("Wg", [2 * D, D], F32, kind="ExternalInput").ap()
    par = {}
    for nm in ("dec", "beta", "lncw", "lncb", "lngw", "lngb", "bg"):
        par[nm] = nc.dram_tensor("p_" + nm, [128, 8], F32, kind="ExternalInput").ap()
    for nm in ("g1w", "g1b"):
        par[nm] = nc.dram_tensor("p_" + nm, [128, D], F32, kind="ExternalInput").ap()

    cst = declare_consts(nc)
    for nm, (shape, dt) in {**ATTN0_CONSTS, **ATTN1_CONSTS}.items():
        cst[nm] = nc.dram_tensor("c_" + nm, shape, dt, kind="ExternalInput").ap()

    xln_tm = nc.dram_tensor("xln_tm", [T, D], BF16, kind="Internal").ap()
    expd = nc.dram_tensor("expd", [E * T], F32, kind="Internal").ap()
    xinf = nc.dram_tensor("xinf", [2, 1024, D], BF16, kind="Internal").ap()
    f0 = {}
    for nm, shape in (("qf", [4096, 4096]), ("kf", [4096, 4096]), ("vf", [4096, 4096]),
                      ("oe", [4096, 2048]), ("oo", [4096, 2048]),
                      ("re", [1024, 2048]), ("ro", [1024, 2048])):
        f0[nm] = nc.dram_tensor(nm, shape, BF16, kind="Internal").ap()
    f1 = {"cxf": nc.dram_tensor("cxf", [D, T], BF16, kind="Internal").ap(),
          "outf": nc.dram_tensor("outf", [D, T], BF16, kind="Internal").ap(),
          "mtf": nc.dram_tensor("mtf", [D, T], BF16, kind="Internal").ap()}

    with ExitStack() as gstack:
        c = Ctx(nc, gstack)
        con = load_consts(c, cst)
        cl = c.sem("cl")
        for nm in {**ATTN0_CONSTS, **ATTN1_CONSTS}:
            con[nm] = gstack.enter_context(nc.sbuf_tensor("cs_" + nm,
                                                          *({**ATTN0_CONSTS, **ATTN1_CONSTS}[nm])))
            c.dma("sync", con[nm][:], cst[nm][:], cl)
        for e in ("tensor", "vector", "scalar", "gpsimd"):
            c.wait(e, cl, cl.n)
        disp = {
            "idx32": gstack.enter_context(nc.sbuf_tensor("g_idx32", [128, E, 8], I32)),
            "g128": gstack.enter_context(nc.sbuf_tensor("g_g128", [128, E, 8], F32)),
            "y_sb": gstack.enter_context(nc.sbuf_tensor("g_y", [128, 2, D], F32)),
            "gtm": gstack.enter_context(nc.sbuf_tensor("g_gtm", [128, 1, D], BF16)),
        }

        def barrier(sem_name, val):
            s = c.sem(sem_name)
            for e in ENGINES:
                c.wait(e, s, val)

        def run_sub(fn):
            ss = ExitStack()
            old = c.stack
            c.stack = ss
            try:
                with ss:
                    fn()
            finally:
                c.stack = old

        run_sub(lambda: emit_ffn(c, xi, xA, gwT0, w10, w20, cst, con, xln_tm, expd,
                                 tag="f1", stag="G", disp=disp, xinf=xinf))
        barrier("Gsca", c.sem("Gsca").n)
        run_sub(lambda: emit_attn0(c, xA, xB, Wq, Wk, Wv, Wo0, bo, con, xln_tm, f0,
                                   tag="a0", stag="G"))
        barrier("Grao", c.sem("Grao").n)
        run_sub(lambda: emit_ffn(c, xB, xA, gwT0, w10, w20, cst, con, xln_tm, expd,
                                 tag="f2", stag="G", disp=disp, xinf=xinf))
        barrier("Gsca", c.sem("Gsca").n)
        run_sub(lambda: emit_ffn(c, xA, xB, gwT1, w11, w21, cst, con, xln_tm, expd,
                                 tag="f3", stag="G", disp=disp, xinf=xinf))
        barrier("Gsca", c.sem("Gsca").n)
        run_sub(lambda: emit_attn1(c, xB, xA, Wt, Wo1, Wg, par, con, xln_tm, f1,
                                   tag="a1", stag="G"))
        barrier("Grao", c.sem("Grao").n)
        run_sub(lambda: emit_ffn(c, xA, xo, gwT1, w11, w21, cst, con, xln_tm, expd,
                                 tag="f4", stag="G", disp=disp, xinf=xinf))
        barrier("Gsca", c.sem("Gsca").n)
        c.finalize()
    mybir.codegen_inst_isa_subclasses(nc)
    return nc


_PROGS = {}


def _fused():
    if "fused" not in _PROGS:
        _PROGS["fused"] = build_fused_prog()
    return _PROGS["fused"]


def kernel(L0_attn_x, L0_attn_Wq, L0_attn_Wk, L0_attn_Wv, L0_attn_Wo, L0_attn_bo,
           L0_ffn_gate_w, L0_ffn_w1, L0_ffn_w2,
           L1_attn_Wt, L1_attn_Wo, L1_attn_Wg, L1_attn_bg,
           L1_attn_ln1_w, L1_attn_ln1_b, L1_attn_lnc_w, L1_attn_lnc_b,
           L1_attn_lng_w, L1_attn_lng_b,
           L1_ffn_gate_w, L1_ffn_w1, L1_ffn_w2):
    from concourse.bass_utils import run_bass_kernel_spmd
    nc = _fused()
    NC = 8
    col = lambda a: np.ascontiguousarray(np.asarray(a, np.float32).reshape(8, 128).T)
    rep = lambda a: np.ascontiguousarray(np.broadcast_to(np.asarray(a, np.float32), (128, D)))
    f32c = lambda a: np.ascontiguousarray(np.asarray(a, np.float32))
    decay = (0.5 ** (1.0 / np.linspace(2.0, 8.0, D))).astype(np.float32)

    shared = {
        "gate_wT0": f32c(L0_ffn_gate_w), "w1_0": f32c(L0_ffn_w1), "w2_0": f32c(L0_ffn_w2),
        "gate_wT1": np.ascontiguousarray(np.asarray(L1_ffn_gate_w, np.float32).T),
        "w1_1": f32c(L1_ffn_w1), "w2_1": f32c(L1_ffn_w2),
        "Wq": f32c(L0_attn_Wq), "Wk": f32c(L0_attn_Wk), "Wv": f32c(L0_attn_Wv),
        "Wo0": f32c(L0_attn_Wo), "bo": col(L0_attn_bo),
        "Wt": f32c(L1_attn_Wt), "Wo1": f32c(L1_attn_Wo), "Wg": f32c(L1_attn_Wg),
        "p_dec": col(decay), "p_beta": col(1.0 - decay),
        "p_lncw": col(L1_attn_lnc_w), "p_lncb": col(L1_attn_lnc_b),
        "p_lngw": col(L1_attn_lng_w), "p_lngb": col(L1_attn_lng_b),
        "p_bg": col(L1_attn_bg),
        "p_g1w": rep(L1_attn_ln1_w), "p_g1b": rep(L1_attn_ln1_b),
    }
    for k, v in host_consts().items():
        shared["c_" + k] = v
    for k, v in attn0_host_consts().items():
        shared["c_" + k] = v
    for k, v in attn1_host_consts().items():
        shared["c_" + k] = v

    x = np.ascontiguousarray(np.asarray(L0_attn_x, np.float32))
    in_maps = []
    for i in range(NC):
        m = dict(shared)
        m["x_in"] = np.ascontiguousarray(x[i])
        in_maps.append(m)
    res = run_bass_kernel_spmd(nc, in_maps, core_ids=list(range(NC)))
    return np.stack([np.asarray(r["x_out"]) for r in res.results], 0)


# version 06
# speedup vs baseline: 1.1796x; 1.1586x over previous
import math
from contextlib import ExitStack

import numpy as np
import ml_dtypes

import concourse.bass as bass
import concourse.mybir as mybir
from concourse import library_config
from concourse.bass_core_rust import IndirectOffsetOnAxis

F32 = mybir.dt.float32
BF16 = mybir.dt.bfloat16
I16 = mybir.dt.int16
I32 = mybir.dt.int32
AF = mybir.ActivationFunctionType
ALU = mybir.AluOpType
bfdt = ml_dtypes.bfloat16

T = 4096
D = 1024
E = 8
CAP = 1024
H = 4096
NTILE = T // 128

ENGINES = ("sync", "scalar", "vector", "tensor", "gpsimd")


class Sem:
    def __init__(self, handle, name):
        self.h = handle
        self.name = name
        self.n = 0

    def add(self, k):
        self.n += k
        return self.n


class Ctx:

    def __init__(self, nc, stack):
        self.nc = nc
        self.stack = stack
        self.ops = {e: [] for e in ENGINES}
        self.sems = {}
        self.last_op = {}

    def sem(self, name):
        if name not in self.sems:
            h = self.stack.enter_context(self.nc.semaphore(name))
            self.sems[name] = Sem(h, name)
        return self.sems[name]

    def emit(self, engine, fn):
        self.ops[engine].append(fn)

    def wait(self, engine, sem, val):
        self.emit(engine, lambda eng, s=sem.h, v=val: eng.wait_ge(s, v))

    def dma(self, engine, out, in_, sem, **kw):
        v = sem.add(16)
        def f(eng, out=out, in_=in_, s=sem.h, kw=kw):
            eng.dma_start(out=out, in_=in_, **kw).then_inc(s, 16)
        self.emit(engine, f)
        return v

    def dma_tp(self, engine, out, in_, sem):
        v = sem.add(16)
        def f(eng, out=out, in_=in_, s=sem.h):
            eng.dma_start_transpose(out=out, in_=in_).then_inc(s, 16)
        self.emit(engine, f)
        return v

    def op(self, engine, meth, sem=None, ninc=1, **kw):
        chain = engine in ("vector", "scalar", "gpsimd")
        if chain:
            prev = self.last_op.get(engine)
            if prev is not None:
                self.wait(engine, prev[0], prev[1])
        if sem is None and chain:
            sem = self.sem("_self_" + engine)
            ninc = 1
        v = sem.add(ninc) if sem is not None else None
        def f(eng, meth=meth, kw=kw, s=sem, v=v, ninc=ninc):
            ins = getattr(eng, meth)(**kw)
            if s is not None:
                ins.then_inc(s.h, ninc)
        self.emit(engine, f)
        if chain:
            self.last_op[engine] = (sem, v)
        return v

    def finalize(self):
        nc = self.nc
        ops = self.ops
        with nc.Block() as blk:
            @blk.sync
            def _(eng):
                for f in ops["sync"]:
                    f(eng)

            @blk.scalar
            def _(eng):
                for f in ops["scalar"]:
                    f(eng)

            @blk.vector
            def _(eng):
                for f in ops["vector"]:
                    f(eng)

            @blk.tensor
            def _(eng):
                for f in ops["tensor"]:
                    f(eng)

            @blk.gpsimd
            def _(eng):
                for f in ops["gpsimd"]:
                    f(eng)



def host_consts():
    c = {}
    c["eye_f32"] = np.eye(128, dtype=np.float32)
    S = np.zeros((128, 128), np.float32)
    for p in range(128):
        for p2 in range(128):
            if p % 16 == p2 % 16:
                S[p, p2] = 1.0
    c["Snorm"] = S
    G8 = np.zeros((128, 8), np.float32)
    for p in range(128):
        G8[p, p // 16] = 1.0
    c["G8"] = G8
    E16 = np.zeros((8, 128), np.float32)
    for p in range(128):
        E16[p // 16, p] = 1.0
    c["E16"] = E16
    L = np.zeros((128, 128), np.float32)
    for p in range(128):
        for p2 in range(128):
            if p // 16 == p2 // 16 and (p2 % 16) < (p % 16):
                L[p2, p] = 1.0
    c["Ltri"] = L
    c["iota16"] = np.broadcast_to(np.arange(16, dtype=np.float32), (128, 16)).astype(bfdt).copy()
    c["iota8"] = np.broadcast_to(np.arange(8, dtype=np.float32), (128, 8)).astype(bfdt).copy()
    thi = np.zeros((128, 2, 128), np.float32)
    tlo = np.zeros((128, 2, 128), np.float32)
    for p in range(128):
        for h in range(2):
            for col in range(128):
                c256 = col % 16
                t = 256 * c256 + 128 * h + p
                thi[p, h, col] = t >> 5
                tlo[p, h, col] = t & 31
    c["thi"] = thi.astype(bfdt)
    c["tlo"] = tlo.astype(bfdt)
    c["ones1"] = np.ones((128, 1), np.float32).astype(bfdt)
    c["eps5"] = np.full((128, 1), 1e-5, np.float32)
    c["eps6"] = np.full((128, 1), 1e-6, np.float32)
    c["ones1_f32"] = np.ones((128, 1), np.float32)
    return c


CONST_SPECS = {
    "eye_f32": ([128, 128], F32),
    "Snorm": ([128, 128], F32),
    "G8": ([128, 8], F32),
    "E16": ([8, 128], F32),
    "Ltri": ([128, 128], F32),
    "iota16": ([128, 16], BF16),
    "iota8": ([128, 8], BF16),
    "thi": ([128, 2, 128], BF16),
    "tlo": ([128, 2, 128], BF16),
    "ones1": ([128, 1], BF16),
    "eps5": ([128, 1], F32),
    "eps6": ([128, 1], F32),
    "ones1_f32": ([128, 1], F32),
}


def declare_consts(nc):
    aps = {}
    for name, (shape, dt) in CONST_SPECS.items():
        aps[name] = nc.dram_tensor("c_" + name, shape, dt, kind="ExternalInput").ap()
    return aps



def emit_ln(c, bufs, x_in, xln_tm, sem_pre=None, gamma=None, beta=None,
            eps_ap=None, stats_out=None, tag="L"):
    nc = c.nc
    ld = c.sem(tag + "ld")
    ve = c.sem(tag + "ve")
    sc = c.sem(tag + "sc")
    st = c.sem(tag + "st")

    GB = 8
    NG = NTILE // GB
    xt = bufs["ln_x"]
    x2 = bufs["ln_x2"]
    sm = bufs["ln_sm"]
    ot = bufs["ln_out"]
    xv = x_in.rearrange("(g p) d -> p g d", p=128)
    ov = xln_tm.rearrange("(g p) d -> p g d", p=128)

    sq_m, sub_m, ap_m, st_m = {}, {}, {}, {}
    for i in range(NG):
        b = i % 2
        if i >= 2:
            c.wait("sync", *sq_m[i - 2])
            c.wait("sync", *sub_m[i - 2])
        c.dma("sync", xt[:, b], xv[:, GB * i:GB * (i + 1), :], ld)
        c.wait("scalar", ld, ld.n)
        if i >= 1:
            c.wait("scalar", *ap_m[i - 1])
        c.op("scalar", "activation", out=x2.rearrange("p g d -> p (g d)"),
             in_=xt[:, b].rearrange("p g d -> p (g d)"), func=AF.Square)
        sq_m[i] = c.last_op["scalar"]
        c.wait("vector", ld, ld.n)
        c.op("vector", "tensor_reduce", out=sm[:, b, 0, :], in_=xt[:, b],
             op=ALU.add, axis=mybir.AxisListType.X)
        c.wait("vector", *sq_m[i])
        c.op("vector", "tensor_reduce", out=sm[:, b, 1, :], in_=x2[:],
             op=ALU.add, axis=mybir.AxisListType.X)
        c.op("vector", "tensor_scalar", out=sm[:, b, 2, :], in0=sm[:, b, 0, :],
             scalar1=1.0 / D, scalar2=None, op0=ALU.mult)
        c.op("vector", "tensor_scalar", out=sm[:, b, 3, :], in0=sm[:, b, 1, :],
             scalar1=1.0 / D, scalar2=None, op0=ALU.mult)
        c.op("vector", "tensor_tensor", out=sm[:, b, 4, :], in0=sm[:, b, 2, :],
             in1=sm[:, b, 2, :], op=ALU.mult)
        c.op("vector", "tensor_tensor", out=sm[:, b, 4, :], in0=sm[:, b, 3, :],
             in1=sm[:, b, 4, :], op=ALU.subtract)
        var_m = c.last_op["vector"]
        c.wait("scalar", *var_m)
        c.op("scalar", "activation", out=sm[:, b, 5, :], in_=sm[:, b, 4, :],
             func=AF.Sqrt, bias=eps_ap)
        sd_m = c.last_op["scalar"]
        c.wait("vector", *sd_m)
        c.op("vector", "reciprocal", out=sm[:, b, 6, :], in_=sm[:, b, 5, :])
        mb = sm[:, b, 2, :].unsqueeze(2).broadcast_to([128, GB, D])
        rb = sm[:, b, 6, :].unsqueeze(2).broadcast_to([128, GB, D])
        c.op("vector", "tensor_tensor", out=x2[:], in0=xt[:, b], in1=mb,
             op=ALU.subtract)
        sub_m[i] = c.last_op["vector"]
        if i >= 2:
            c.wait("vector", st, 16 * st_m[i - 2])
        if gamma is None:
            c.op("vector", "tensor_tensor", out=ot[:, b], in0=x2[:], in1=rb,
                 op=ALU.mult)
        else:
            c.op("vector", "tensor_tensor", out=x2[:], in0=x2[:], in1=rb,
                 op=ALU.mult)
            c.op("vector", "tensor_tensor", out=x2[:], in0=x2[:],
                 in1=gamma[:].unsqueeze(1).broadcast_to([128, GB, D]),
                 op=ALU.mult)
            c.op("vector", "tensor_tensor", out=ot[:, b], in0=x2[:],
                 in1=beta[:].unsqueeze(1).broadcast_to([128, GB, D]),
                 op=ALU.add)
        ap_m[i] = c.last_op["vector"]
        c.wait("sync", *ap_m[i])
        c.dma("sync", ov[:, GB * i:GB * (i + 1), :], ot[:, b], st)
        st_m[i] = st.n // 16
    return st.n



def emit_ffn(c, x_in, x_out, gate_wT, w1, w2, cst, con, xln_tm, expd, tag="", stag=None, disp=None, xinf=None, dbg=None, routing_only=False):
    if stag is None:
        stag = tag
    """con: dict of const SBUF tiles (loaded by caller)."""
    nc = c.nc
    st_ = c.stack
    t_ = lambda n: tag + n
    s_ = lambda n: stag + n
    sb = lambda name, shape, dt: st_.enter_context(nc.sbuf_tensor(t_(name), shape, dt))

    if xinf is None:
        xinf = nc.dram_tensor(t_("xinf"), [2, 1024, D], BF16, kind="Internal").ap()
    if disp is None:
        disp = {
            "idx32": sb("idx32", [128, E, 8], I32),
            "g128": sb("g128", [128, E, 8], F32),
            "y_sb": sb("ex_y", [128, 2, D], F32),
            "gtm": sb("gtm", [128, 1, D], BF16),
        }
    idx32 = disp["idx32"]
    g128 = disp["g128"]
    gw_sb = sb("gw", [128, 8, E], BF16)
    w1sb = sb("w1sb", [128, 8, H], BF16)

    seedS = c.sem(s_("seed"))
    c.dma("sync", x_out[:], x_in[:], seedS)
    seed_done = seedS.n

    mm = c.sem(s_("mm"))
    ve = c.sem(s_("ve"))
    sc = c.sem(s_("sc"))
    po = c.sem(s_("po"))
    ld = c.sem(s_("ld"))
    gat = c.sem(s_("gat"))
    sca = c.sem(s_("sca"))
    w1c = c.sem(s_("w1c"))
    w2c = c.sem(s_("w2c"))
    mmG = c.sem(s_("mmG"))
    mmK = c.sem(s_("mmK"))
    mmY = c.sem(s_("mmY"))
    geS = c.sem(s_("geS"))
    yv = c.sem(s_("yv"))
    gld = c.sem(s_("gld"))
    xga = c.sem(s_("xga"))

    w1c_done = {}
    w2c_done = {}
    mmG_b = c.sem(s_("mmG")).n
    mmK_b = c.sem(s_("mmK")).n
    mmY_b = c.sem(s_("mmY")).n
    geS_b = c.sem(s_("geS")).n
    yv_b = c.sem(s_("yv")).n

    def emit_w1c(e):
        if e > 0:
            c.wait("gpsimd", mmG, mmG_b + 64 * e)
        vals = []
        for k in range(8):
            v = c.dma("gpsimd", w1sb[:, k, :], w1[e, 128 * k:128 * (k + 1), :], w1c)
            vals.append(v)
        w1c_done[e] = vals

    emit_w1c(0)
    c.dma("gpsimd", gw_sb[:], gate_wT.rearrange("(c p) e -> p c e", p=128), gld)
    gw_ok = gld.n

    with ExitStack() as ph:
        sbp = lambda name, shape, dt: ph.enter_context(nc.sbuf_tensor(t_("ln_" + name), shape, dt))
        bufs = {
            "ln_x": sbp("x", [128, 2, 8, D], F32),
            "ln_x2": sbp("x2", [128, 8, D], F32),
            "ln_sm": sbp("sm", [128, 2, 8, 8], F32),
            "ln_out": sbp("o", [128, 2, 8, D], BF16),
        }
        emit_ln(c, bufs, x_in, xln_tm, eps_ap=con["eps5"][:, 0:1], tag=s_("L"))
    ln_st = c.sems[s_("L") + "st"]
    ln_done = ln_st.n
    for _e in ("gpsimd", "vector", "scalar"):
        c.wait(_e, ln_st, ln_done)

    with ExitStack() as ph:
        sbp = lambda name, shape, dt: ph.enter_context(nc.sbuf_tensor(t_("rt_" + name), shape, dt))
        psp = lambda name, shape: ph.enter_context(nc.psum_tensor(t_("rp_" + name), shape, F32))
        xlnT = sbp("xlnT", [128, 2, T], BF16)
        exp_s = sbp("exps", [8, T], F32)
        expc = sbp("expc", [128, 256], F32)
        rnorm = sbp("rnorm", [128, 256], F32)
        probs = sbp("probs", [128, 256], F32)
        junk = sbp("junk", [128, 256], BF16)
        cntp = sbp("cntp", [128, 1], F32)
        bis = sbp("bis", [8, 8], F32)
        thc = sbp("thc", [128, 1], F32)
        offs = sbp("offs", [128, 1], F32)
        maskc = sbp("maskc", [128, 256], F32)
        cum = sbp("cum", [128, 256], F32)
        pos = sbp("pos", [128, 256], F32)
        sel = sbp("sel", [128, 256], F32)
        pos_m = sbp("pos_m", [128, 256], F32)
        posT = sbp("posT", [128, 2, 128], F32)
        probT = sbp("probT", [128, 2, 128], F32)
        q_i16 = sbp("qi", [128, 2, 128], I16)
        qf = sbp("qf", [128, 2, 128], F32)
        q_bf = sbp("qb", [128, 2, 128], BF16)
        r_bf = sbp("rb", [128, 2, 128], BF16)
        j_i16 = sbp("ji", [128, 2, 128], I16)
        jf = sbp("jf", [128, 2, 128], F32)
        j_bf = sbp("jb", [128, 2, 128], BF16)
        a_bf = sbp("ab", [128, 2, 128], BF16)
        rf_t = sbp("rf", [128, 2, 128], F32)
        jr_t = sbp("jr", [128, 2, 128], F32)
        neg_t = sbp("neg", [128, 2, 128], F32)
        phi = sbp("phi", [128, 2, 128], BF16)
        plo = sbp("plo", [128, 2, 128], BF16)
        GW = 32
        Fl16 = sbp("Fl16", [128, 2, GW, 16], BF16)
        Fm8 = sbp("Fm8", [128, 2, GW, 8], BF16)
        Fh8 = sbp("Fh8", [128, 2, GW, 8], BF16)
        Fh8w = sbp("Fh8w", [128, 2, GW, 32], BF16)
        Fl128 = sbp("Fl128", [128, 2, GW, 8, 16], BF16)
        g2s = sbp("g2s", [128, 32], F32)
        idx128f = sbp("idx128f", [128, 8], F32)
        gtmp = sbp("gtmp", [128, 8], F32)
        ps_sc_stack = ExitStack()
        ps_sc = ps_sc_stack.enter_context(nc.psum_tensor(t_("rp_sc"), [8, T], F32))
        xbar_vals = []
        mm_base = mm.n
        gate_mm_marks = [mm_base + ci + 1 for ci in range(8)]
        for ci in range(8):
            if ci >= 2:
                c.wait("sync", mm, gate_mm_marks[ci - 2])
            if ci == 0:
                c.wait("sync", ln_st, ln_done)
            v = c.dma_tp("sync", xlnT[:, ci % 2, :], xln_tm[:, 128 * ci:128 * (ci + 1)], ld)
            xbar_vals.append(v)
        for ci in range(8):
            c.wait("tensor", ld, xbar_vals[ci])
            if ci == 0:
                c.wait("tensor", gld, gw_ok)
            for n in range(8):
                c.op("tensor", "matmul", sem=(mm if n == 7 else None),
                     out=ps_sc[:, 512 * n:512 * (n + 1)],
                     lhsT=gw_sb[:, ci, :], rhs=xlnT[:, ci % 2, 512 * n:512 * (n + 1)],
                     start=(ci == 0), stop=(ci == 7), skip_group_check=True)
            assert mm.n == gate_mm_marks[ci]
        gate_mm_done = mm.n

        c.wait("scalar", mm, gate_mm_done)
        for n in range(8):
            c.op("scalar", "activation", sem=(sc if n == 7 else None),
                 out=exp_s[:, 512 * n:512 * (n + 1)], in_=ps_sc[:, 512 * n:512 * (n + 1)],
                 func=AF.Exp)
        ps_sc_stack.close()
        ps_nrm = psp("nrm", [128, 256])
        ps_c8 = psp("c8", [8, 1])
        ps_thc = psp("thc", [128, 1])
        ps_off = psp("off", [128, 1])
        ps_tp = psp("tp", [128, 4, 128])
        ps_G2 = psp("G2", [128, 32])
        c.wait("sync", sc, sc.n)
        v1 = c.dma("sync", expd[:].rearrange("(e t) -> e t", e=8), exp_s[:], ld)
        c.wait("sync", ld, v1)
        v2 = c.dma("sync", expc[:], expd[:].rearrange("(p t) -> p t", p=128), ld)
        c.wait("tensor", ld, v2)
        c.op("tensor", "matmul", sem=mm, out=ps_nrm[:], lhsT=con["Snorm"][:],
             rhs=expc[:], start=True, stop=True)
        c.wait("vector", mm, mm.n)
        c.op("vector", "reciprocal", out=rnorm[:], in_=ps_nrm[:])
        c.op("vector", "tensor_tensor", sem=ve, out=probs[:], in0=expc[:],
             in1=rnorm[:], op=ALU.mult)

        c.op("vector", "memset", ap=bis[:], constant=0.0)
        c.op("vector", "memset", ap=bis[:, 1:2], constant=1.0)
        c.op("vector", "memset", ap=bis[:, 2:3], constant=0.5)
        c.op("vector", "memset", sem=ve, ap=thc[:], constant=0.5)
        NIT = 16
        for it in range(NIT):
            c.op("vector", "tensor_scalar", out=junk[:], in0=probs[:],
                 scalar1=thc[:], scalar2=None, op0=ALU.is_ge)
            c.op("vector", "reduce_sum", sem=ve, out=cntp[:], in_=junk[:],
                 axis=mybir.AxisListType.X)
            c.wait("tensor", ve, ve.n)
            c.op("tensor", "matmul", sem=mm, out=ps_c8[:], lhsT=con["G8"][:],
                 rhs=cntp[:], start=True, stop=True)
            c.wait("vector", mm, mm.n)
            c.op("vector", "tensor_copy", out=bis[:, 3:4], in_=ps_c8[:])
            c.op("vector", "tensor_scalar", out=bis[:, 4:5], in0=bis[:, 3:4],
                 scalar1=1023.5, scalar2=None, op0=ALU.is_gt)
            c.op("vector", "tensor_tensor", out=bis[:, 5:6], in0=bis[:, 2:3], in1=bis[:, 0:1], op=ALU.subtract)
            c.op("vector", "tensor_tensor", out=bis[:, 5:6], in0=bis[:, 4:5], in1=bis[:, 5:6], op=ALU.mult)
            c.op("vector", "tensor_tensor", out=bis[:, 0:1], in0=bis[:, 0:1], in1=bis[:, 5:6], op=ALU.add)
            c.op("vector", "tensor_tensor", out=bis[:, 6:7], in0=bis[:, 1:2], in1=bis[:, 2:3], op=ALU.subtract)
            c.op("vector", "tensor_tensor", out=bis[:, 6:7], in0=bis[:, 4:5], in1=bis[:, 6:7], op=ALU.mult)
            c.op("vector", "tensor_tensor", out=bis[:, 1:2], in0=bis[:, 2:3], in1=bis[:, 6:7], op=ALU.add)
            c.op("vector", "tensor_tensor", out=bis[:, 7:8], in0=bis[:, 0:1], in1=bis[:, 1:2], op=ALU.add)
            c.op("vector", "tensor_scalar", sem=ve, out=bis[:, 2:3], in0=bis[:, 7:8],
                 scalar1=0.5, scalar2=None, op0=ALU.mult)
            if it < NIT - 1:
                c.wait("tensor", ve, ve.n)
                c.op("tensor", "matmul", sem=mm, out=ps_thc[:], lhsT=con["E16"][:],
                     rhs=bis[:, 2:3], start=True, stop=True)
                c.wait("vector", mm, mm.n)
                c.op("vector", "tensor_copy", sem=ve, out=thc[:], in_=ps_thc[:])
        c.wait("tensor", ve, ve.n)
        c.op("tensor", "matmul", sem=mm, out=ps_thc[:], lhsT=con["E16"][:],
             rhs=bis[:, 0:1], start=True, stop=True)
        c.wait("vector", mm, mm.n)
        c.op("vector", "tensor_copy", out=thc[:], in_=ps_thc[:])

        c.op("vector", "tensor_scalar", out=maskc[:], in0=probs[:],
             scalar1=thc[:], scalar2=None, op0=ALU.is_ge)
        c.op("vector", "tensor_tensor_scan", out=cum[:],
             data0=con["ones1_f32"][:].broadcast_to([128, 256]), data1=maskc[:],
             initial=0.0, op0=ALU.mult, op1=ALU.add)
        c.op("vector", "tensor_tensor", sem=ve, out=pos[:], in0=cum[:], in1=maskc[:], op=ALU.subtract)
        c.wait("tensor", ve, ve.n)
        c.op("tensor", "matmul", sem=mm, out=ps_off[:], lhsT=con["Ltri"][:],
             rhs=cum[:, 255:256], start=True, stop=True)
        c.wait("vector", mm, mm.n)
        c.op("vector", "tensor_copy", out=offs[:], in_=ps_off[:])
        c.op("vector", "tensor_scalar", out=pos[:], in0=pos[:], scalar1=offs[:],
             scalar2=None, op0=ALU.add)
        c.op("vector", "tensor_scalar", out=sel[:], in0=pos[:], scalar1=1023.5,
             scalar2=None, op0=ALU.is_lt)
        c.op("vector", "tensor_tensor", out=sel[:], in0=maskc[:], in1=sel[:], op=ALU.mult)
        c.op("vector", "tensor_scalar", out=pos_m[:], in0=pos[:], scalar1=4096.0,
             scalar2=None, op0=ALU.subtract)
        c.op("vector", "tensor_tensor", out=pos_m[:], in0=pos_m[:], in1=sel[:], op=ALU.mult)
        c.op("vector", "tensor_scalar", sem=ve, out=pos_m[:], in0=pos_m[:],
             scalar1=4096.0, scalar2=None, op0=ALU.add)

        c.wait("tensor", ve, ve.n)
        for h in range(2):
            c.op("tensor", "transpose", out=ps_tp[:, h, :], in_=pos_m[:, 128 * h:128 * (h + 1)],
                 identity=con["eye_f32"][:])
            c.op("tensor", "transpose", sem=(mm if h == 1 else None),
                 out=ps_tp[:, 2 + h, :], in_=probs[:, 128 * h:128 * (h + 1)],
                 identity=con["eye_f32"][:])
        c.wait("vector", mm, mm.n)
        for h in range(2):
            c.op("vector", "tensor_copy", out=posT[:, h, :], in_=ps_tp[:, h, :])
            c.op("vector", "tensor_copy", sem=(ve if h == 1 else None),
                 out=probT[:, h, :], in_=ps_tp[:, 2 + h, :])

        rf = qf
        c.op("vector", "tensor_scalar", out=q_i16[:], in0=posT[:],
             scalar1=1.0 / 16, scalar2=None, op0=ALU.mult)
        c.op("vector", "tensor_copy", out=qf[:], in_=q_i16[:])
        c.op("vector", "scalar_tensor_tensor", out=rf_t[:], in0=qf[:], scalar=-16.0,
             in1=posT[:], op0=ALU.mult, op1=ALU.add)
        c.op("vector", "tensor_scalar", out=neg_t[:], in0=rf_t[:],
             scalar1=0.0, scalar2=None, op0=ALU.is_lt)
        c.op("vector", "tensor_tensor", out=qf[:], in0=qf[:], in1=neg_t[:], op=ALU.subtract)
        c.op("vector", "scalar_tensor_tensor", out=rf_t[:], in0=neg_t[:], scalar=16.0,
             in1=rf_t[:], op0=ALU.mult, op1=ALU.add)
        c.op("vector", "tensor_copy", out=q_bf[:], in_=qf[:])
        c.op("vector", "tensor_copy", out=r_bf[:], in_=rf_t[:])
        c.op("vector", "tensor_scalar", out=j_i16[:], in0=posT[:],
             scalar1=1.0 / 128, scalar2=None, op0=ALU.mult)
        c.op("vector", "tensor_copy", out=jf[:], in_=j_i16[:])
        c.op("vector", "scalar_tensor_tensor", out=jr_t[:], in0=jf[:], scalar=-128.0,
             in1=posT[:], op0=ALU.mult, op1=ALU.add)
        c.op("vector", "tensor_scalar", out=neg_t[:], in0=jr_t[:],
             scalar1=0.0, scalar2=None, op0=ALU.is_lt)
        c.op("vector", "tensor_tensor", out=jf[:], in0=jf[:], in1=neg_t[:], op=ALU.subtract)
        c.op("vector", "tensor_copy", out=j_bf[:], in_=jf[:])
        c.op("vector", "scalar_tensor_tensor", out=a_bf[:], in0=jf[:], scalar=-8.0,
             in1=qf[:], op0=ALU.mult, op1=ALU.add)
        c.op("vector", "tensor_copy", out=phi[:], in_=probT[:])
        c.op("vector", "tensor_tensor", sem=ve, out=plo[:], in0=probT[:], in1=phi[:],
             op=ALU.subtract)
        cols_ve = ve.n

        def iotab(nm, w):
            return con[nm][:].unsqueeze(1).unsqueeze(2).broadcast_to([128, 2, GW, w])

        def colb(ap, g, w):
            return ap[:, :, GW * g:GW * (g + 1)].unsqueeze(3).broadcast_to([128, 2, GW, w])

        for g in range(4):
            if g > 0:
                c.wait("vector", mm, mm.n)
            c.op("vector", "tensor_tensor", out=Fl16[:],
                 in0=iotab("iota16", 16), in1=colb(r_bf[:], g, 16), op=ALU.is_equal)
            c.op("vector", "tensor_tensor", out=Fm8[:],
                 in0=iotab("iota8", 8), in1=colb(a_bf[:], g, 8), op=ALU.is_equal)
            c.op("vector", "tensor_tensor", out=Fh8[:],
                 in0=iotab("iota8", 8), in1=colb(j_bf[:], g, 8), op=ALU.is_equal)
            c.op("vector", "tensor_tensor", out=Fh8w[:, :, :, 0:8], in0=Fh8[:],
                 in1=colb(phi[:], g, 8), op=ALU.mult)
            c.op("vector", "tensor_tensor", out=Fh8w[:, :, :, 8:16], in0=Fh8[:],
                 in1=colb(plo[:], g, 8), op=ALU.mult)
            c.op("vector", "tensor_tensor", out=Fh8w[:, :, :, 16:24], in0=Fh8[:],
                 in1=con["thi"][:, :, GW * g:GW * (g + 1)].unsqueeze(3).broadcast_to([128, 2, GW, 8]),
                 op=ALU.mult)
            c.op("vector", "tensor_tensor", sem=ve, out=Fh8w[:, :, :, 24:32], in0=Fh8[:],
                 in1=con["tlo"][:, :, GW * g:GW * (g + 1)].unsqueeze(3).broadcast_to([128, 2, GW, 8]),
                 op=ALU.mult)
            c.op("vector", "tensor_tensor", sem=ve, out=Fl128[:],
                 in0=Fl16[:].unsqueeze(3).broadcast_to([128, 2, GW, 8, 16]),
                 in1=Fm8[:].unsqueeze(4).broadcast_to([128, 2, GW, 8, 16]), op=ALU.mult)
            c.wait("tensor", ve, ve.n)
            for el in range(2):
                e = 2 * g + el
                for c128 in range(32):
                    h = c128 & 1
                    col = 16 * el + (c128 >> 1)
                    c.op("tensor", "matmul", sem=(mm if c128 == 31 else None),
                         out=ps_G2[:], lhsT=Fl128[:, h, col, :, :].rearrange("p a l -> p (a l)"),
                         rhs=Fh8w[:, h, col, :],
                         start=(c128 == 0), stop=(c128 == 31), skip_group_check=True)
                c.wait("vector", mm, mm.n)
                c.op("vector", "tensor_copy", out=g2s[:], in_=ps_G2[:])
                c.op("vector", "tensor_tensor", out=gtmp[:], in0=g2s[:, 0:8], in1=g2s[:, 8:16], op=ALU.add)
                c.op("vector", "tensor_scalar", out=g128[:, e, :], in0=gtmp[:],
                     scalar1=0.5, scalar2=None, op0=ALU.mult)
                c.op("vector", "scalar_tensor_tensor", out=idx128f[:], in0=g2s[:, 16:24],
                     scalar=32.0, in1=g2s[:, 24:32], op0=ALU.mult, op1=ALU.add)
                c.op("vector", "tensor_copy", sem=ve, out=idx32[:, e, :], in_=idx128f[:])
                c.wait("tensor", ve, ve.n)
        routing_done = ve.n
        if routing_only:
            dsem = c.sem(t_("dsem"))
            c.wait("sync", ve, routing_done)
            for nm, tile in dbg.items():
                if nm.startswith("rt_"):
                    c.dma("sync", tile, {"rt_jbf": j_bf, "rt_abf": a_bf, "rt_phi": phi,
                                         "rt_plo": plo, "rt_Fh8": Fh8, "rt_Fm8": Fm8,
                                         "rt_Fh8w": Fh8w, "rt_Fl16": Fl16, "rt_g2s": g2s,
                                         "rt_posT": posT, "rt_probT": probT, "rt_qbf": q_bf,
                                         "rt_probs": probs, "rt_expc": expc,
                                         "rt_rnorm": rnorm, "rt_exps": exp_s,
                                         "rt_maskc": maskc, "rt_pos": pos, "rt_thc": thc,
                                         "rt_rbf": r_bf}[nm][:], dsem)
            c.dma("sync", dbg["idx32"][:], idx32[:], dsem)
            c.dma("sync", dbg["g128"][:], g128[:], dsem)
            if "xln" in dbg:
                c.dma("sync", dbg["xln"][:], xln_tm[:], dsem)
            c.wait("sync", dsem, dsem.n)
            return None

    xin = sb("ex_xin", [128, 2, 8, 1024], BF16)
    y_sb = disp["y_sb"]
    gtm = disp["gtm"]
    with ExitStack() as ph:
        sbp = lambda name, shape, dt: ph.enter_context(nc.sbuf_tensor(t_("ex_" + name), shape, dt))
        w2sb = sbp("w2sb", [128, 32, D], BF16)
        h_sb = sbp("h", [128, 32, 512], BF16)
        ps8 = ph.enter_context(nc.psum_tensor(t_("ps8"), [128, 8, 512], F32))

        gather_done = {}
        scatter_done = {}
        gtm_free = [None]
        xinf_free = {}

        def emit_gathers(e):
            c.wait("gpsimd", ve, routing_done)
            if e >= 2:
                c.wait("gpsimd", xga, xinf_free[e % 2])
            for j in range(8):
                if gtm_free[0] is not None:
                    c.wait("gpsimd", gat, gtm_free[0])
                v = gat.add(16)
                def f(eng, e=e, j=j):
                    eng.indirect_dma_start(
                        out=gtm[:, 0, :], out_offset=None, in_=xln_tm[:],
                        in_offset=IndirectOffsetOnAxis(idx32[:, e, j:j + 1], 0)
                    ).then_inc(gat.h, 16)
                c.emit("gpsimd", f)
                c.wait("gpsimd", gat, gat.n)
                vout = c.dma("gpsimd", xinf[e % 2][128 * j:128 * (j + 1), :],
                             gtm[:, 0, :], gat)
                gtm_free[0] = vout
            c.wait("sync", gat, gtm_free[0])
            if e >= 2:
                c.wait("sync", mmG, mmG_b + 64 * (e - 1))
            for k in range(8):
                c.dma_tp("sync", xin[:, e % 2, k, :],
                         xinf[e % 2][:, 128 * k:128 * (k + 1)], xga)
            xinf_free[e % 2] = xga.n
            gather_done[e] = xga.n

        def emit_w2c(e):
            if e == 0:
                c.wait("gpsimd", ve, routing_done)
            for k in range(32):
                if e > 0:
                    if k <= 30:
                        c.wait("gpsimd", mmK, mmK_b + 62 * (e - 1) + 31 + k + 1)
                    else:
                        c.wait("gpsimd", mmY, mmY_b + 8 * e)
                w2c_done[(e, k)] = c.dma("gpsimd", w2sb[:, k, :],
                                         w2[e, 128 * k:128 * (k + 1), :], w2c)

        def emit_scatters(e, hcall):
            if True:
                for s in range(4):
                    c.wait("gpsimd", yv, yv_b + 8 * e + 4 * hcall + s + 1)
                    c.wait("gpsimd", seedS, seed_done)
                    if sca.n > 0:
                        c.wait("gpsimd", sca, sca.n)
                    v = sca.add(16)
                    def f(eng, e=e, hcall=hcall, s=s):
                        eng.indirect_dma_start(
                            out=x_out[:],
                            out_offset=IndirectOffsetOnAxis(idx32[:, e, 4 * hcall + s:4 * hcall + s + 1], 0),
                            in_=y_sb[:, s % 2, :], in_offset=None,
                            compute_op=ALU.add).then_inc(sca.h, 16)
                    c.emit("gpsimd", f)
                    scatter_done[(e, hcall, s)] = v

        emit_gathers(0)
        emit_gathers(1)
        for e in range(E):
            emit_w2c(e)
            emit_scatters(e, 0)
            if e + 1 < E:
                emit_w1c(e + 1)
            emit_scatters(e, 1)
            if e + 2 < E:
                emit_gathers(e + 2)
        if dbg is not None:
            dbgs = c.sem(t_("dbgs"))
            c.wait("sync", ve, routing_done)
            c.dma("sync", dbg["idx32"][:], idx32[:], dbgs)
            c.dma("sync", dbg["g128"][:], g128[:], dbgs)
            c.wait("sync", dbgs, dbgs.n)
        final_sca = sca.n
        c.wait("gpsimd", sca, final_sca)

        grp = 0
        for e in range(E):
            c.wait("tensor", xga, gather_done[e])
            for hl in range(2):
                if 2 * e + hl > 0:
                    c.wait("tensor", yv, yv_b + 8 * e + 4 * hl)
                for mg in range(4):
                    if grp > 0:
                        c.wait("tensor", geS, geS_b + 8 * grp)
                    for k in range(8):
                        if hl == 0 and mg == 0:
                            c.wait("tensor", w1c, w1c_done[e][k])
                        for ml in range(8):
                            m = mg * 8 + ml
                            c.op("tensor", "matmul", sem=(mmG if k == 7 else None),
                                 out=ps8[:, ml, :],
                                 lhsT=w1sb[:, k, 128 * m:128 * (m + 1)],
                                 rhs=xin[:, e % 2, k, 512 * hl:512 * (hl + 1)],
                                 start=(k == 0), stop=(k == 7), skip_group_check=True)
                    for ml in range(8):
                        m = mg * 8 + ml
                        c.wait("scalar", mmG, mmG_b + 8 * grp + ml + 1)
                        c.op("scalar", "activation", sem=geS,
                             out=h_sb[:, m, :], in_=ps8[:, ml, :], func=AF.Gelu)
                    grp += 1
                c.wait("tensor", geS, geS_b + 8 * grp)
                for k in range(32):
                    c.wait("tensor", w2c, w2c_done[(e, k)])
                    for s in range(4):
                        c.op("tensor", "matmul",
                             out=ps8[:, 2 * s, :],
                             lhsT=h_sb[:, k, 128 * s:128 * (s + 1)],
                             rhs=w2sb[:, k, 0:512],
                             start=(k == 0), stop=(k == 31), skip_group_check=True)
                        c.op("tensor", "matmul",
                             sem=(mmY if k == 31 else (mmK if s == 3 else None)),
                             out=ps8[:, 2 * s + 1, :],
                             lhsT=h_sb[:, k, 128 * s:128 * (s + 1)],
                             rhs=w2sb[:, k, 512:1024],
                             start=(k == 0), stop=(k == 31), skip_group_check=True)
                for s in range(4):
                    c.wait("vector", mmY, mmY_b + 4 * (2 * e + hl) + s + 1)
                    if s >= 2:
                        c.wait("vector", sca, scatter_done[(e, hl, s - 2)])
                    elif hl == 1:
                        c.wait("vector", sca, scatter_done[(e, 0, 2 + s)])
                    elif e >= 1:
                        c.wait("vector", sca, scatter_done[(e - 1, 1, 2 + s)])
                    c.op("vector", "tensor_scalar", sem=yv,
                         out=y_sb[:, s % 2, :].rearrange("p (a b) -> p a b", a=2),
                         in0=ps8[:, 2 * s:2 * s + 2, :],
                         scalar1=g128[:, e, 4 * hl + s:4 * hl + s + 1], scalar2=None,
                         op0=ALU.mult)
    return final_sca


def load_consts(c, cst):
    nc = c.nc
    con = {}
    for nm, (shape, dt) in CONST_SPECS.items():
        con[nm] = c.stack.enter_context(nc.sbuf_tensor("cs_" + nm, shape, dt))
    cl = c.sem("cl")
    for nm in CONST_SPECS:
        c.dma("sync", con[nm][:], cst[nm][:], cl)
    for engx in ("tensor", "vector", "scalar", "gpsimd"):
        c.wait(engx, cl, cl.n)
    return con


def build_ffn_program():
    nc = bass.Bass("TRN2", target_bir_lowering=False, detect_race_conditions=False)
    x_in = nc.dram_tensor("x_in", [T, D], F32, kind="ExternalInput").ap()
    gate_wT = nc.dram_tensor("gate_wT", [D, E], F32, kind="ExternalInput").ap()
    w1 = nc.dram_tensor("w1", [E, D, H], F32, kind="ExternalInput").ap()
    w2 = nc.dram_tensor("w2", [E, H, D], F32, kind="ExternalInput").ap()
    x_out = nc.dram_tensor("x_out", [T, D], F32, kind="ExternalOutput").ap()
    cst = declare_consts(nc)
    xln_tm = nc.dram_tensor("xln_tm", [T, D], BF16, kind="Internal").ap()
    expd = nc.dram_tensor("expd", [E * T], F32, kind="Internal").ap()

    dbg = {}
    dbg["idx"] = nc.dram_tensor("dbg_idx", [128, E, 64], I16, kind="ExternalOutput").ap()
    dbg["idx32"] = nc.dram_tensor("dbg_idx32", [128, E, 8], I32, kind="ExternalOutput").ap()
    dbg["g128"] = nc.dram_tensor("dbg_g128", [128, E, 8], F32, kind="ExternalOutput").ap()
    with ExitStack() as stack:
        c = Ctx(nc, stack)
        con = load_consts(c, cst)
        emit_ffn(c, x_in, x_out, gate_wT, w1, w2, cst, con, xln_tm, expd, dbg=dbg)
        c.finalize()
    mybir.codegen_inst_isa_subclasses(nc)
    return nc


def ffn_inputs(x, gate_w, w1, w2, l0_style=True):
    cc = host_consts()
    gw = gate_w if l0_style else np.ascontiguousarray(gate_w.T)
    m = {"x_in": np.ascontiguousarray(x), "gate_wT": np.ascontiguousarray(gw),
         "w1": np.ascontiguousarray(w1), "w2": np.ascontiguousarray(w2)}
    for k, v in cc.items():
        m["c_" + k] = v
    return m


def build_routing_probe():
    nc = bass.Bass("TRN2", target_bir_lowering=False, detect_race_conditions=False)
    x_in = nc.dram_tensor("x_in", [T, D], F32, kind="ExternalInput").ap()
    gate_wT = nc.dram_tensor("gate_wT", [D, E], F32, kind="ExternalInput").ap()
    w1 = nc.dram_tensor("w1", [E, D, H], F32, kind="ExternalInput").ap()
    w2 = nc.dram_tensor("w2", [E, H, D], F32, kind="ExternalInput").ap()
    x_out = nc.dram_tensor("x_out", [T, D], F32, kind="ExternalOutput").ap()
    cst = declare_consts(nc)
    xln_tm = nc.dram_tensor("xln_tm", [T, D], BF16, kind="Internal").ap()
    expd = nc.dram_tensor("expd", [E * T], F32, kind="Internal").ap()
    dbg = {}
    dbg["idx"] = nc.dram_tensor("dbg_idx", [128, E, 64], I16, kind="ExternalOutput").ap()
    dbg["idx32"] = nc.dram_tensor("dbg_idx32", [128, E, 8], I32, kind="ExternalOutput").ap()
    dbg["g128"] = nc.dram_tensor("dbg_g128", [128, E, 8], F32, kind="ExternalOutput").ap()
    dbg["xln"] = nc.dram_tensor("dbg_xln", [T, D], BF16, kind="ExternalOutput").ap()
    for nm, shape, dt in (("rt_jbf", [128,2,128], BF16), ("rt_abf", [128,2,128], BF16),
                          ("rt_phi", [128,2,128], BF16), ("rt_plo", [128,2,128], BF16),
                          ("rt_Fh8", [128,2,32,8], BF16), ("rt_Fm8", [128,2,32,8], BF16),
                          ("rt_Fh8w", [128,2,32,32], BF16), ("rt_Fl16", [128,2,32,16], BF16),
                          ("rt_g2s", [128,32], F32), ("rt_posT", [128,2,128], F32),
                          ("rt_probT", [128,2,128], F32), ("rt_qbf", [128,2,128], BF16),
                          ("rt_rbf", [128,2,128], BF16),
                          ("rt_probs", [128,256], F32), ("rt_expc", [128,256], F32),
                          ("rt_rnorm", [128,256], F32), ("rt_exps", [8, T], F32),
                          ("rt_maskc", [128,256], F32), ("rt_pos", [128,256], F32),
                          ("rt_thc", [128,1], F32)):
        dbg[nm] = nc.dram_tensor("o" + nm, shape, dt, kind="ExternalOutput").ap()
    with ExitStack() as stack:
        c = Ctx(nc, stack)
        con = load_consts(c, cst)
        emit_ffn(c, x_in, x_out, gate_wT, w1, w2, cst, con, xln_tm, expd, dbg=dbg, routing_only=True)
        c.finalize()
    mybir.codegen_inst_isa_subclasses(nc)
    return nc



ATTN0_CONSTS = {
    "D64": ([128, 64], BF16),
    "DUP": ([64, 128], F32),
}


def attn0_host_consts():
    c = {}
    D64 = np.zeros((128, 64), np.float32)
    for p in range(128):
        D64[p, p & 63] = 1.0
    c["D64"] = D64.astype(bfdt)
    DUP = np.zeros((64, 128), np.float32)
    for p in range(128):
        DUP[p & 63, p] = 1.0
    c["DUP"] = DUP
    return c


def emit_attn0(c, x_in, x_out, Wq, Wk, Wv, Wo, bo, con, xln_tm, files, tag="a0", stag=None):
    if stag is None:
        stag = tag
    nc = c.nc
    st_ = c.stack
    t_ = lambda n: tag + n
    s_ = lambda n: stag + n
    sb = lambda name, shape, dt: st_.enter_context(nc.sbuf_tensor(t_(name), shape, dt))

    mm = c.sem(s_("mm"))
    ld = c.sem(s_("ld"))
    gld = c.sem(s_("gld"))
    stf = c.sem(s_("stf"))
    SCALE = 64.0 ** -0.5
    sc_self = lambda: c.last_op.get("scalar")
    ve_self = lambda: c.last_op.get("vector")

    with ExitStack() as ph:
        sbp = lambda name, shape, dt: ph.enter_context(nc.sbuf_tensor(t_("ln_" + name), shape, dt))
        bufs = {
            "ln_x": sbp("x", [128, 2, 8, D], F32),
            "ln_x2": sbp("x2", [128, 8, D], F32),
            "ln_sm": sbp("sm", [128, 2, 8, 8], F32),
            "ln_out": sbp("o", [128, 2, 8, D], BF16),
        }
        emit_ln(c, bufs, x_in, xln_tm, eps_ap=con["eps5"][:, 0:1], tag=s_("L"))
    ln_st = c.sems[s_("L") + "st"]
    ln_done = ln_st.n
    for _e in ("gpsimd", "vector", "scalar"):
        c.wait(_e, ln_st, ln_done)

    with ExitStack() as ph:
        sbp = lambda name, shape, dt: ph.enter_context(nc.sbuf_tensor(t_("qk_" + name), shape, dt))
        xlnT = sbp("xlnT", [128, 8, T], BF16)
        wt = sbp("wt", [128, 2, 8, 128], BF16)
        ev = sbp("ev", [128, 2, 8, 512], BF16)
        qtmp = sbp("qtmp", [128, 2, 512], F32)
        ks_t = sbp("ks", [64, 2, 512], F32)
        ksr = sbp("ksr", [128, 8, 512], BF16)
        krb = sbp("krb", [128, 2, 8, 512], BF16)
        ps = ph.enter_context(nc.psum_tensor(t_("ps_qk"), [128, 8, 512], F32))

        c.wait("sync", ln_st, ln_done)
        xb_vals = []
        for ci in range(8):
            v = c.dma_tp("sync", xlnT[:, ci, :], xln_tm[:, 128 * ci:128 * (ci + 1)], ld)
            xb_vals.append(v)

        pass_tail = [None, None]

        def gemm_pass(W, outfile, kind):
            base_mm = mm.n
            evdone = []
            for mt in range(32):
                b = mt % 2
                if mt >= 2:
                    c.wait("gpsimd", mm, base_mm + 8 * (mt - 1))
                else:
                    c.wait("gpsimd", mm, base_mm)
                c.dma("gpsimd", wt[:, b],
                      W[:, 128 * mt:128 * (mt + 1)].rearrange("(c p) m -> p c m", p=128), gld)
                wv_ok = gld.n
                c.wait("tensor", gld, wv_ok)
                if mt == 0:
                    for v in xb_vals:
                        c.wait("tensor", ld, v)
                if mt >= 1:
                    prev = c.last_op.get("scalar")
                    if prev is not None:
                        c.wait("tensor", prev[0], prev[1])
                    if kind == "q":
                        pv = c.last_op.get("vector")
                        if pv is not None:
                            c.wait("tensor", pv[0], pv[1])
                for ci in range(8):
                    for n in range(8):
                        c.op("tensor", "matmul", sem=(mm if n == 7 else None),
                             out=ps[:, n, :], lhsT=wt[:, b, ci, :],
                             rhs=xlnT[:, ci, 512 * n:512 * (n + 1)],
                             start=(ci == 0), stop=(ci == 7), skip_group_check=True)
                c.wait("scalar", mm, base_mm + 8 * (mt + 1))
                if mt >= 2:
                    c.wait("scalar", stf, evdone[mt - 2])
                elif pass_tail[mt] is not None:
                    c.wait("scalar", stf, pass_tail[mt])
                vec_marks = {}
                for n in range(8):
                    if kind == "q":
                        if n >= 2:
                            m2 = vec_marks[n - 2]
                            c.wait("scalar", m2[0], m2[1])
                        c.op("scalar", "activation", out=qtmp[:, n % 2, :], in_=ps[:, n, :],
                             func=AF.Relu, scale=SCALE)
                        pv = c.last_op.get("scalar")
                        c.wait("vector", pv[0], pv[1])
                        if mt >= 2 and n == 0:
                            c.wait("vector", stf, evdone[mt - 2])
                        elif mt < 2 and n == 0 and pass_tail[mt] is not None:
                            c.wait("vector", stf, pass_tail[mt])
                        c.op("vector", "tensor_tensor", out=ev[:, b, n, :],
                             in0=qtmp[:, n % 2, :], in1=ksr[:, n, :], op=ALU.mult)
                        vec_marks[n] = c.last_op["vector"]
                    elif kind == "k":
                        c.op("scalar", "activation", out=ev[:, b, n, :], in_=ps[:, n, :],
                             func=AF.Relu)
                    else:
                        c.op("scalar", "activation", out=ev[:, b, n, :], in_=ps[:, n, :],
                             func=AF.Copy)
                prev = c.last_op.get("vector" if kind == "q" else "scalar")
                c.wait("sync", prev[0], prev[1])
                v = c.dma("sync", outfile[128 * mt:128 * (mt + 1), :].rearrange("p (n t) -> p n t", n=8),
                          ev[:, b], stf)
                evdone.append(v)
            pass_tail[0] = evdone[-2]
            pass_tail[1] = evdone[-1]
            return base_mm

        gemm_pass(Wk, files["kf"], "k")
        k_done = stf.n

        pv = c.last_op["scalar"]
        c.wait("tensor", pv[0], pv[1])
        for nt in range(8):
            for grp in range(4):
                b = grp % 2
                if grp >= 2 or nt > 0:
                    c.wait("sync", mm, mm.n)
                if nt == 0 and grp == 0:
                    c.wait("sync", stf, k_done)
                c.dma("sync", krb[:, b],
                      files["kf"][1024 * grp:1024 * (grp + 1), 512 * nt:512 * (nt + 1)]
                      .rearrange("(ct p) t -> p ct t", p=128), ld)
                c.wait("tensor", ld, ld.n)
                for ct in range(8):
                    c.op("tensor", "matmul",
                         sem=(mm if (ct == 7) else None),
                         out=ps[0:64, 0, :], lhsT=con["D64"][:], rhs=krb[:, b, ct, :],
                         start=(grp == 0 and ct == 0), stop=(grp == 3 and ct == 7),
                         skip_group_check=True)
            c.wait("vector", mm, mm.n)
            c.op("vector", "tensor_copy", out=ks_t[:, nt % 2, :], in_=ps[0:64, 0, :])
            pv = c.last_op.get("vector")
            c.wait("tensor", pv[0], pv[1])
            c.op("tensor", "matmul", sem=mm, out=ps[:, 1, :], lhsT=con["DUP"][:],
                 rhs=ks_t[:, nt % 2, :], start=True, stop=True, skip_group_check=True)
            c.wait("vector", mm, mm.n)
            c.op("vector", "tensor_copy", out=ksr[:, nt, :], in_=ps[:, 1, :])
        pv = c.last_op.get("vector")
        c.wait("tensor", pv[0], pv[1])

        gemm_pass(Wq, files["qf"], "q")
        gemm_pass(Wv, files["vf"], "v")
        qv_done = stf.n

    G = 256
    NG = T // G
    with ExitStack() as ph:
        sbp = lambda name, shape, dt: ph.enter_context(nc.sbuf_tensor(t_("sm_" + name), shape, dt))
        qrb = sbp("qrb", [64, 2, 64, G], BF16)
        vrb = sbp("vrb", [64, 2, 65, G], BF16)
        og = sbp("og", [128, 2, 64, 128], BF16)
        rc = sbp("rc", [128, 2, 16], F32)
        ps_a = ph.enter_context(nc.psum_tensor(t_("ps_a"), [128, 2, 16, 128], F32))
        sld = c.sem(s_("sld"))
        st2 = c.sem(s_("st2"))

        c.op("vector", "memset", ap=vrb[:, 0, 64, :], constant=1.0)
        c.op("vector", "memset", ap=vrb[:, 1, 64, :], constant=1.0)
        ones_set = c.last_op["vector"]

        qview = files["qf"].rearrange("(i d) t -> d i t", d=64)
        vview = files["vf"].rearrange("(j d) t -> d j t", d=64)
        c.wait("scalar", stf, qv_done)
        rb_vals = {}

        def load_half(g, half, wait_mm):
            b = g % 2
            if wait_mm is not None:
                c.wait("scalar", mm, wait_mm)
                c.wait("sync", mm, wait_mm)
            tsl = slice(G * g + 128 * half, G * g + 128 * (half + 1))
            lsl = slice(128 * half, 128 * (half + 1))
            c.dma("scalar", qrb[:, b, :, lsl], qview[:, :, tsl], sld)
            v = c.dma("sync", vrb[:, b, 0:64, lsl], vview[:, :, tsl], ld)
            return (sld.n, v)

        rb_vals[0] = [load_half(0, 0, None), load_half(0, 1, None)]
        rb_vals[1] = [load_half(1, 0, None), load_half(1, 1, None)]
        ev_done_flight = {}
        og_store = {}
        NF = 8
        for g in range(NG):
            b = g % 2
            c.wait("tensor", sld, rb_vals[g][0][0])
            c.wait("tensor", ld, rb_vals[g][0][1])
            for f in range(NF):
                if f == 4:
                    c.wait("tensor", sld, rb_vals[g][1][0])
                    c.wait("tensor", ld, rb_vals[g][1][1])
                pb = f % 2
                key = (g, f - 2) if f >= 2 else (g - 1, f + NF - 2)
                if key in ev_done_flight:
                    c.wait("tensor", ev_done_flight[key][0], ev_done_flight[key][1])
                for u in range(32):
                    tloc = 32 * f + u
                    h = u & 1
                    s = u >> 1
                    tp = (0, 64) if h else None
                    c.op("tensor", "matmul", sem=(mm if u == 31 else None),
                         out=ps_a[64 * h:64 * (h + 1), pb, s, 0:65],
                         lhsT=qrb[:, b, :, tloc], rhs=vrb[:, b, :, tloc],
                         start=True, stop=True, skip_group_check=True,
                         tile_position=tp)
                mmv = mm.n
                c.wait("vector", mm, mmv)
                c.op("vector", "tensor_scalar", out=rc[:, pb, :],
                     in0=ps_a[:, pb, :, 64:65], scalar1=1e-6, scalar2=None, op0=ALU.add)
                c.op("vector", "reciprocal", out=rc[:, pb, :], in_=rc[:, pb, :])
                if f == 0 and g >= 2:
                    c.wait("vector", stf, og_store[g - 2][0])
                    c.wait("vector", st2, og_store[g - 2][1])
                c.op("vector", "tensor_tensor",
                     out=og[:, b, :, 16 * f:16 * (f + 1)].transpose([0, 2, 1]),
                     in0=ps_a[:, pb, :, 0:64],
                     in1=rc[:, pb, :].unsqueeze(2).broadcast_to([128, 16, 64]),
                     op=ALU.mult)
                ev_done_flight[(g, f)] = c.last_op["vector"]
                if f == 3 and g + 2 < NG:
                    rb_vals[g + 2] = [load_half(g + 2, 0, mm.n)]
            if g + 2 < NG:
                rb_vals[g + 2].append(load_half(g + 2, 1, mm.n))
            pv = c.last_op["vector"]
            c.wait("sync", pv[0], pv[1])
            c.wait("scalar", pv[0], pv[1])
            v_e = c.dma("sync", files["oe"].rearrange("(i j) u -> i j u", j=64)[:, :, 128 * g:128 * (g + 1)],
                        og[0:64, b], stf)
            v_o = c.dma("scalar", files["oo"].rearrange("(i j) u -> i j u", j=64)[:, :, 128 * g:128 * (g + 1)],
                        og[64:128, b], st2)
            og_store[g] = (v_e, v_o)
        c.wait("sync", st2, st2.n)
        o_done = stf.n

    with ExitStack() as ph:
        sbp = lambda name, shape, dt: ph.enter_context(nc.sbuf_tensor(t_("wo_" + name), shape, dt))
        wo_sb = sbp("wo", [128, 32, D], BF16)
        orb = sbp("orb", [128, 2, 512], BF16)
        rev = sbp("rev", [128, 8, 512], BF16)
        bo_sb = sbp("bo", [128, 8], F32)
        ps_r = ph.enter_context(nc.psum_tensor(t_("ps_r"), [128, 8, 512], F32))

        st2 = c.sem(s_("st2"))
        c.wait("sync", stf, o_done)
        c.wait("gpsimd", stf, o_done)
        c.wait("gpsimd", st2, st2.n)
        c.dma("sync", bo_sb[:], bo[:], ld)
        for ct in range(32):
            c.dma("gpsimd", wo_sb[:, ct, :], Wo[128 * ct:128 * (ct + 1), :], gld)
        wo_ok = gld.n
        c.wait("tensor", gld, wo_ok)
        c.wait("tensor", ld, ld.n)

        xt = sbp("ra_x", [128, 2, D], F32)
        rt = sbp("ra_r", [128, 2, D], BF16)
        ot = sbp("ra_o", [128, 2, D], F32)
        rao = c.sem(s_("rao"))
        rlx = c.sem(s_("rlx"))
        xv = x_in.rearrange("(u two) d -> two u d", two=2)
        ov = x_out.rearrange("(u two) d -> two u d", two=2)

        rfiles = {0: files["re"], 1: files["ro"]}
        ofiles = {0: files["oe"], 1: files["oo"]}
        first = True
        add_m = {}
        rao_m = {}
        for par in range(2):
            for ut in range(4):
                base_mm = mm.n
                for ct in range(32):
                    bb = ct % 2
                    if ct >= 2:
                        c.wait("sync", mm, base_mm + ct - 1)
                    if first and ct < 2:
                        c.wait("sync", stf, o_done)
                        c.wait("sync", st2, st2.n)
                    c.dma("sync", orb[:, bb, :],
                          ofiles[par][128 * ct:128 * (ct + 1), 512 * ut:512 * (ut + 1)], ld)
                first = False
                lv = ld.n
                for ct in range(32):
                    bb = ct % 2
                    c.wait("tensor", ld, lv - 16 * (31 - ct))
                    if ct == 0 and (par + ut > 0):
                        prev = ev_m
                        c.wait("tensor", prev[0], prev[1])
                    for cc in range(8):
                        c.op("tensor", "matmul", sem=(mm if cc == 7 else None),
                             out=ps_r[:, cc, :], lhsT=wo_sb[:, ct, 128 * cc:128 * (cc + 1)],
                             rhs=orb[:, bb, :],
                             start=(ct == 0), stop=(ct == 31), skip_group_check=True)
                c.wait("vector", mm, mm.n)
                if par + ut > 0:
                    c.wait("vector", stf, stf.n)
                for cc in range(8):
                    c.op("vector", "tensor_scalar", out=rev[:, cc, :], in0=ps_r[:, cc, :],
                         scalar1=bo_sb[:, cc:cc + 1], scalar2=None, op0=ALU.add)
                ev_m = c.last_op["vector"]
                c.wait("sync", ev_m[0], ev_m[1])
                st_r = c.dma("sync", rfiles[par][:, 512 * ut:512 * (ut + 1)]
                             .rearrange("(cc p) t -> p cc t", p=128), rev[:], stf)
                for pt in range(4 * ut, 4 * ut + 4):
                    idx = par * 16 + pt
                    b = idx % 2
                    if idx >= 2:
                        c.wait("scalar", *add_m[idx - 2])
                    c.dma("scalar", xt[:, b, :], xv[par, 128 * pt:128 * (pt + 1), :], rlx)
                    c.wait("sync", stf, st_r)
                    c.dma_tp("sync", rt[:, b, :], rfiles[par][:, 128 * pt:128 * (pt + 1)], ld)
                    c.wait("vector", ld, ld.n)
                    c.wait("vector", rlx, rlx.n)
                    if idx >= 2:
                        c.wait("vector", rao, rao_m[idx - 2])
                    c.op("vector", "tensor_tensor", out=ot[:, b, :], in0=xt[:, b, :],
                         in1=rt[:, b, :], op=ALU.add)
                    add_m[idx] = c.last_op["vector"]
                    c.wait("sync", *add_m[idx])
                    rao_m[idx] = c.dma("sync", ov[par, 128 * pt:128 * (pt + 1), :],
                                       ot[:, b, :], rao)
        c.wait("sync", rao, rao.n)
    return rao.n



def attn1_host_consts(extra=True):
    return {"onesrow": np.ones((1, 128), np.float32)}

ATTN1_CONSTS = {"onesrow": ([1, 128], F32)}


def emit_attn1(c, x_in, x_out, Wt, Wo, Wg, par, con, xln_tm, files, tag="a1", stag=None):
    if stag is None:
        stag = tag
    nc = c.nc
    st_ = c.stack
    t_ = lambda n: tag + n
    s_ = lambda n: stag + n
    sb = lambda name, shape, dt: st_.enter_context(nc.sbuf_tensor(t_(name), shape, dt))

    mm = c.sem(s_("mm"))
    ld = c.sem(s_("ld"))
    gld = c.sem(s_("gld"))
    stf = c.sem(s_("stf"))

    pt = {}
    pl = c.sem(s_("pl"))
    for nm in ("dec", "beta", "lncw", "lncb", "lngw", "lngb", "bg"):
        pt[nm] = sb(nm, [128, 8], F32)
        c.dma("sync", pt[nm][:], par[nm][:], pl)
    for nm in ("g1w", "g1b"):
        pt[nm] = sb(nm, [128, D], F32)
        c.dma("sync", pt[nm][:], par[nm][:], pl)
    pl_done = pl.n
    for e in ("vector", "scalar", "gpsimd", "tensor"):
        c.wait(e, pl, pl_done)

    with ExitStack() as ph:
        sbp = lambda name, shape, dt: ph.enter_context(nc.sbuf_tensor(t_("ln_" + name), shape, dt))
        bufs = {
            "ln_x": sbp("x", [128, 2, 8, D], F32),
            "ln_x2": sbp("x2", [128, 8, D], F32),
            "ln_sm": sbp("sm", [128, 2, 8, 8], F32),
            "ln_out": sbp("o", [128, 2, 8, D], BF16),
        }
        emit_ln(c, bufs, x_in, xln_tm, eps_ap=con["eps6"][:, 0:1],
                gamma=pt["g1w"], beta=pt["g1b"], tag=s_("L"))
    ln_st = c.sems[s_("L") + "st"]
    ln_done = ln_st.n
    for _e in ("gpsimd", "vector", "scalar"):
        c.wait(_e, ln_st, ln_done)

    xlnT1 = sb("xlnT1", [128, 8, T], BF16)
    c.wait("sync", ln_st, ln_done)
    xb_vals = []
    for ci in range(8):
        v = c.dma_tp("sync", xlnT1[:, ci, :], xln_tm[:, 128 * ci:128 * (ci + 1)], ld)
        xb_vals.append(v)

    with ExitStack() as ph:
        sbp = lambda name, shape, dt: ph.enter_context(nc.sbuf_tensor(t_("u_" + name), shape, dt))
        wt_sb = sbp("wt", [128, 2, 8, 128], BF16)
        ud = sbp("ud", [128, T], BF16)
        cxd = sbp("cxd", [128, 2, T], BF16)
        ps = ph.enter_context(nc.psum_tensor(t_("ps_u"), [128, 8, 512], F32))

        base_mm = mm.n
        cx_store = {}
        for mt in range(8):
            b = mt % 2
            if mt >= 2:
                c.wait("gpsimd", mm, base_mm + 8 * (mt - 1))
            c.dma("gpsimd", wt_sb[:, b],
                  Wt[:, 128 * mt:128 * (mt + 1)].rearrange("(c p) m -> p c m", p=128), gld)
            c.wait("tensor", gld, gld.n)
            if mt == 0:
                for v in xb_vals:
                    c.wait("tensor", ld, v)
            if mt >= 1:
                pv = c.last_op["vector"]
                c.wait("tensor", pv[0], pv[1])
            for ci in range(8):
                for n in range(8):
                    c.op("tensor", "matmul", sem=(mm if n == 7 else None),
                         out=ps[:, n, :], lhsT=wt_sb[:, b, ci, :],
                         rhs=xlnT1[:, ci, 512 * n:512 * (n + 1)],
                         start=(ci == 0), stop=(ci == 7), skip_group_check=True)
            c.wait("vector", mm, base_mm + 8 * (mt + 1))
            for n in range(8):
                c.op("vector", "tensor_scalar", out=ud[:, 512 * n:512 * (n + 1)],
                     in0=ps[:, n, :], scalar1=pt["beta"][:, mt:mt + 1], scalar2=None,
                     op0=ALU.mult)
            if mt >= 2:
                c.wait("vector", stf, cx_store[mt - 2])
            c.op("vector", "tensor_tensor_scan", out=cxd[:, b, :],
                 data0=pt["dec"][:, mt:mt + 1].broadcast_to([128, T]), data1=ud[:],
                 initial=0.0, op0=ALU.mult, op1=ALU.add)
            pv = c.last_op["vector"]
            c.wait("sync", pv[0], pv[1])
            cx_store[mt] = c.dma("sync", files["cxf"][128 * mt:128 * (mt + 1), :], cxd[:, b, :], stf)
        cx_done = stf.n

    with ExitStack() as ph:
        sbp = lambda name, shape, dt: ph.enter_context(nc.sbuf_tensor(t_("cs_" + name), shape, dt))
        wo_sb = sbp("wo", [128, 8, D], BF16)
        cxg = sbp("cxg", [128, 8, 512], BF16)
        cx2 = sbp("cx2", [128, 8, 512], BF16)
        rrows = sbp("rr", [1, 4, 512], F32)
        mcb = sbp("mcb", [128, 512], F32)
        rcb = sbp("rcb", [128, 512], F32)
        lncg = sbp("lncg", [128, 8, 512], BF16)
        oev = sbp("oev", [128, 8, 512], BF16)
        ps = ph.enter_context(nc.psum_tensor(t_("ps_c"), [128, 8, 512], F32))

        c.wait("gpsimd", stf, cx_done)
        for ci in range(8):
            c.dma("gpsimd", wo_sb[:, ci, :], Wo[128 * ci:128 * (ci + 1), :], gld)
        wo_ok = gld.n
        cview = files["cxf"].rearrange("(dt p) t -> p dt t", p=128)
        out_store = {}
        for nt in range(8):
            if nt == 0:
                c.wait("sync", stf, cx_done)
            else:
                c.wait("sync", mm, mm.n)
            c.dma("sync", cxg[:], cview[:, :, 512 * nt:512 * (nt + 1)], ld)
            lv = ld.n
            c.wait("scalar", ld, lv)
            c.op("scalar", "activation", out=cx2[:].rearrange("p a t -> p (a t)"),
                 in_=cxg[:].rearrange("p a t -> p (a t)"), func=AF.Square)
            pv = c.last_op["scalar"]
            c.wait("tensor", ld, lv)
            c.wait("tensor", pv[0], pv[1])
            for dt_ in range(8):
                c.op("tensor", "matmul", out=ps[0:1, 0, :], lhsT=con["ones1"][:],
                     rhs=cxg[:, dt_, :], start=(dt_ == 0), stop=(dt_ == 7),
                     skip_group_check=True)
                c.op("tensor", "matmul", sem=(mm if dt_ == 7 else None),
                     out=ps[0:1, 1, :], lhsT=con["ones1"][:],
                     rhs=cx2[:, dt_, :], start=(dt_ == 0), stop=(dt_ == 7),
                     skip_group_check=True)
            c.wait("vector", mm, mm.n)
            c.op("vector", "tensor_scalar", out=rrows[:, 0, :], in0=ps[0:1, 0, :],
                 scalar1=1.0 / D, scalar2=None, op0=ALU.mult)
            c.op("vector", "tensor_scalar", out=rrows[:, 1, :], in0=ps[0:1, 1, :],
                 scalar1=1.0 / D, scalar2=None, op0=ALU.mult)
            c.op("vector", "tensor_tensor", out=rrows[:, 2, :], in0=rrows[:, 0, :],
                 in1=rrows[:, 0, :], op=ALU.mult)
            c.op("vector", "tensor_tensor", out=rrows[:, 2, :], in0=rrows[:, 1, :],
                 in1=rrows[:, 2, :], op=ALU.subtract)
            pv = c.last_op["vector"]
            c.wait("scalar", pv[0], pv[1])
            c.op("scalar", "activation", out=rrows[:, 3, :], in_=rrows[:, 2, :],
                 func=AF.Sqrt, bias=con["eps6"][0:1, 0:1])
            pv = c.last_op["scalar"]
            c.wait("vector", pv[0], pv[1])
            c.op("vector", "reciprocal", out=rrows[:, 3, :], in_=rrows[:, 3, :])
            pv = c.last_op["vector"]
            c.wait("tensor", pv[0], pv[1])
            c.op("tensor", "matmul", out=ps[:, 2, :], lhsT=con["onesrow"][:],
                 rhs=rrows[:, 0, :], start=True, stop=True, skip_group_check=True)
            c.op("tensor", "matmul", sem=mm, out=ps[:, 3, :], lhsT=con["onesrow"][:],
                 rhs=rrows[:, 3, :], start=True, stop=True, skip_group_check=True)
            c.wait("vector", mm, mm.n)
            c.op("vector", "tensor_copy", out=mcb[:], in_=ps[:, 2, :])
            c.op("vector", "tensor_copy", out=rcb[:], in_=ps[:, 3, :])
            c.op("vector", "tensor_tensor", out=lncg[:], in0=cxg[:],
                 in1=mcb[:].unsqueeze(1).broadcast_to([128, 8, 512]), op=ALU.subtract)
            c.op("vector", "tensor_tensor", out=lncg[:], in0=lncg[:],
                 in1=rcb[:].unsqueeze(1).broadcast_to([128, 8, 512]), op=ALU.mult)
            for dt_ in range(8):
                c.op("vector", "tensor_scalar", out=lncg[:, dt_, :],
                     in0=lncg[:, dt_, :], scalar1=pt["lncw"][:, dt_:dt_ + 1],
                     scalar2=pt["lncb"][:, dt_:dt_ + 1], op0=ALU.mult, op1=ALU.add)
            pv = c.last_op["vector"]
            c.wait("tensor", pv[0], pv[1])
            if nt == 0:
                c.wait("tensor", gld, wo_ok)
            if nt >= 1:
                c.wait("tensor", stf, out_store[nt - 1])
            for cc in range(8):
                for ci in range(8):
                    c.op("tensor", "matmul",
                         sem=(mm if ci == 7 else None),
                         out=ps[:, cc, :], lhsT=wo_sb[:, ci, 128 * cc:128 * (cc + 1)],
                         rhs=lncg[:, ci, :],
                         start=(ci == 0), stop=(ci == 7), skip_group_check=True)
            c.wait("vector", mm, mm.n)
            for cc in range(8):
                c.op("vector", "tensor_copy", out=oev[:, cc, :], in_=ps[:, cc, :])
            pv = c.last_op["vector"]
            c.wait("sync", pv[0], pv[1])
            out_store[nt] = c.dma("sync",
                                  files["outf"][:, 512 * nt:512 * (nt + 1)]
                                  .rearrange("(cc p) t -> p cc t", p=128), oev[:], stf)
        out_done = stf.n

    with ExitStack() as ph:
        sbp = lambda name, shape, dt: ph.enter_context(nc.sbuf_tensor(t_("g_" + name), shape, dt))
        wg_sb = sbp("wg", [128, 16, D], BF16)
        gpt = sbp("gpt", [128, 8, 512], BF16)
        gp2 = sbp("gp2", [128, 8, 512], BF16)
        grow = sbp("grow", [1, 4, 512], F32)
        mgb = sbp("mgb", [128, 512], F32)
        rgb = sbp("rgb", [128, 512], F32)
        gt = sbp("gt", [128, 8, 512], BF16)
        ofr = sbp("ofr", [128, 8, 512], BF16)
        mt_ = sbp("mt", [128, 8, 512], BF16)
        ra_x = sbp("ra_x", [128, 2, D], F32)
        ra_r = sbp("ra_r", [128, 2, D], BF16)
        ra_o = sbp("ra_o", [128, 2, D], F32)
        rao = c.sem(s_("rao"))
        rlx = c.sem(s_("rlx"))
        add_m = {}
        rao_m = {}
        ps = ph.enter_context(nc.psum_tensor(t_("ps_g"), [128, 8, 512], F32))

        c.wait("gpsimd", stf, out_done)
        for ci in range(16):
            c.dma("gpsimd", wg_sb[:, ci, :], Wg[128 * ci:128 * (ci + 1), :], gld)
        wg_ok = gld.n
        oview = files["outf"].rearrange("(cc p) t -> p cc t", p=128)
        mt_store = {}
        for nt in range(8):
            if nt == 0:
                c.wait("sync", stf, out_done)
            else:
                c.wait("sync", mm, mm.n)
            c.dma("sync", ofr[:], oview[:, :, 512 * nt:512 * (nt + 1)], ld)
            ol = ld.n
            if nt == 0:
                c.wait("tensor", gld, wg_ok)
            if nt >= 1:
                pv = c.last_op["vector"]
                c.wait("tensor", pv[0], pv[1])
            c.wait("tensor", ld, ol)
            for cc in range(8):
                for ci in range(16):
                    rhs = xlnT1[:, ci, 512 * nt:512 * (nt + 1)] if ci < 8 else \
                        ofr[:, ci - 8, :]
                    c.op("tensor", "matmul", sem=(mm if ci == 15 else None),
                         out=ps[:, cc, :], lhsT=wg_sb[:, ci, 128 * cc:128 * (cc + 1)],
                         rhs=rhs, start=(ci == 0), stop=(ci == 15), skip_group_check=True)
            c.wait("vector", mm, mm.n)
            for cc in range(8):
                c.op("vector", "tensor_scalar", out=gpt[:, cc, :], in0=ps[:, cc, :],
                     scalar1=pt["bg"][:, cc:cc + 1], scalar2=None, op0=ALU.add)
            pv = c.last_op["vector"]
            c.wait("scalar", pv[0], pv[1])
            c.op("scalar", "activation", out=gp2[:].rearrange("p a t -> p (a t)"),
                 in_=gpt[:].rearrange("p a t -> p (a t)"), func=AF.Square)
            pv2 = c.last_op["scalar"]
            c.wait("tensor", pv[0], pv[1])
            c.wait("tensor", pv2[0], pv2[1])
            for cc in range(8):
                c.op("tensor", "matmul", out=ps[0:1, 0, :], lhsT=con["ones1"][:],
                     rhs=gpt[:, cc, :], start=(cc == 0), stop=(cc == 7),
                     skip_group_check=True)
                c.op("tensor", "matmul", sem=(mm if cc == 7 else None),
                     out=ps[0:1, 1, :], lhsT=con["ones1"][:], rhs=gp2[:, cc, :],
                     start=(cc == 0), stop=(cc == 7), skip_group_check=True)
            c.wait("vector", mm, mm.n)
            c.op("vector", "tensor_scalar", out=grow[:, 0, :], in0=ps[0:1, 0, :],
                 scalar1=1.0 / D, scalar2=None, op0=ALU.mult)
            c.op("vector", "tensor_scalar", out=grow[:, 1, :], in0=ps[0:1, 1, :],
                 scalar1=1.0 / D, scalar2=None, op0=ALU.mult)
            c.op("vector", "tensor_tensor", out=grow[:, 2, :], in0=grow[:, 0, :],
                 in1=grow[:, 0, :], op=ALU.mult)
            c.op("vector", "tensor_tensor", out=grow[:, 2, :], in0=grow[:, 1, :],
                 in1=grow[:, 2, :], op=ALU.subtract)
            pv = c.last_op["vector"]
            c.wait("scalar", pv[0], pv[1])
            c.op("scalar", "activation", out=grow[:, 3, :], in_=grow[:, 2, :],
                 func=AF.Sqrt, bias=con["eps6"][0:1, 0:1])
            pv = c.last_op["scalar"]
            c.wait("vector", pv[0], pv[1])
            c.op("vector", "reciprocal", out=grow[:, 3, :], in_=grow[:, 3, :])
            pv = c.last_op["vector"]
            c.wait("tensor", pv[0], pv[1])
            c.op("tensor", "matmul", out=ps[:, 2, :], lhsT=con["onesrow"][:],
                 rhs=grow[:, 0, :], start=True, stop=True, skip_group_check=True)
            c.op("tensor", "matmul", sem=mm, out=ps[:, 3, :], lhsT=con["onesrow"][:],
                 rhs=grow[:, 3, :], start=True, stop=True, skip_group_check=True)
            c.wait("vector", mm, mm.n)
            c.op("vector", "tensor_copy", out=mgb[:], in_=ps[:, 2, :])
            c.op("vector", "tensor_copy", out=rgb[:], in_=ps[:, 3, :])
            for cc in range(8):
                c.op("vector", "tensor_tensor", out=gt[:, cc, :], in0=gpt[:, cc, :],
                     in1=mgb[:], op=ALU.subtract)
                c.op("vector", "tensor_tensor", out=gt[:, cc, :], in0=gt[:, cc, :],
                     in1=rgb[:], op=ALU.mult)
                c.op("vector", "tensor_scalar", out=gt[:, cc, :], in0=gt[:, cc, :],
                     scalar1=pt["lngw"][:, cc:cc + 1], scalar2=pt["lngb"][:, cc:cc + 1],
                     op0=ALU.mult, op1=ALU.add)
            pv = c.last_op["vector"]
            c.wait("scalar", pv[0], pv[1])
            c.op("scalar", "activation", out=gt[:].rearrange("p a t -> p (a t)"),
                 in_=gt[:].rearrange("p a t -> p (a t)"), func=AF.Sigmoid)
            pv = c.last_op["scalar"]
            c.wait("vector", pv[0], pv[1])
            if nt >= 2:
                c.wait("vector", stf, mt_store[nt - 2])
            for cc in range(8):
                c.op("vector", "tensor_tensor", out=mt_[:, cc, :], in0=ofr[:, cc, :],
                     in1=xlnT1[:, cc, 512 * nt:512 * (nt + 1)], op=ALU.subtract)
                c.op("vector", "tensor_tensor", out=mt_[:, cc, :], in0=gt[:, cc, :],
                     in1=mt_[:, cc, :], op=ALU.mult)
                c.op("vector", "tensor_tensor", out=mt_[:, cc, :],
                     in0=xlnT1[:, cc, 512 * nt:512 * (nt + 1)], in1=mt_[:, cc, :],
                     op=ALU.add)
            pv = c.last_op["vector"]
            c.wait("sync", pv[0], pv[1])
            mt_store[nt] = c.dma("sync", files["mtf"][:, 512 * nt:512 * (nt + 1)]
                                 .rearrange("(cc p) t -> p cc t", p=128), mt_[:], stf)
            for tt in range(4 * nt, 4 * nt + 4):
                b = tt % 2
                if tt >= 2:
                    c.wait("scalar", *add_m[tt - 2])
                c.dma("scalar", ra_x[:, b, :], x_in[128 * tt:128 * (tt + 1), :], rlx)
                c.wait("sync", stf, mt_store[nt])
                c.dma_tp("sync", ra_r[:, b, :], files["mtf"][:, 128 * tt:128 * (tt + 1)], ld)
                c.wait("vector", ld, ld.n)
                c.wait("vector", rlx, rlx.n)
                if tt >= 2:
                    c.wait("vector", rao, rao_m[tt - 2])
                c.op("vector", "tensor_tensor", out=ra_o[:, b, :], in0=ra_x[:, b, :],
                     in1=ra_r[:, b, :], op=ALU.add)
                add_m[tt] = c.last_op["vector"]
                c.wait("sync", *add_m[tt])
                rao_m[tt] = c.dma("sync", x_out[128 * tt:128 * (tt + 1), :],
                                  ra_o[:, b, :], rao)
        c.wait("sync", rao, rao.n)
    return rao.n



def build_fused_prog():
    nc = bass.Bass("TRN2", target_bir_lowering=False, detect_race_conditions=False)
    xi = nc.dram_tensor("x_in", [T, D], F32, kind="ExternalInput").ap()
    xo = nc.dram_tensor("x_out", [T, D], F32, kind="ExternalOutput").ap()
    xA = nc.dram_tensor("xA", [T, D], F32, kind="Internal").ap()
    xB = nc.dram_tensor("xB", [T, D], F32, kind="Internal").ap()

    gwT0 = nc.dram_tensor("gate_wT0", [D, E], F32, kind="ExternalInput").ap()
    w10 = nc.dram_tensor("w1_0", [E, D, H], F32, kind="ExternalInput").ap()
    w20 = nc.dram_tensor("w2_0", [E, H, D], F32, kind="ExternalInput").ap()
    gwT1 = nc.dram_tensor("gate_wT1", [D, E], F32, kind="ExternalInput").ap()
    w11 = nc.dram_tensor("w1_1", [E, D, H], F32, kind="ExternalInput").ap()
    w21 = nc.dram_tensor("w2_1", [E, H, D], F32, kind="ExternalInput").ap()
    Wq = nc.dram_tensor("Wq", [D, 4096], F32, kind="ExternalInput").ap()
    Wk = nc.dram_tensor("Wk", [D, 4096], F32, kind="ExternalInput").ap()
    Wv = nc.dram_tensor("Wv", [D, 4096], F32, kind="ExternalInput").ap()
    Wo0 = nc.dram_tensor("Wo0", [4096, D], F32, kind="ExternalInput").ap()
    bo = nc.dram_tensor("bo", [128, 8], F32, kind="ExternalInput").ap()
    Wt = nc.dram_tensor("Wt", [D, D], F32, kind="ExternalInput").ap()
    Wo1 = nc.dram_tensor("Wo1", [D, D], F32, kind="ExternalInput").ap()
    Wg = nc.dram_tensor("Wg", [2 * D, D], F32, kind="ExternalInput").ap()
    par = {}
    for nm in ("dec", "beta", "lncw", "lncb", "lngw", "lngb", "bg"):
        par[nm] = nc.dram_tensor("p_" + nm, [128, 8], F32, kind="ExternalInput").ap()
    for nm in ("g1w", "g1b"):
        par[nm] = nc.dram_tensor("p_" + nm, [128, D], F32, kind="ExternalInput").ap()

    cst = declare_consts(nc)
    for nm, (shape, dt) in {**ATTN0_CONSTS, **ATTN1_CONSTS}.items():
        cst[nm] = nc.dram_tensor("c_" + nm, shape, dt, kind="ExternalInput").ap()

    xln_tm = nc.dram_tensor("xln_tm", [T, D], BF16, kind="Internal").ap()
    expd = nc.dram_tensor("expd", [E * T], F32, kind="Internal").ap()
    xinf = nc.dram_tensor("xinf", [2, 1024, D], BF16, kind="Internal").ap()
    f0 = {}
    for nm, shape in (("qf", [4096, 4096]), ("kf", [4096, 4096]), ("vf", [4096, 4096]),
                      ("oe", [4096, 2048]), ("oo", [4096, 2048]),
                      ("re", [1024, 2048]), ("ro", [1024, 2048])):
        f0[nm] = nc.dram_tensor(nm, shape, BF16, kind="Internal").ap()
    f1 = {"cxf": nc.dram_tensor("cxf", [D, T], BF16, kind="Internal").ap(),
          "outf": nc.dram_tensor("outf", [D, T], BF16, kind="Internal").ap(),
          "mtf": nc.dram_tensor("mtf", [D, T], BF16, kind="Internal").ap()}

    with ExitStack() as gstack:
        c = Ctx(nc, gstack)
        con = load_consts(c, cst)
        cl = c.sem("cl")
        for nm in {**ATTN0_CONSTS, **ATTN1_CONSTS}:
            con[nm] = gstack.enter_context(nc.sbuf_tensor("cs_" + nm,
                                                          *({**ATTN0_CONSTS, **ATTN1_CONSTS}[nm])))
            c.dma("sync", con[nm][:], cst[nm][:], cl)
        for e in ("tensor", "vector", "scalar", "gpsimd"):
            c.wait(e, cl, cl.n)
        disp = {
            "idx32": gstack.enter_context(nc.sbuf_tensor("g_idx32", [128, E, 8], I32)),
            "g128": gstack.enter_context(nc.sbuf_tensor("g_g128", [128, E, 8], F32)),
            "y_sb": gstack.enter_context(nc.sbuf_tensor("g_y", [128, 2, D], F32)),
            "gtm": gstack.enter_context(nc.sbuf_tensor("g_gtm", [128, 1, D], BF16)),
        }

        def barrier(sem_name, val):
            s = c.sem(sem_name)
            for e in ENGINES:
                c.wait(e, s, val)

        def run_sub(fn):
            ss = ExitStack()
            old = c.stack
            c.stack = ss
            try:
                with ss:
                    fn()
            finally:
                c.stack = old

        run_sub(lambda: emit_ffn(c, xi, xA, gwT0, w10, w20, cst, con, xln_tm, expd,
                                 tag="f1", stag="G", disp=disp, xinf=xinf))
        barrier("Gsca", c.sem("Gsca").n)
        run_sub(lambda: emit_attn0(c, xA, xB, Wq, Wk, Wv, Wo0, bo, con, xln_tm, f0,
                                   tag="a0", stag="G"))
        barrier("Grao", c.sem("Grao").n)
        run_sub(lambda: emit_ffn(c, xB, xA, gwT0, w10, w20, cst, con, xln_tm, expd,
                                 tag="f2", stag="G", disp=disp, xinf=xinf))
        barrier("Gsca", c.sem("Gsca").n)
        run_sub(lambda: emit_ffn(c, xA, xB, gwT1, w11, w21, cst, con, xln_tm, expd,
                                 tag="f3", stag="G", disp=disp, xinf=xinf))
        barrier("Gsca", c.sem("Gsca").n)
        run_sub(lambda: emit_attn1(c, xB, xA, Wt, Wo1, Wg, par, con, xln_tm, f1,
                                   tag="a1", stag="G"))
        barrier("Grao", c.sem("Grao").n)
        run_sub(lambda: emit_ffn(c, xA, xo, gwT1, w11, w21, cst, con, xln_tm, expd,
                                 tag="f4", stag="G", disp=disp, xinf=xinf))
        barrier("Gsca", c.sem("Gsca").n)
        c.finalize()
    mybir.codegen_inst_isa_subclasses(nc)
    return nc


_PROGS = {}


def _fused():
    if "fused" not in _PROGS:
        _PROGS["fused"] = build_fused_prog()
    return _PROGS["fused"]


def kernel(L0_attn_x, L0_attn_Wq, L0_attn_Wk, L0_attn_Wv, L0_attn_Wo, L0_attn_bo,
           L0_ffn_gate_w, L0_ffn_w1, L0_ffn_w2,
           L1_attn_Wt, L1_attn_Wo, L1_attn_Wg, L1_attn_bg,
           L1_attn_ln1_w, L1_attn_ln1_b, L1_attn_lnc_w, L1_attn_lnc_b,
           L1_attn_lng_w, L1_attn_lng_b,
           L1_ffn_gate_w, L1_ffn_w1, L1_ffn_w2):
    from concourse.bass_utils import run_bass_kernel_spmd
    nc = _fused()
    NC = 8
    col = lambda a: np.ascontiguousarray(np.asarray(a, np.float32).reshape(8, 128).T)
    rep = lambda a: np.ascontiguousarray(np.broadcast_to(np.asarray(a, np.float32), (128, D)))
    f32c = lambda a: np.ascontiguousarray(np.asarray(a, np.float32))
    decay = (0.5 ** (1.0 / np.linspace(2.0, 8.0, D))).astype(np.float32)

    shared = {
        "gate_wT0": f32c(L0_ffn_gate_w), "w1_0": f32c(L0_ffn_w1), "w2_0": f32c(L0_ffn_w2),
        "gate_wT1": np.ascontiguousarray(np.asarray(L1_ffn_gate_w, np.float32).T),
        "w1_1": f32c(L1_ffn_w1), "w2_1": f32c(L1_ffn_w2),
        "Wq": f32c(L0_attn_Wq), "Wk": f32c(L0_attn_Wk), "Wv": f32c(L0_attn_Wv),
        "Wo0": f32c(L0_attn_Wo), "bo": col(L0_attn_bo),
        "Wt": f32c(L1_attn_Wt), "Wo1": f32c(L1_attn_Wo), "Wg": f32c(L1_attn_Wg),
        "p_dec": col(decay), "p_beta": col(1.0 - decay),
        "p_lncw": col(L1_attn_lnc_w), "p_lncb": col(L1_attn_lnc_b),
        "p_lngw": col(L1_attn_lng_w), "p_lngb": col(L1_attn_lng_b),
        "p_bg": col(L1_attn_bg),
        "p_g1w": rep(L1_attn_ln1_w), "p_g1b": rep(L1_attn_ln1_b),
    }
    for k, v in host_consts().items():
        shared["c_" + k] = v
    for k, v in attn0_host_consts().items():
        shared["c_" + k] = v
    for k, v in attn1_host_consts().items():
        shared["c_" + k] = v

    x = np.ascontiguousarray(np.asarray(L0_attn_x, np.float32))
    in_maps = []
    for i in range(NC):
        m = dict(shared)
        m["x_in"] = np.ascontiguousarray(x[i])
        in_maps.append(m)
    res = run_bass_kernel_spmd(nc, in_maps, core_ids=list(range(NC)))
    return np.stack([np.asarray(r["x_out"]) for r in res.results], 0)
